# version 01
# baseline (speedup 1.0000x reference)
import numpy as np
import ml_dtypes
import concourse.bass as bass
import concourse.mybir as mybir
from concourse.bass_utils import run_bass_kernel_spmd

F32 = mybir.dt.float32
BF16 = mybir.dt.bfloat16
AF = mybir.ActivationFunctionType
ALU = mybir.AluOpType
AX = mybir.AxisListType

BF = ml_dtypes.bfloat16

T = 16384
D = 2048
P = 128
NCORE = 8
BLK = 256
NS = 8
OWN = NS * BLK
KC = D // P
NH = 32
HD = 64
LH = 4
CS = 16
CPT = 2048 // CS
E = 8
I1 = 1408
MO1 = 2 * I1 // P
NP1 = I1 // P
KC2 = I1 // P
MO2 = D // P
EPS_LN = 1e-5
EPS_TTT = 1e-6
GRP = 512
NPASS = 2
PASS_TOK = OWN // NPASS
RG = [list(range(NCORE))]

ROUND = 32

DBG_PHASE = "full"
DBG_DUMP = False
DBG_TILES = NS
DBG_CHUNKS = CPT
DBG_WO = True
DBG_SCANLV = 4
DBG_SKIP_FFN = False
DBG_NORACE = False



def _fm_blocks(x2d, core):
    out = np.empty((NS, KC, P, BLK), np.float32)
    for s in range(NS):
        rows = x2d[2048 * s + BLK * core: 2048 * s + BLK * (core + 1)]
        out[s] = rows.T.reshape(KC, P, BLK)
    return np.ascontiguousarray(out)


def _unblocks(out_own, core, acc):
    for s in range(NS):
        acc[2048 * s + BLK * core: 2048 * s + BLK * (core + 1)] = \
            out_own[s].reshape(D, BLK).T


def _host_consts(core, W1, b1, tnw, tnb):
    c = {}
    g_rep = np.zeros((P, HD), np.float32)
    b_rep = np.zeros((P, HD), np.float32)
    g2_rep = np.zeros((P, HD), np.float32)
    b1_rep = np.zeros((P, HD), np.float32)
    for l in range(LH):
        h = LH * core + l
        for t in range(CS):
            g_rep[32 * l + t] = tnw[h]
            b_rep[32 * l + t] = tnb[h]
            g2_rep[32 * l + t] = tnw[h] * tnw[h]
            b1_rep[32 * l + t] = b1[h, 0]
    c["g_rep"] = g_rep
    c["b_rep"] = b_rep
    c["g2_rep"] = g2_rep
    c["gb_rep"] = g_rep * b_rep
    c["b1_init"] = b1_rep
    w1i = np.zeros((P, 2, HD), np.float32)
    for l in range(LH):
        h = LH * core + l
        hp, hg = l % 2, l // 2
        w1i[64 * hp:64 * hp + 64, hg, :] = W1[h]
    c["w1_init"] = w1i
    eta = 1.0 / np.arange(1, CS + 1, dtype=np.float32)
    mneg = np.zeros((P, CS), np.float32)
    for l in range(LH):
        for tl in range(CS):
            for k in range(CS):
                if tl <= k:
                    mneg[32 * l + tl, k] = -eta[k]
    c["mneg"] = mneg
    tri = np.zeros((P, P), np.float32)
    for l in range(LH):
        for tl in range(CS):
            for k in range(CS):
                if tl <= k:
                    tri[32 * l + tl, 32 * l + k] = -eta[k]
    c["trilTneg"] = tri.astype(BF)
    ob = np.zeros((P, P), np.float32)
    for l in range(LH):
        ob[32 * l:32 * l + CS, 32 * l:32 * l + CS] = 1.0
    c["ones_bd"] = ob.astype(BF)
    return c


def _rope_tables():
    inv = 1.0 / (10000.0 ** (np.arange(0, HD, 2, dtype=np.float32) / HD))
    pos = np.arange(T, dtype=np.float32)
    th = np.outer(inv, pos)
    C = np.empty((P, T), np.float32)
    S = np.empty((P, T), np.float32)
    for j in range(HD // 2):
        C[2 * j] = C[2 * j + 1] = np.cos(th[j])
        S[2 * j] = -np.sin(th[j])
        S[2 * j + 1] = np.sin(th[j])
    C[64:] = C[:64]
    S[64:] = S[:64]
    return np.ascontiguousarray(C.astype(BF)), np.ascontiguousarray(S.astype(BF))



class Sem:
    def __init__(self, stack, nc, name):
        self.h = stack.enter_context(nc.semaphore(name))
        self.n = 0

    def inc(self, inst, k=1):
        inst.then_inc(self.h, k)
        self.n += k
        return self.n


class Ctx:

    def __init__(self, nc, stack):
        self.nc = nc
        self.stack = stack
        self.V = nc.vector
        self.S = nc.scalar
        self.G = nc.gpsimd
        self.Tn = nc.tensor
        self.SY = nc.sync
        self.pe = Sem(stack, nc, "pe")
        self.dve = Sem(stack, nc, "dve")
        self.act = Sem(stack, nc, "act")
        self.gp = Sem(stack, nc, "gp")
        self.dq_in = Sem(stack, nc, "dq_in")
        self.dq_w = Sem(stack, nc, "dq_w")
        self.dq_out = Sem(stack, nc, "dq_out")
        self.cc = Sem(stack, nc, "cc")
        self.sw1 = [Sem(stack, nc, "sw1a"), Sem(stack, nc, "sw1b")]
        self.sw2 = [Sem(stack, nc, "sw2a"), Sem(stack, nc, "sw2b")]
        self.sxt = [Sem(stack, nc, "sxta"), Sem(stack, nc, "sxtb")]

    def dma_par(self, eng, sems, par, dst, src):
        eng.dma_start(out=dst, in_=src).then_inc(sems[par].h, 16)
        sems[par].n += 16
        return sems[par].n

    def dma_in(self, dst, src, serialize=True):
        if serialize:
            self.SY.wait_ge(self.dq_in.h, self.dq_in.n)
        self.SY.dma_start(out=dst, in_=src).then_inc(self.dq_in.h, 16)
        self.dq_in.n += 16
        return self.dq_in.n

    def dma_w(self, dst, src, serialize=True):
        if serialize:
            self.G.wait_ge(self.dq_w.h, self.dq_w.n)
        self.G.dma_start(out=dst, in_=src).then_inc(self.dq_w.h, 16)
        self.dq_w.n += 16
        return self.dq_w.n

    def dma_out(self, dst, src, serialize=True):
        if serialize:
            self.S.wait_ge(self.dq_out.h, self.dq_out.n)
        self.S.dma_start(out=dst, in_=src).then_inc(self.dq_out.h, 16)
        self.dq_out.n += 16
        return self.dq_out.n

    def barrier(self):
        for e in (self.V, self.S, self.G, self.Tn, self.SY):
            e.wait_ge(self.pe.h, self.pe.n)
            e.wait_ge(self.dve.h, self.dve.n)
            e.wait_ge(self.act.h, self.act.n)
            e.wait_ge(self.gp.h, self.gp.n)
            e.wait_ge(self.dq_in.h, self.dq_in.n)
            e.wait_ge(self.dq_w.h, self.dq_w.n)
            e.wait_ge(self.dq_out.h, self.dq_out.n)
            for st_ in (self.sw1, self.sw2, self.sxt):
                e.wait_ge(st_[0].h, st_[0].n)
                e.wait_ge(st_[1].h, st_[1].n)



def build_program(dbg_phase=None):
    dbg = dbg_phase or DBG_PHASE
    nc = bass.Bass("TRN2", target_bir_lowering=False,
                   detect_race_conditions=not DBG_NORACE)

    x_own = nc.dram_tensor("x_own", [NS, KC, P, BLK], F32, kind="ExternalInput").ap()
    w1t = nc.dram_tensor("w1t", [E, KC, P, 2 * I1], F32, kind="ExternalInput").ap()
    w2t = nc.dram_tensor("w2t", [E, KC2, P, D], F32, kind="ExternalInput").ap()
    rwT = nc.dram_tensor("rwT", [KC, P, E], F32, kind="ExternalInput").ap()
    wqkv = nc.dram_tensor("wqkv", [3, KC, P, LH * HD], F32, kind="ExternalInput").ap()
    wot = nc.dram_tensor("wot", [KC, P, D], F32, kind="ExternalInput").ap()
    ropeC = nc.dram_tensor("ropeC", [P, T], BF16, kind="ExternalInput").ap()
    ropeS = nc.dram_tensor("ropeS", [P, T], BF16, kind="ExternalInput").ap()
    g_rep_d = nc.dram_tensor("g_rep", [P, HD], F32, kind="ExternalInput").ap()
    b_rep_d = nc.dram_tensor("b_rep", [P, HD], F32, kind="ExternalInput").ap()
    g2_rep_d = nc.dram_tensor("g2_rep", [P, HD], F32, kind="ExternalInput").ap()
    gb_rep_d = nc.dram_tensor("gb_rep", [P, HD], F32, kind="ExternalInput").ap()
    w1_init_d = nc.dram_tensor("w1_init", [P, 2, HD], F32, kind="ExternalInput").ap()
    b1_init_d = nc.dram_tensor("b1_init", [P, HD], F32, kind="ExternalInput").ap()
    mneg_d = nc.dram_tensor("mneg", [P, CS], F32, kind="ExternalInput").ap()
    tril_d = nc.dram_tensor("trilTneg", [P, P], BF16, kind="ExternalInput").ap()
    onesbd_d = nc.dram_tensor("ones_bd", [P, P], BF16, kind="ExternalInput").ap()
    onehot_d = nc.dram_tensor("onehot", [E, E * P], F32, kind="ExternalInput").ap()

    out_own = nc.dram_tensor("out_own", [NS, KC, P, BLK], F32, kind="ExternalOutput").ap()
    if DBG_DUMP:
        dbg_xn = nc.dram_tensor("dbg_xn", [P, KC, GRP], BF16, kind="ExternalOutput").ap()
        dbg_m = nc.dram_tensor("dbg_m", [E, 2, GRP], F32, kind="ExternalOutput").ap()
        dbg_yacc = nc.dram_tensor("dbg_yacc", [P, MO2, GRP], F32, kind="ExternalOutput").ap()
        dbg_logit = nc.dram_tensor("dbg_logit", [P, E], F32, kind="ExternalOutput").ap()

    x1sp = nc.dram_tensor("x1sp", [NS, KC, P, BLK], F32, kind="Internal").ap()
    x2sp = nc.dram_tensor("x2sp", [NS, KC, P, BLK], F32, kind="Internal").ap()
    ag1i = nc.dram_tensor("ag1i", [NS, D, BLK], BF16, kind="Internal").ap()
    ag1o = nc.dram_tensor("ag1o", [NS, NCORE * D, BLK], BF16, kind="Internal").ap()
    ag2i = nc.dram_tensor("ag2i", [NS, LH * HD, 2048], BF16, kind="Internal").ap()
    ag2o = nc.dram_tensor("ag2o", [NS, D, 2048], BF16, kind="Internal").ap()

    import contextlib
    stack = contextlib.ExitStack()
    with stack:
        cx = Ctx(nc, stack)
        sb = lambda name, shape, dt: stack.enter_context(nc.sbuf_tensor(name, shape, dt))
        psb = lambda name, shape, dt=F32: stack.enter_context(nc.psum_tensor(name, shape, dt))

        cpool = sb("cpool", [P, 32], F32)
        ones_k1 = sb("ones_k1", [1, P], F32)
        ones_c1 = sb("ones_c1", [P, 1], BF16)
        ident_b = sb("ident_b", [P, P], BF16)
        ident_f = sb("ident_f", [P, P], F32)
        onehot = sb("onehot_s", [E, E * P], F32)
        g_rep = sb("g_rep_s", [P, HD], F32)
        b_rep = sb("b_rep_s", [P, HD], F32)
        g2_rep = sb("g2_rep_s", [P, HD], F32)
        gb_rep = sb("gb_rep_s", [P, HD], F32)
        mneg = sb("mneg_s", [P, CS], F32)
        tril_s = sb("tril_s", [P, P], BF16)
        onesbd_s = sb("onesbd_s", [P, P], BF16)
        W1s = sb("W1s", [P, 2, HD], F32)
        b1s = sb("b1s", [P, HD], F32)
        W1bf = sb("W1bf", [P, 2, HD], BF16)

        V, S, G, Tn, SY = cx.V, cx.S, cx.G, cx.Tn, cx.SY

        cx.dma_in(g_rep[:, :], g_rep_d[:, :])
        cx.dma_in(b_rep[:, :], b_rep_d[:, :])
        cx.dma_in(g2_rep[:, :], g2_rep_d[:, :])
        cx.dma_in(gb_rep[:, :], gb_rep_d[:, :])
        cx.dma_in(mneg[:, :], mneg_d[:, :])
        cx.dma_in(tril_s[:, :], tril_d[:, :])
        cx.dma_in(onesbd_s[:, :], onesbd_d[:, :])
        cx.dma_in(W1s.ap(), w1_init_d[:, :, :])
        cx.dma_in(b1s[:, :], b1_init_d[:, :])
        for col, val in ((0, EPS_LN), (4, EPS_TTT), (8, 1.0), (12, 0.5),
                         (16, -1.0 / CS), (20, 1.0 / HD), (24, -1.0 / HD)):
            inst = V.memset(cpool[:, col:col + 1], val)
            cx.dve.inc(inst)
            V.wait_ge(cx.dve.h, cx.dve.n)
        V.memset(ones_k1[:, :], 1.0)
        V.memset(ones_c1[:, :], 1.0)
        inst = V.memset(ident_f[:, :], 1.0)
        cx.dve.inc(inst)
        G.wait_ge(cx.dve.h, cx.dve.n)
        inst = G.affine_select(ident_f[:, :], ident_f[:, :], [[-1, P]], ALU.is_equal, 0.0,
                               base=0, channel_multiplier=1)
        cx.gp.inc(inst)
        V.wait_ge(cx.gp.h, cx.gp.n)
        cx.dma_in(onehot[:, :], onehot_d[:, :])
        inst = V.tensor_copy(ident_b[:, :], ident_f[:, :])
        cx.dve.inc(inst)
        cx.barrier()

        nc.const_aps.register(F32, EPS_LN, cpool[:, 0:1])
        nc.const_aps.register(F32, EPS_TTT, cpool[:, 4:5])
        half_ap = cpool[:, 12:13]
        nleta_ap = cpool[:, 16:17]
        inv64_ap = cpool[:, 20:21]
        ninv64_ap = cpool[:, 24:25]

        ffn_call_no = [0]

        def run_ffn_call(call, src, dst, do_ag):
          ffn_call_no[0] += 1
          cn = ffn_call_no[0]
          with contextlib.ExitStack() as fstack:
            fsb = lambda name, shape, dt: fstack.enter_context(
                nc.sbuf_tensor(f"{name}_{cn}", shape, dt))
            fpsb = lambda name, shape: fstack.enter_context(
                nc.psum_tensor(f"{name}_{cn}", shape, F32))
            xb = fsb("xb", [P, KC, PASS_TOK], BF16)
            xf32 = fsb("xf32", [P, KC, GRP], F32)
            xlo = xf32.ap().bitcast(BF16)
            y_acc = fsb("y_acc", [P, MO2, PASS_TOK], F32)
            y_act = fsb("y_act", [P, NP1, PASS_TOK], BF16)
            sqs = fsb("sqs", [P, 2, GRP], BF16)
            tscr = fsb("tscr", [P, GRP], F32)
            sgb = fsb("sgb", [P, 2, GRP], BF16)
            tub = tscr.ap().bitcast(BF16).rearrange("p (g t) -> p g t", g=2)
            m_bc = fsb("m_bc", [P, E, 2, GRP], BF16)
            m_fm = fsb("m_fm", [E, 2, GRP], F32)
            rwT_s = fsb("rwT_s", [P, KC, E], BF16)
            rwT_32 = fsb("rwT_32", [P, KC, E], F32)
            rwT_lo = fsb("rwT_lo", [P, KC, E], BF16)
            w1b = fsb("w1b", [P, 2, KC, P], BF16)
            w2b = fsb("w2b", [P, 2, KC2, P], BF16)
            rowA = fsb("rowA", [1, 4 * GRP], F32)
            msk = fsb("msk", [P, 64], F32)
            pb = [fpsb(f"pb{i}", [P, GRP]) for i in range(8)]

            cx.dma_w(rwT_s.ap(), rwT[:, :, :].rearrange("kc f e -> f kc e"))
            cx.dma_in(rwT_32.ap(), rwT[:, :, :].rearrange("kc f e -> f kc e"))
            V.wait_ge(cx.dq_w.h, cx.dq_w.n)
            V.wait_ge(cx.dq_in.h, cx.dq_in.n)
            inst = V.tensor_sub(rwT_lo.ap(), rwT_32.ap(), rwT_s.ap())
            cx.dve.inc(inst)

            def emit_ln_group(src_f32_view, dst_bf_view, grp_tok, with_router, g):
                ps_sum, ps_sq, ps_b1, ps_b2, ps_lg, ps_mt = pb[0], pb[1], pb[2], pb[3], pb[4], pb[5]
                if src_f32_view is not None:
                    inst = V.tensor_copy(dst_bf_view, src_f32_view)
                    cx.dve.inc(inst)
                xready = cx.dve.n
                Tn.wait_ge(cx.dq_w.h, cx.dq_w.n)
                Tn.wait_ge(cx.dq_in.h, cx.dq_in.n)
                Tn.wait_ge(cx.dve.h, xready)
                S.wait_ge(cx.dq_w.h, cx.dq_w.n)
                S.wait_ge(cx.dq_in.h, cx.dq_in.n)
                S.wait_ge(cx.dve.h, xready)
                for kc in range(KC):
                    inst = Tn.matmul(ps_sum[0:1, 0:grp_tok], ones_c1[:, :],
                                     dst_bf_view[:, kc, :], start=(kc == 0), stop=(kc == KC - 1))
                cx.pe.inc(inst)
                for kc in range(KC):
                    par = kc % 2
                    if kc >= 2:
                        S.wait_ge(cx.pe.h, cx.pe.n)
                    S.wait_ge(cx.dve.h, xready)
                    inst = S.activation(sqs[:, par, 0:grp_tok], dst_bf_view[:, kc, :], AF.Square)
                    cx.act.inc(inst)
                    Tn.wait_ge(cx.act.h, cx.act.n)
                    inst = Tn.matmul(ps_sq[0:1, 0:grp_tok], ones_c1[:, :], sqs[:, par, 0:grp_tok],
                                     start=(kc == 0), stop=(kc == KC - 1))
                    cx.pe.inc(inst)
                mu = rowA[0:1, 0:grp_tok]
                ex2 = rowA[0:1, GRP:GRP + grp_tok]
                var = rowA[0:1, 2 * GRP:2 * GRP + grp_tok]
                sd = var
                rstd = ex2
                numu = rowA[0:1, 3 * GRP:3 * GRP + grp_tok]
                V.wait_ge(cx.pe.h, cx.pe.n)
                V.tensor_scalar_mul(mu, ps_sum[0:1, 0:grp_tok], 1.0 / D)
                V.tensor_scalar_mul(ex2, ps_sq[0:1, 0:grp_tok], 1.0 / D)
                V.tensor_tensor(var, mu, mu, ALU.mult)
                inst = V.tensor_sub(var, ex2, var)
                cx.dve.inc(inst)
                S.wait_ge(cx.dve.h, cx.dve.n)
                inst = S.activation(sd, var, AF.Sqrt, bias=EPS_LN)
                cx.act.inc(inst)
                V.wait_ge(cx.act.h, cx.act.n)
                V.reciprocal(rstd, sd)
                V.tensor_mul(numu, mu, rstd)
                inst = V.tensor_scalar_mul(numu, numu, -1.0)
                cx.dve.inc(inst)
                Tn.wait_ge(cx.dve.h, cx.dve.n)
                inst = Tn.matmul(ps_b1[:, 0:grp_tok], ones_k1[:, :], rstd, start=True, stop=True)
                inst = Tn.matmul(ps_b2[:, 0:grp_tok], ones_k1[:, :], numu, start=True, stop=True)
                cx.pe.inc(inst)
                V.wait_ge(cx.pe.h, cx.pe.n)
                for kc in range(KC):
                    V.tensor_mul(tscr[:, 0:grp_tok], dst_bf_view[:, kc, :], ps_b1[:, 0:grp_tok])
                    if with_router:
                        V.tensor_add(tscr[:, 0:grp_tok], tscr[:, 0:grp_tok], ps_b2[:, 0:grp_tok])
                        V.tensor_copy(dst_bf_view[:, kc, :], tscr[:, 0:grp_tok])
                        inst = V.tensor_sub(xlo[:, kc, 0:grp_tok], tscr[:, 0:grp_tok],
                                            dst_bf_view[:, kc, :])
                    else:
                        inst = V.tensor_add(dst_bf_view[:, kc, :], tscr[:, 0:grp_tok],
                                            ps_b2[:, 0:grp_tok])
                cx.dve.inc(inst)
                xn_ready = cx.dve.n

                if not with_router:
                    return xn_ready

                nmb = grp_tok // P
                for mb in range(nmb):
                    Tn.wait_ge(cx.dve.h, xn_ready)
                    if mb > 0:
                        Tn.wait_ge(cx.dve.h, cx.dve.n)
                    for ti, kc in enumerate([(a, b) for a in range(KC) for b in range(3)]):
                        kcc, term = kc
                        lhs = (dst_bf_view if term != 1 else xlo)[:, kcc, mb * P:(mb + 1) * P]
                        rhs = (rwT_s if term != 2 else rwT_lo)[:, kcc, :]
                        inst = Tn.matmul(ps_lg[0:P, 0:E], lhs, rhs,
                                         start=(ti == 0), stop=(ti == 3 * KC - 1))
                    cx.pe.inc(inst)
                    l_ = msk[:, 0:E]
                    mx1 = msk[:, 8:9]
                    mx2 = msk[:, 12:13]
                    dd = msk[:, 16:17]
                    ew = msk[:, 20:21]
                    inv = msk[:, 24:25]
                    teq = msk[:, 32:40]
                    sel = msk[:, 40:48]
                    expw = msk[:, 48:56]
                    mm_ = msk[:, 56:64]

                    def sv(f):
                        inst = f()
                        cx.dve.inc(inst)
                        V.wait_ge(cx.dve.h, cx.dve.n)
                        return inst

                    V.wait_ge(cx.pe.h, cx.pe.n)
                    sv(lambda: V.tensor_copy(l_, ps_lg[0:P, 0:E]))
                    if DBG_DUMP and g == 0 and mb == 0 and ffn_call_no[0] == 1:
                        S.wait_ge(cx.dve.h, cx.dve.n)
                        cx.dma_out(dbg_logit[:, :], l_)
                    sv(lambda: V.reduce_max(mx1, l_, AX.X))
                    sv(lambda: V.tensor_tensor(teq, l_, mx1.broadcast_to([P, E]), ALU.is_equal))
                    sv(lambda: V.tensor_scalar_mul(teq, teq, -1e30))
                    sv(lambda: V.tensor_add(teq, teq, l_))
                    sv(lambda: V.reduce_max(mx2, teq, AX.X))
                    sv(lambda: V.tensor_sub(dd, mx2, mx1))
                    S.wait_ge(cx.dve.h, cx.dve.n)
                    inst = S.activation(ew, dd, AF.Exp)
                    cx.act.inc(inst)
                    V.wait_ge(cx.act.h, cx.act.n)
                    sv(lambda: V.tensor_scalar_add(ew, ew, 1.0))
                    sv(lambda: V.reciprocal(inv, ew))
                    sv(lambda: V.tensor_tensor(sel, l_, mx2.broadcast_to([P, E]), ALU.is_ge))
                    sv(lambda: V.tensor_sub(expw, l_, mx1.broadcast_to([P, E])))
                    S.wait_ge(cx.dve.h, cx.dve.n)
                    inst = S.activation(expw, expw, AF.Exp)
                    cx.act.inc(inst)
                    V.wait_ge(cx.act.h, cx.act.n)
                    sv(lambda: V.tensor_mul(mm_, sel, expw))
                    sv(lambda: V.tensor_scalar_mul(mm_, mm_, inv))
                    Tn.wait_ge(cx.dve.h, cx.dve.n)
                    inst = Tn.transpose(ps_mt[0:E, 0:P], mm_, ident_f[:, 0:P])
                    cx.pe.inc(inst)
                    V.wait_ge(cx.pe.h, cx.pe.n)
                    inst = V.tensor_copy(m_fm[:, g, mb * P:(mb + 1) * P], ps_mt[0:E, 0:P])
                    cx.dve.inc(inst)
                for e in range(E):
                    Tn.wait_ge(cx.dve.h, cx.dve.n)
                    inst = Tn.matmul(ps_lg[:, 0:grp_tok], onehot[:, e * P:(e + 1) * P],
                                     m_fm[:, g, 0:grp_tok], start=True, stop=True)
                    cx.pe.inc(inst)
                    V.wait_ge(cx.pe.h, cx.pe.n)
                    inst = V.tensor_copy(m_bc[:, e, g, 0:grp_tok], ps_lg[:, 0:grp_tok])
                    cx.dve.inc(inst)
                return xn_ready

            def emit_ffn_pass(call, p, src_, dst, do_ag):
                blocks = [4 * p + i for i in range(4)]
                for g in range(2):
                    bs = blocks[2 * g: 2 * g + 2]
                    for bi, s in enumerate(bs):
                        cx.dma_w(xb[:, :, g * GRP + bi * BLK: g * GRP + (bi + 1) * BLK],
                                 src_[s].rearrange("kc f t -> f kc t"))
                    V.wait_ge(cx.dq_w.h, cx.dq_w.n)
                    emit_ln_group(None, xb[:, :, g * GRP:(g + 1) * GRP], GRP, True, g)

                ln_done_dve = cx.dve.n
                if DBG_DUMP and call == 1 and p == 0:
                    S.wait_ge(cx.dve.h, cx.dve.n)
                    cx.dma_out(dbg_xn[:, :, :], xb[:, :, 0:GRP])
                    cx.dma_out(dbg_m[:, :, :], m_fm.ap())
                Tn.wait_ge(cx.dve.h, ln_done_dve)

                wpar1 = [0]
                wpar2 = [0]
                w1_last_use = [0, 0]
                w2_last_use = [0, 0]
                y2_evac_dve = [0, 0, 0, 0]

                def load_w1(e, mo):
                    par = wpar1[0] % 2
                    wpar1[0] += 1
                    G.wait_ge(cx.pe.h, w1_last_use[par])
                    n = cx.dma_par(G, cx.sw1, par, w1b[:, par, :, :],
                                   w1t[e, :, :, mo * P:(mo + 1) * P].rearrange("kc f m -> f kc m"))
                    return par, n

                def load_w2(e, mo2):
                    par = wpar2[0] % 2
                    wpar2[0] += 1
                    G.wait_ge(cx.pe.h, w2_last_use[par])
                    n = cx.dma_par(G, cx.sw2, par, w2b[:, par, :, :],
                                   w2t[e, :, :, mo2 * P:(mo2 + 1) * P].rearrange("kc f m -> f kc m"))
                    return par, n

                for e in range(E):
                    for i in range(NP1):
                        evac_dve = [0, 0]
                        evac_act = [0, 0]
                        for half, mo in ((0, i), (1, NP1 + i)):
                            par, wn = load_w1(e, mo)
                            Tn.wait_ge(cx.sw1[par].h, wn)
                            for g in range(2):
                                bank = pb[half * 2 + g]
                                if i > 0:
                                    if half == 0:
                                        Tn.wait_ge(cx.act.h, evac_act_prev[g])
                                    else:
                                        Tn.wait_ge(cx.dve.h, evac_dve_prev[g])
                                for kc in range(KC):
                                    inst = Tn.matmul(bank[:, :], w1b[:, par, kc, :],
                                                     xb[:, kc, g * GRP:(g + 1) * GRP],
                                                     start=(kc == 0), stop=(kc == KC - 1))
                                cx.pe.inc(inst)
                            w1_last_use[par] = cx.pe.n
                            if half == 0:
                                gate_pe = cx.pe.n
                            else:
                                up_pe = cx.pe.n
                        for g in range(2):
                            S.wait_ge(cx.pe.h, gate_pe)
                            if i > 0:
                                S.wait_ge(cx.dve.h, yact_dve[g])
                            inst = S.activation(sgb[:, g, :], pb[g][:, :], AF.Silu)
                            evac_act[g] = cx.act.inc(inst)
                            V.wait_ge(cx.pe.h, up_pe)
                            V.tensor_mul(tub[:, g, :], pb[2 + g][:, :], m_bc[:, e, g, :])
                            V.wait_ge(cx.act.h, evac_act[g])
                            inst = V.tensor_mul(y_act[:, i, g * GRP:(g + 1) * GRP],
                                                sgb[:, g, :], tub[:, g, :])
                            evac_dve[g] = cx.dve.inc(inst)
                        yact_dve = list(evac_dve)
                        evac_act_prev = list(evac_act)
                        evac_dve_prev = list(evac_dve)
                    for mo2 in range(MO2):
                        par, wn = load_w2(e, mo2)
                        Tn.wait_ge(cx.sw2[par].h, wn)
                        for g in range(2):
                            bank = pb[4 + (mo2 % 2) * 2 + g]
                            Tn.wait_ge(cx.dve.h, max(y2_evac_dve[(mo2 % 2) * 2 + g],
                                                     yact_dve[g]))
                            for kc in range(KC2):
                                inst = Tn.matmul(bank[:, :], w2b[:, par, kc, :],
                                                 y_act[:, kc, g * GRP:(g + 1) * GRP],
                                                 start=(kc == 0), stop=(kc == KC2 - 1))
                            cx.pe.inc(inst)
                            w2_last_use[par] = cx.pe.n
                            V.wait_ge(cx.pe.h, cx.pe.n)
                            if e == 0:
                                inst = V.tensor_copy(y_acc[:, mo2, g * GRP:(g + 1) * GRP], bank[:, :])
                            else:
                                inst = V.tensor_add(y_acc[:, mo2, g * GRP:(g + 1) * GRP],
                                                    y_acc[:, mo2, g * GRP:(g + 1) * GRP], bank[:, :])
                            y2_evac_dve[(mo2 % 2) * 2 + g] = cx.dve.inc(inst)

                if DBG_DUMP and call == 1 and p == 0:
                    S.wait_ge(cx.dve.h, cx.dve.n)
                    cx.dma_out(dbg_yacc[:, :, :], y_acc[:, :, 0:GRP])
                for g in range(2):
                    bs = blocks[2 * g: 2 * g + 2]
                    SY.wait_ge(cx.dq_out.h, cx.dq_out.n)
                    SY.wait_ge(cx.dve.h, cx.dve.n)
                    for bi, s in enumerate(bs):
                        cx.dma_in(xf32[:, :, bi * BLK:(bi + 1) * BLK],
                                  src_[s].rearrange("kc f t -> f kc t"))
                    V.wait_ge(cx.dq_in.h, cx.dq_in.n)
                    V.wait_ge(cx.dve.h, max(y2_evac_dve))
                    inst = V.scalar_tensor_tensor(
                        xf32.ap(), y_acc[:, :, g * GRP:(g + 1) * GRP], half_ap, xf32.ap(),
                        ALU.mult, ALU.add)
                    xnew = cx.dve.inc(inst)
                    S.wait_ge(cx.dve.h, xnew)
                    for bi, s in enumerate(bs):
                        cx.dma_out(dst[s].rearrange("kc f t -> f kc t"),
                                   xf32[:, :, bi * BLK:(bi + 1) * BLK])
                    if do_ag:
                        emit_ln_group(xf32.ap(), xb[:, :, g * GRP:(g + 1) * GRP], GRP, False, g)
                        S.wait_ge(cx.dve.h, cx.dve.n)
                        for bi, s in enumerate(bs):
                            cx.dma_out(ag1i[s].rearrange("(kc f) t -> f kc t", f=P),
                                       xb[:, :, g * GRP + bi * BLK: g * GRP + (bi + 1) * BLK])
                            G.wait_ge(cx.dq_out.h, cx.dq_out.n)
                            inst = G.collective_compute(
                                "AllGather", ALU.bypass, replica_groups=RG,
                                ins=[ag1i[s].opt()], outs=[ag1o[s].opt()])
                            cx.cc.inc(inst)
            for p in range(NPASS):
                emit_ffn_pass(call, p, src, dst, do_ag)
                cx.barrier()

        if DBG_SKIP_FFN:
            for s in range(NS):
                inst = G.collective_compute(
                    "AllGather", ALU.bypass, replica_groups=RG,
                    ins=[ag1i[s].opt()], outs=[ag1o[s].opt()])
                cx.cc.inc(inst)
        else:
            run_ffn_call(1, x_own, x1sp if dbg != "ffn1" else out_own,
                         do_ag=(dbg != "ffn1"))

        if dbg == "ffn1":
            SY.wait_ge(cx.dq_out.h, cx.dq_out.n)
            return nc


        import contextlib as _ctl
        with _ctl.ExitStack() as astack:
            asb = lambda name, shape, dt: astack.enter_context(nc.sbuf_tensor(name, shape, dt))
            apsb = lambda name: astack.enter_context(nc.psum_tensor(name, [P, GRP], F32))
            wq_s = asb("wq_s", [P, KC, 2, P], BF16)
            wk_s = asb("wk_s", [P, KC, 2, P], BF16)
            wv_s = asb("wv_s", [P, KC, 2, P], BF16)
            Qf = asb("Qf", [P, 2, 2048], BF16)
            Kf = asb("Kf", [P, 2, 2048], BF16)
            VMKf = asb("VMKf", [P, 2, 2048], BF16)
            Cs = asb("Cs", [P, 2048], BF16)
            Ss = asb("Ss", [P, 2048], BF16)
            xt = asb("xt", [P, 2, KC, BLK], BF16)
            rsc = asb("rsc", [P, 3, BLK], BF16)
            out_fm = asb("out_fm", [P, 2, 2048], BF16)
            x1b = asb("x1b", [P, KC, BLK], F32)
            worhs = asb("worhs", [P, KC, BLK], BF16)
            wo_b = asb("wo_b", [P, 2, KC, P], BF16)
            bdK = asb("bdK", [P, 2, 48], BF16)
            bdQ = asb("bdQ", [P, 2, 48], BF16)
            bdW = asb("bdW", [P, P], BF16)
            atbd = asb("atbd", [P, P], BF16)
            gbt = asb("gbt", [P, HD], F32)
            vm_s = asb("vm_s", [P, HD], BF16)
            xk_s = asb("xk_s", [P, HD], BF16)
            zs = asb("zs", [P, HD], F32)
            xh = asb("xh", [P, HD], F32)
            gxh = asb("gxh", [P, HD], F32)
            tp = asb("tp", [P, HD], F32)
            t5 = asb("t5", [P, HD], F32)
            grd = asb("grd", [P, HD], F32)
            grdbf = asb("grdbf", [P, HD], BF16)
            zbar = asb("zbar", [P, HD], F32)
            xh2 = asb("xh2", [P, HD], F32)
            ot = asb("ot", [P, HD], BF16)
            bn6 = asb("bn6", [P, 6], F32)
            st2 = asb("st2", [P, 2], F32)
            st2b = asb("st2b", [P, 2], F32)
            ssc = asb("ssc", [P, 16], F32)
            ssc2 = asb("ssc2", [P, 8], F32)
            pa = [apsb(f"pa{i}") for i in range(6)]
            pa5b = pa[5].ap().bitcast(BF16)
            p_z1 = pa[2][:, 0:HD]
            p_zq = pa[2][:, 64:128]
            p_at = pa[1][:, 0:48]
            p_zb = pa[3][:, 0:HD]
            p_w = pa[4][:, 0:P]
            p_b = pa[4][:, P:P + HD]
            p_xkT = pa5b[:, 0:HD]
            p_vmT = pa5b[:, 64:128]
            pa3b = pa[3].ap().bitcast(BF16)
            p_bt0 = pa5b[:, 128:144]
            p_bt1 = pa3b[:, 256:272]

            sd_a = ssc[:, 0:1]
            rstd_a = ssc[:, 4:5]
            aps_a = ssc[:, 8:9]
            b2n_a = ssc[:, 12:13]
            sd2_a = ssc2[:, 0:1]
            rstd2_a = ssc2[:, 4:5]

            pid_sy = SY.partition_id()

            for w_i, wbuf in ((0, wq_s), (1, wk_s), (2, wv_s)):
                cx.dma_w(wbuf.ap(),
                         wqkv[w_i].rearrange("kc f (hg m) -> f kc hg m", hg=2))
            V.memset(bdK.ap(), 0.0)
            V.memset(bdQ.ap(), 0.0)
            V.memset(bdW.ap(), 0.0)
            V.memset(atbd.ap(), 0.0)
            V.memset(zs.ap(), 0.0)
            V.memset(pa[5].ap(), 0.0)
            V.memset(pa[2].ap(), 0.0)
            inst = V.memset(out_fm.ap(), 0.0)
            cx.dve.inc(inst)
            ag1_cc = cx.cc.n - NS

            def dv(f):
                inst = f()
                cx.dve.inc(inst)
                V.wait_ge(cx.dve.h, cx.dve.n)
                return cx.dve.n

            def gv(f):
                inst = f()
                cx.gp.inc(inst)
                G.wait_ge(cx.gp.h, cx.gp.n)
                return cx.gp.n

            def av(f):
                inst = f()
                cx.act.inc(inst)
                S.wait_ge(cx.act.h, cx.act.n)
                return cx.act.n

            for s in range(DBG_TILES):
                SY.wait_ge(cx.cc.h, ag1_cc + s + 1)
                cx.dma_in(Cs[:, :], ropeC[:, 2048 * s:2048 * (s + 1)])
                cx.dma_in(Ss[:, :], ropeS[:, 2048 * s:2048 * (s + 1)])
                xt_pe_last = [0, 0]
                for r in range(NCORE):
                    par = r % 2
                    SY.wait_ge(cx.pe.h, xt_pe_last[par])
                    xt_n = cx.dma_par(SY, cx.sxt, par, xt[:, par, :, :],
                                      ag1o[s][2048 * r:2048 * (r + 1), :]
                                      .rearrange("(kc f) t -> f kc t", f=P))
                    tok0 = BLK * r
                    for w_i, wbuf in ((0, wq_s), (1, wk_s), (2, wv_s)):
                        for hg in range(2):
                            bank = pa[(w_i * 2 + hg) % 2]
                            Tn.wait_ge(cx.sxt[par].h, xt_n)
                            Tn.wait_ge(cx.dq_w.h, cx.dq_w.n)
                            Tn.wait_ge(cx.dve.h, cx.dve.n)
                            for kc in range(KC):
                                inst = Tn.matmul(bank[:, 0:BLK], wbuf[:, kc, hg, :],
                                                 xt[:, par, kc, :],
                                                 start=(kc == 0), stop=(kc == KC - 1))
                            cx.pe.inc(inst)
                            xt_pe_last[par] = cx.pe.n
                            V.wait_ge(cx.pe.h, cx.pe.n)
                            V.wait_ge(cx.dq_in.h, cx.dq_in.n)
                            if w_i < 2:
                                dst = (Qf if w_i == 0 else Kf)
                                V.tensor_copy(rsc[:, 0, :], bank[:, 0:BLK])
                                shmask = []
                                for i2 in range(0, 32, 2):
                                    shmask += [i2 + 1, i2]
                                V.stream_shuffle(rsc[:, 1, :], rsc[:, 0, :], shmask)
                                V.tensor_mul(rsc[:, 2, :], rsc[:, 0, :],
                                             Cs[:, tok0:tok0 + BLK])
                                V.tensor_mul(rsc[:, 1, :], rsc[:, 1, :],
                                             Ss[:, tok0:tok0 + BLK])
                                inst = V.tensor_add(dst[:, hg, tok0:tok0 + BLK],
                                                    rsc[:, 2, :], rsc[:, 1, :])
                            else:
                                inst = V.tensor_sub(VMKf[:, hg, tok0:tok0 + BLK],
                                                    bank[:, 0:BLK],
                                                    Kf[:, hg, tok0:tok0 + BLK])
                            cx.dve.inc(inst)
                proj_done_dve = cx.dve.n

                Tn.wait_ge(cx.dve.h, proj_done_dve)
                V.wait_ge(cx.dve.h, proj_done_dve)
                prev_pe = cx.pe.n
                for j in range(DBG_CHUNKS):
                    sl = slice(CS * j, CS * (j + 1))
                    V.wait_ge(cx.pe.h, prev_pe)
                    for hg in range(2):
                        V.tensor_copy(bdK[0:64, hg, 0:CS], Kf[0:64, hg, sl])
                        V.tensor_copy(bdK[64:128, hg, 32:48], Kf[64:128, hg, sl])
                        V.tensor_copy(bdQ[0:64, hg, 0:CS], Qf[0:64, hg, sl])
                        V.tensor_copy(bdQ[64:128, hg, 32:48], Qf[64:128, hg, sl])
                    bd_dve = dv(lambda: V.engine_nop())
                    if DBG_SCANLV < 1:
                        prev_pe = cx.pe.n
                        continue
                    Tn.wait_ge(cx.dve.h, bd_dve)
                    Tn.wait_ge(cx.act.h, cx.act.n)
                    for l in range(LH):
                        hp, hg = l % 2, l // 2
                        inst = Tn.transpose(
                            p_xkT[32 * l:32 * l + CS, :], Kf[64 * hp:64 * hp + 64, hg, sl],
                            ident_b[64 * hp:64 * hp + 64, 64 * hp:64 * hp + 64],
                            tile_position=(64 * hp, 32 * l))
                        inst = Tn.transpose(
                            p_vmT[32 * l:32 * l + CS, :], VMKf[64 * hp:64 * hp + 64, hg, sl],
                            ident_b[64 * hp:64 * hp + 64, 64 * hp:64 * hp + 64],
                            tile_position=(64 * hp, 32 * l))
                    inst = Tn.matmul(p_at[0:48, :], bdK[:, 0, :], bdQ[:, 0, 0:48],
                                     start=True, stop=True)
                    inst = Tn.matmul(p_at[64:112, :], bdK[:, 1, :], bdQ[:, 1, 0:48],
                                     start=True, stop=True, tile_position=(0, 64))
                    cx.pe.inc(inst)
                    tr_pe = cx.pe.n
                    S.wait_ge(cx.pe.h, tr_pe)
                    av(lambda: S.copy(xk_s[:, :], p_xkT[:, :]))
                    vm_act = av(lambda: S.copy(vm_s[:, :], p_vmT[:, :]))
                    V.wait_ge(cx.act.h, vm_act)
                    for l in range(LH):
                        hp = l % 2
                        V.tensor_copy(bdW[32 * l:32 * l + CS, 64 * hp:64 * hp + 64],
                                      xk_s[32 * l:32 * l + CS, :])
                        inst = V.tensor_mul(
                            atbd[32 * l:32 * l + CS, 32 * l:32 * l + CS],
                            p_at[32 * l:32 * l + CS, 32 * hp:32 * hp + CS],
                            mneg[32 * l:32 * l + CS, :])
                    evac_dve = dv(lambda: V.engine_nop())
                    G.wait_ge(cx.act.h, vm_act)
                    gv(lambda: G.tensor_mul(gbt[:, :], g_rep[:, :], vm_s[:, :]))
                    gbt_gp = gv(lambda: G.tensor_sub(gbt[:, :], gb_rep[:, :], gbt[:, :]))
                    if DBG_SCANLV < 2:
                        prev_pe = cx.pe.n
                        continue

                    wcast = dv(lambda: V.tensor_copy(W1bf.ap(), W1s.ap()))
                    Tn.wait_ge(cx.dve.h, wcast)
                    inst = Tn.matmul(p_z1[0:48, :], bdK[:, 0, 0:48], W1bf[:, 0, :],
                                     start=True, stop=True)
                    inst = Tn.matmul(p_z1[64:112, :], bdK[:, 1, 0:48], W1bf[:, 1, :],
                                     start=True, stop=True, tile_position=(0, 64))
                    cx.pe.inc(inst)
                    z1_pe = cx.pe.n
                    V.wait_ge(cx.pe.h, z1_pe)
                    V.tensor_add(zs[0:48, :], p_z1[0:48, :], b1s[0:48, :])
                    dv(lambda: V.tensor_add(zs[64:112, :], p_z1[64:112, :], b1s[64:112, :]))
                    dv(lambda: V.bn_stats(bn6[:, :], zs[:, :]))
                    chain_dve = dv(lambda: V.bn_aggr(st2[:, :], bn6[:, :]))
                    S.wait_ge(cx.dve.h, chain_dve)
                    sd_act = av(lambda: S.activation(sd_a, st2[:, 1:2], AF.Sqrt, bias=EPS_TTT))
                    V.wait_ge(cx.act.h, sd_act)
                    dv(lambda: V.reciprocal(rstd_a, sd_a))
                    dv(lambda: V.tensor_scalar(xh[:, :], zs[:, :], st2[:, 0:1], rstd_a,
                                               ALU.subtract, ALU.mult))
                    xh_dve = cx.dve.n
                    G.wait_ge(cx.dve.h, xh_dve)
                    G.wait_ge(cx.gp.h, gbt_gp)
                    gv(lambda: G.tensor_mul(gxh[:, :], g2_rep[:, :], xh[:, :]))
                    gxh_gp = gv(lambda: G.tensor_add(gxh[:, :], gxh[:, :], gbt[:, :]))
                    V.wait_ge(cx.gp.h, gxh_gp)
                    dv(lambda: V.reduce_sum(ssc[:, 8:9], gxh[:, :], AX.X))
                    dv(lambda: V.tensor_mul(tp[:, :], gxh[:, :], xh[:, :]))
                    dv(lambda: V.reduce_sum(ssc[:, 12:13], tp[:, :], AX.X))
                    dv(lambda: V.tensor_scalar_mul(aps_a, aps_a, inv64_ap))
                    dv(lambda: V.tensor_scalar(b2n_a, b2n_a, ninv64_ap, rstd_a,
                                               ALU.mult, ALU.mult))
                    dv(lambda: V.tensor_scalar(t5[:, :], gxh[:, :], aps_a, rstd_a,
                                               ALU.subtract, ALU.mult))
                    dv(lambda: V.scalar_tensor_tensor(grd[:, :], xh[:, :], b2n_a, t5[:, :],
                                                      ALU.mult, ALU.add))
                    grad_dve = cx.dve.n
                    if DBG_SCANLV < 3:
                        prev_pe = cx.pe.n
                        continue
                    S.wait_ge(cx.dve.h, grad_dve)
                    grad_act = av(lambda: S.activation(grdbf[:, :], grd[:, :], AF.Copy))
                    if DBG_SCANLV == 30:
                        prev_pe = cx.pe.n
                        continue
                    Tn.wait_ge(cx.act.h, grad_act)
                    Tn.wait_ge(cx.dve.h, evac_dve)
                    inst = Tn.matmul(p_zb[:, :], atbd[:, :], grdbf[:, :],
                                     start=True, stop=False)
                    inst = Tn.matmul(p_zb[:, :], tril_s[:, :], grdbf[:, :],
                                     start=False, stop=True)
                    if DBG_SCANLV == 31:
                        cx.pe.inc(inst)
                        prev_pe = cx.pe.n
                        continue
                    inst = Tn.matmul(p_zq[0:48, :], bdQ[:, 0, 0:48], W1bf[:, 0, :],
                                     start=True, stop=True)
                    inst = Tn.matmul(p_zq[64:112, :], bdQ[:, 1, 0:48], W1bf[:, 1, :],
                                     start=True, stop=True, tile_position=(0, 64))
                    if DBG_SCANLV == 32:
                        cx.pe.inc(inst)
                        prev_pe = cx.pe.n
                        continue
                    if DBG_SCANLV != 326:
                        inst = Tn.matmul(p_b[:, :], onesbd_s[:, :], grdbf[:, :],
                                         start=True, stop=True)
                    if DBG_SCANLV == 325:
                        cx.pe.inc(inst)
                        prev_pe = cx.pe.n
                        continue
                    inst = Tn.matmul(pa[4][:, 0:64], bdW[0:48, :], grdbf[0:48, :],
                                     start=True, stop=True)
                    inst = Tn.matmul(pa[0][:, 0:64], bdW[64:112, :], grdbf[64:112, :],
                                     start=True, stop=True)
                    cx.pe.inc(inst)
                    upd_pe = cx.pe.n
                    if DBG_SCANLV == 33:
                        prev_pe = cx.pe.n
                        continue
                    V.wait_ge(cx.pe.h, upd_pe)
                    dv(lambda: V.tensor_add(zbar[:, :], p_zb[:, :], b1s[:, :]))
                    dv(lambda: V.tensor_add(zbar[:, :], zbar[:, :], p_zq[:, :]))
                    dv(lambda: V.scalar_tensor_tensor(
                        W1s[:, 0, :], pa[4][:, 0:64], nleta_ap, W1s[:, 0, :],
                        ALU.mult, ALU.add))
                    dv(lambda: V.scalar_tensor_tensor(
                        W1s[:, 1, :], pa[0][:, 0:64], nleta_ap, W1s[:, 1, :],
                        ALU.mult, ALU.add))
                    dv(lambda: V.scalar_tensor_tensor(b1s[:, :], p_b[:, :],
                                                      nleta_ap, b1s[:, :],
                                                      ALU.mult, ALU.add))
                    if DBG_SCANLV == 34:
                        prev_pe = cx.pe.n
                        continue
                    if DBG_SCANLV < 4:
                        prev_pe = cx.pe.n
                        continue
                    dv(lambda: V.bn_stats(bn6[:, :], zbar[:, :]))
                    lf_dve = dv(lambda: V.bn_aggr(st2b[:, :], bn6[:, :]))
                    S.wait_ge(cx.dve.h, lf_dve)
                    sd2_act = av(lambda: S.activation(sd2_a, st2b[:, 1:2], AF.Sqrt,
                                                      bias=EPS_TTT))
                    V.wait_ge(cx.act.h, sd2_act)
                    dv(lambda: V.reciprocal(rstd2_a, sd2_a))
                    xh2_dve = dv(lambda: V.tensor_scalar(xh2[:, :], zbar[:, :], st2b[:, 0:1],
                                                         rstd2_a, ALU.subtract, ALU.mult))
                    G.wait_ge(cx.dve.h, xh2_dve)
                    gv(lambda: G.tensor_mul(xh2[:, :], g_rep[:, :], xh2[:, :]))
                    ot_gp = gv(lambda: G.tensor_add(ot[:, :], xh2[:, :], b_rep[:, :]))
                    Tn.wait_ge(cx.gp.h, ot_gp)
                    Tn.wait_ge(cx.dve.h, cx.dve.n)
                    for l in range(LH):
                        hp, hg = l % 2, l // 2
                        p_bt = p_bt0 if hg == 0 else p_bt1
                        inst = Tn.transpose(
                            p_bt[64 * hp:64 * hp + 64, :],
                            ot[32 * l:32 * l + CS, :],
                            ident_b[32 * l:32 * l + CS, 32 * l:32 * l + CS],
                            tile_position=(32 * l, 64 * hp))
                    cx.pe.inc(inst)
                    bt_pe = cx.pe.n
                    V.wait_ge(cx.pe.h, bt_pe)
                    for l in range(LH):
                        hp, hg = l % 2, l // 2
                        p_bt = p_bt0 if hg == 0 else p_bt1
                        inst = V.tensor_add(out_fm[64 * hp:64 * hp + 64, hg, sl],
                                            p_bt[64 * hp:64 * hp + 64, :],
                                            Qf[64 * hp:64 * hp + 64, hg, sl])
                    cx.dve.inc(inst)
                    prev_pe = cx.pe.n

                if not DBG_WO:
                    cx.barrier()
                    continue
                S.wait_ge(cx.dve.h, cx.dve.n)
                cx.dma_out(ag2i[s].rearrange("(hg hp g) t -> (hp g) hg t", hg=2, hp=2),
                           out_fm.ap())
                G.wait_ge(cx.dq_out.h, cx.dq_out.n)
                inst = G.collective_compute(
                    "AllGather", ALU.bypass, replica_groups=RG,
                    ins=[ag2i[s].opt()], outs=[ag2o[s].opt()])
                ag2_cc = cx.cc.inc(inst)
                SY.wait_ge(cx.cc.h, ag2_cc)
                cx.dma_in(worhs.ap(),
                          ag2o[s].rearrange("(kc f) t -> f kc t", f=P)
                          [:, :, bass.ds(pid_sy * BLK, BLK)])
                cx.dma_in(x1b.ap(), x1sp[s].rearrange("kc f t -> f kc t"))
                rb_n = cx.dq_in.n
                x2_dve = [0, 0]
                wo_pe_last = [0, 0]
                for mo in range(MO2):
                    par = mo % 2
                    G.wait_ge(cx.pe.h, wo_pe_last[par])
                    won = cx.dma_par(G, cx.sw1, par, wo_b[:, par, :, :],
                                     wot[:, :, mo * P:(mo + 1) * P]
                                     .rearrange("kc f m -> f kc m"))
                    bank = pa[par]
                    Tn.wait_ge(cx.sw1[par].h, won)
                    Tn.wait_ge(cx.dq_in.h, rb_n)
                    Tn.wait_ge(cx.dve.h, x2_dve[par])
                    for kc in range(KC):
                        inst = Tn.matmul(bank[:, 0:BLK], wo_b[:, par, kc, :],
                                         worhs[:, kc, :],
                                         start=(kc == 0), stop=(kc == KC - 1))
                    cx.pe.inc(inst)
                    wo_pe_last[par] = cx.pe.n
                    V.wait_ge(cx.pe.h, cx.pe.n)
                    inst = V.tensor_add(x1b[:, mo, :], bank[:, 0:BLK], x1b[:, mo, :])
                    x2_dve[par] = cx.dve.inc(inst)
                S.wait_ge(cx.dve.h, cx.dve.n)
                cx.dma_out(x2sp[s].rearrange("kc f t -> f kc t"), x1b.ap())
                cx.barrier()

        if dbg == "attn":
            for s in range(NS):
                cx.dma_in(out_own[s], x2sp[s])
            SY.wait_ge(cx.dq_in.h, cx.dq_in.n)
            SY.wait_ge(cx.dq_out.h, cx.dq_out.n)
            return nc

        run_ffn_call(2, x2sp, out_own, do_ag=False)

        SY.wait_ge(cx.dq_out.h, cx.dq_out.n)
        return nc



def kernel(L0_attn_hidden_states, L0_attn_wq, L0_attn_wk, L0_attn_wv, L0_attn_wo,
           L0_attn_W1, L0_attn_b1, L0_attn_ttt_norm_weight, L0_attn_ttt_norm_bias,
           L0_ffn_router_w, L0_ffn_w1, L0_ffn_w2):
    x2d = np.ascontiguousarray(np.asarray(L0_attn_hidden_states, np.float32).reshape(T, D))
    w1 = np.asarray(L0_ffn_w1, np.float32)
    w2 = np.asarray(L0_ffn_w2, np.float32)
    rw = np.asarray(L0_ffn_router_w, np.float32)
    wq = np.asarray(L0_attn_wq, np.float32)
    wk = np.asarray(L0_attn_wk, np.float32)
    wv = np.asarray(L0_attn_wv, np.float32)
    wo = np.asarray(L0_attn_wo, np.float32)
    W1 = np.asarray(L0_attn_W1, np.float32)
    b1 = np.asarray(L0_attn_b1, np.float32)
    tnw = np.asarray(L0_attn_ttt_norm_weight, np.float32)
    tnb = np.asarray(L0_attn_ttt_norm_bias, np.float32)

    w1t = np.ascontiguousarray(w1.transpose(0, 2, 1)).reshape(E, KC, P, 2 * I1)
    w2t = np.ascontiguousarray(w2.transpose(0, 2, 1)).reshape(E, KC2, P, D)
    rwT = np.ascontiguousarray(rw.T).reshape(KC, P, E)
    wot = np.ascontiguousarray(wo).reshape(KC, P, D)
    ropeC, ropeS = _rope_tables()
    onehot_h = np.zeros((E, E * P), np.float32)
    for e in range(E):
        onehot_h[e, e * P:(e + 1) * P] = 1.0

    in_maps = []
    for c in range(NCORE):
        consts = _host_consts(c, W1, b1, tnw, tnb)
        wqkv = np.stack([
            np.ascontiguousarray(w[:, LH * HD * c:LH * HD * (c + 1)]).reshape(KC, P, LH * HD)
            for w in (wq, wk, wv)])
        m = {
            "x_own": _fm_blocks(x2d, c),
            "w1t": w1t, "w2t": w2t, "rwT": rwT,
            "wqkv": np.ascontiguousarray(wqkv), "wot": wot,
            "ropeC": ropeC, "ropeS": ropeS,
            "g_rep": consts["g_rep"], "b_rep": consts["b_rep"],
            "g2_rep": consts["g2_rep"], "gb_rep": consts["gb_rep"],
            "w1_init": consts["w1_init"], "b1_init": consts["b1_init"],
            "mneg": consts["mneg"], "trilTneg": consts["trilTneg"],
            "ones_bd": consts["ones_bd"], "onehot": onehot_h,
        }
        in_maps.append(m)

    nc = build_program()
    res = run_bass_kernel_spmd(nc, in_maps, core_ids=list(range(NCORE)))
    out = np.empty((T, D), np.float32)
    for c in range(NCORE):
        _unblocks(res.results[c]["out_own"], c, out)
    return out.reshape(1, T, D)


# version 02
# speedup vs baseline: 1.0045x; 1.0045x over previous
import numpy as np
import ml_dtypes
import concourse.bass as bass
import concourse.mybir as mybir
from concourse.bass_utils import run_bass_kernel_spmd

F32 = mybir.dt.float32
BF16 = mybir.dt.bfloat16
AF = mybir.ActivationFunctionType
ALU = mybir.AluOpType
AX = mybir.AxisListType

BF = ml_dtypes.bfloat16

T = 16384
D = 2048
P = 128
NCORE = 8
BLK = 256
NS = 8
OWN = NS * BLK
KC = D // P
NH = 32
HD = 64
LH = 4
CS = 16
CPT = 2048 // CS
E = 8
I1 = 1408
MO1 = 2 * I1 // P
NP1 = I1 // P
KC2 = I1 // P
MO2 = D // P
EPS_LN = 1e-5
EPS_TTT = 1e-6
GRP = 512
NPASS = 2
PASS_TOK = OWN // NPASS
RG = [list(range(NCORE))]

ROUND = 32

DBG_PHASE = "full"
DBG_DUMP = False
DBG_TILES = NS
DBG_CHUNKS = CPT
DBG_WO = True
DBG_SCANLV = 4
DBG_SKIP_FFN = False
DBG_NORACE = False



def _fm_blocks(x2d, core):
    out = np.empty((NS, KC, P, BLK), np.float32)
    for s in range(NS):
        rows = x2d[2048 * s + BLK * core: 2048 * s + BLK * (core + 1)]
        out[s] = rows.T.reshape(KC, P, BLK)
    return np.ascontiguousarray(out)


def _unblocks(out_own, core, acc):
    for s in range(NS):
        acc[2048 * s + BLK * core: 2048 * s + BLK * (core + 1)] = \
            out_own[s].reshape(D, BLK).T


def _host_consts(core, W1, b1, tnw, tnb):
    c = {}
    g_rep = np.zeros((P, HD), np.float32)
    b_rep = np.zeros((P, HD), np.float32)
    g2_rep = np.zeros((P, HD), np.float32)
    b1_rep = np.zeros((P, HD), np.float32)
    for l in range(LH):
        h = LH * core + l
        for t in range(CS):
            g_rep[32 * l + t] = tnw[h]
            b_rep[32 * l + t] = tnb[h]
            g2_rep[32 * l + t] = tnw[h] * tnw[h]
            b1_rep[32 * l + t] = b1[h, 0]
    c["g_rep"] = g_rep
    c["b_rep"] = b_rep
    c["g2_rep"] = g2_rep
    c["gb_rep"] = g_rep * b_rep
    c["b1_init"] = b1_rep
    w1i = np.zeros((P, 2, HD), np.float32)
    for l in range(LH):
        h = LH * core + l
        hp, hg = l % 2, l // 2
        w1i[64 * hp:64 * hp + 64, hg, :] = W1[h]
    c["w1_init"] = w1i
    eta = 1.0 / np.arange(1, CS + 1, dtype=np.float32)
    mneg = np.zeros((P, CS), np.float32)
    for l in range(LH):
        for tl in range(CS):
            for k in range(CS):
                if tl <= k:
                    mneg[32 * l + tl, k] = -eta[k]
    c["mneg"] = mneg
    tri = np.zeros((P, P), np.float32)
    for l in range(LH):
        for tl in range(CS):
            for k in range(CS):
                if tl <= k:
                    tri[32 * l + tl, 32 * l + k] = -eta[k]
    c["trilTneg"] = tri.astype(BF)
    ob = np.zeros((P, P), np.float32)
    for l in range(LH):
        ob[32 * l:32 * l + CS, 32 * l:32 * l + CS] = 1.0
    c["ones_bd"] = ob.astype(BF)
    return c


def _rope_tables():
    inv = 1.0 / (10000.0 ** (np.arange(0, HD, 2, dtype=np.float32) / HD))
    pos = np.arange(T, dtype=np.float32)
    th = np.outer(inv, pos)
    C = np.empty((P, T), np.float32)
    S = np.empty((P, T), np.float32)
    for j in range(HD // 2):
        C[2 * j] = C[2 * j + 1] = np.cos(th[j])
        S[2 * j] = -np.sin(th[j])
        S[2 * j + 1] = np.sin(th[j])
    C[64:] = C[:64]
    S[64:] = S[:64]
    return np.ascontiguousarray(C.astype(BF)), np.ascontiguousarray(S.astype(BF))



class Sem:
    def __init__(self, stack, nc, name):
        self.h = stack.enter_context(nc.semaphore(name))
        self.n = 0

    def inc(self, inst, k=1):
        inst.then_inc(self.h, k)
        self.n += k
        return self.n


class Ctx:

    def __init__(self, nc, stack):
        self.nc = nc
        self.stack = stack
        self.V = nc.vector
        self.S = nc.scalar
        self.G = nc.gpsimd
        self.Tn = nc.tensor
        self.SY = nc.sync
        self.pe = Sem(stack, nc, "pe")
        self.dve = Sem(stack, nc, "dve")
        self.act = Sem(stack, nc, "act")
        self.gp = Sem(stack, nc, "gp")
        self.dq_in = Sem(stack, nc, "dq_in")
        self.dq_w = Sem(stack, nc, "dq_w")
        self.dq_out = Sem(stack, nc, "dq_out")
        self.cc = Sem(stack, nc, "cc")
        self.sw1 = [Sem(stack, nc, "sw1a"), Sem(stack, nc, "sw1b")]
        self.sw2 = [Sem(stack, nc, "sw2a"), Sem(stack, nc, "sw2b")]
        self.sxt = [Sem(stack, nc, "sxta"), Sem(stack, nc, "sxtb")]

    def dma_par(self, eng, sems, par, dst, src):
        eng.dma_start(out=dst, in_=src).then_inc(sems[par].h, 16)
        sems[par].n += 16
        return sems[par].n

    def dma_in(self, dst, src, serialize=True):
        if serialize:
            self.SY.wait_ge(self.dq_in.h, self.dq_in.n)
        self.SY.dma_start(out=dst, in_=src).then_inc(self.dq_in.h, 16)
        self.dq_in.n += 16
        return self.dq_in.n

    def dma_w(self, dst, src, serialize=True):
        if serialize:
            self.G.wait_ge(self.dq_w.h, self.dq_w.n)
        self.G.dma_start(out=dst, in_=src).then_inc(self.dq_w.h, 16)
        self.dq_w.n += 16
        return self.dq_w.n

    def dma_out(self, dst, src, serialize=True):
        if serialize:
            self.S.wait_ge(self.dq_out.h, self.dq_out.n)
        self.S.dma_start(out=dst, in_=src).then_inc(self.dq_out.h, 16)
        self.dq_out.n += 16
        return self.dq_out.n

    def barrier(self):
        for e in (self.V, self.S, self.G, self.Tn, self.SY):
            e.wait_ge(self.pe.h, self.pe.n)
            e.wait_ge(self.dve.h, self.dve.n)
            e.wait_ge(self.act.h, self.act.n)
            e.wait_ge(self.gp.h, self.gp.n)
            e.wait_ge(self.dq_in.h, self.dq_in.n)
            e.wait_ge(self.dq_w.h, self.dq_w.n)
            e.wait_ge(self.dq_out.h, self.dq_out.n)
            for st_ in (self.sw1, self.sw2, self.sxt):
                e.wait_ge(st_[0].h, st_[0].n)
                e.wait_ge(st_[1].h, st_[1].n)



def build_program(dbg_phase=None):
    dbg = dbg_phase or DBG_PHASE
    nc = bass.Bass("TRN2", target_bir_lowering=False,
                   detect_race_conditions=not DBG_NORACE)

    x_own = nc.dram_tensor("x_own", [NS, KC, P, BLK], F32, kind="ExternalInput").ap()
    w1t = nc.dram_tensor("w1t", [E, KC, P, 2 * I1], F32, kind="ExternalInput").ap()
    w2t = nc.dram_tensor("w2t", [E, KC2, P, D], F32, kind="ExternalInput").ap()
    rwT = nc.dram_tensor("rwT", [KC, P, E], F32, kind="ExternalInput").ap()
    wqkv = nc.dram_tensor("wqkv", [3, KC, P, LH * HD], F32, kind="ExternalInput").ap()
    wot = nc.dram_tensor("wot", [KC, P, D], F32, kind="ExternalInput").ap()
    ropeC = nc.dram_tensor("ropeC", [P, T], BF16, kind="ExternalInput").ap()
    ropeS = nc.dram_tensor("ropeS", [P, T], BF16, kind="ExternalInput").ap()
    g_rep_d = nc.dram_tensor("g_rep", [P, HD], F32, kind="ExternalInput").ap()
    b_rep_d = nc.dram_tensor("b_rep", [P, HD], F32, kind="ExternalInput").ap()
    g2_rep_d = nc.dram_tensor("g2_rep", [P, HD], F32, kind="ExternalInput").ap()
    gb_rep_d = nc.dram_tensor("gb_rep", [P, HD], F32, kind="ExternalInput").ap()
    w1_init_d = nc.dram_tensor("w1_init", [P, 2, HD], F32, kind="ExternalInput").ap()
    b1_init_d = nc.dram_tensor("b1_init", [P, HD], F32, kind="ExternalInput").ap()
    mneg_d = nc.dram_tensor("mneg", [P, CS], F32, kind="ExternalInput").ap()
    tril_d = nc.dram_tensor("trilTneg", [P, P], BF16, kind="ExternalInput").ap()
    onesbd_d = nc.dram_tensor("ones_bd", [P, P], BF16, kind="ExternalInput").ap()
    onehot_d = nc.dram_tensor("onehot", [E, E * P], F32, kind="ExternalInput").ap()

    out_own = nc.dram_tensor("out_own", [NS, KC, P, BLK], F32, kind="ExternalOutput").ap()
    if DBG_DUMP:
        dbg_xn = nc.dram_tensor("dbg_xn", [P, KC, GRP], BF16, kind="ExternalOutput").ap()
        dbg_m = nc.dram_tensor("dbg_m", [E, 2, GRP], F32, kind="ExternalOutput").ap()
        dbg_yacc = nc.dram_tensor("dbg_yacc", [P, MO2, GRP], F32, kind="ExternalOutput").ap()
        dbg_logit = nc.dram_tensor("dbg_logit", [P, E], F32, kind="ExternalOutput").ap()

    x1sp = nc.dram_tensor("x1sp", [NS, KC, P, BLK], F32, kind="Internal").ap()
    x2sp = nc.dram_tensor("x2sp", [NS, KC, P, BLK], F32, kind="Internal").ap()
    ag1i = nc.dram_tensor("ag1i", [NS, D, BLK], BF16, kind="Internal").ap()
    ag1o = nc.dram_tensor("ag1o", [NS, NCORE * D, BLK], BF16, kind="Internal").ap()
    ag2i = nc.dram_tensor("ag2i", [NS, LH * HD, 2048], BF16, kind="Internal").ap()
    ag2o = nc.dram_tensor("ag2o", [NS, D, 2048], BF16, kind="Internal").ap()

    import contextlib
    stack = contextlib.ExitStack()
    with stack:
        cx = Ctx(nc, stack)
        sb = lambda name, shape, dt: stack.enter_context(nc.sbuf_tensor(name, shape, dt))
        psb = lambda name, shape, dt=F32: stack.enter_context(nc.psum_tensor(name, shape, dt))

        cpool = sb("cpool", [P, 32], F32)
        ones_k1 = sb("ones_k1", [1, P], F32)
        ones_c1 = sb("ones_c1", [P, 1], BF16)
        ident_b = sb("ident_b", [P, P], BF16)
        ident_f = sb("ident_f", [P, P], F32)
        onehot = sb("onehot_s", [E, E * P], F32)
        g_rep = sb("g_rep_s", [P, HD], F32)
        b_rep = sb("b_rep_s", [P, HD], F32)
        g2_rep = sb("g2_rep_s", [P, HD], F32)
        gb_rep = sb("gb_rep_s", [P, HD], F32)
        mneg = sb("mneg_s", [P, CS], F32)
        tril_s = sb("tril_s", [P, P], BF16)
        onesbd_s = sb("onesbd_s", [P, P], BF16)
        W1s = sb("W1s", [P, 2, HD], F32)
        b1s = sb("b1s", [P, HD], F32)
        W1bf = sb("W1bf", [P, 2, HD], BF16)

        V, S, G, Tn, SY = cx.V, cx.S, cx.G, cx.Tn, cx.SY

        cx.dma_in(g_rep[:, :], g_rep_d[:, :])
        cx.dma_in(b_rep[:, :], b_rep_d[:, :])
        cx.dma_in(g2_rep[:, :], g2_rep_d[:, :])
        cx.dma_in(gb_rep[:, :], gb_rep_d[:, :])
        cx.dma_in(mneg[:, :], mneg_d[:, :])
        cx.dma_in(tril_s[:, :], tril_d[:, :])
        cx.dma_in(onesbd_s[:, :], onesbd_d[:, :])
        cx.dma_in(W1s.ap(), w1_init_d[:, :, :])
        cx.dma_in(b1s[:, :], b1_init_d[:, :])
        for col, val in ((0, EPS_LN), (4, EPS_TTT), (8, 1.0), (12, 0.5),
                         (16, -1.0 / CS), (20, 1.0 / HD), (24, -1.0 / HD)):
            inst = V.memset(cpool[:, col:col + 1], val)
            cx.dve.inc(inst)
            V.wait_ge(cx.dve.h, cx.dve.n)
        V.memset(ones_k1[:, :], 1.0)
        V.memset(ones_c1[:, :], 1.0)
        inst = V.memset(ident_f[:, :], 1.0)
        cx.dve.inc(inst)
        G.wait_ge(cx.dve.h, cx.dve.n)
        inst = G.affine_select(ident_f[:, :], ident_f[:, :], [[-1, P]], ALU.is_equal, 0.0,
                               base=0, channel_multiplier=1)
        cx.gp.inc(inst)
        V.wait_ge(cx.gp.h, cx.gp.n)
        cx.dma_in(onehot[:, :], onehot_d[:, :])
        inst = V.tensor_copy(ident_b[:, :], ident_f[:, :])
        cx.dve.inc(inst)
        cx.barrier()

        nc.const_aps.register(F32, EPS_LN, cpool[:, 0:1])
        nc.const_aps.register(F32, EPS_TTT, cpool[:, 4:5])
        half_ap = cpool[:, 12:13]
        nleta_ap = cpool[:, 16:17]
        inv64_ap = cpool[:, 20:21]
        ninv64_ap = cpool[:, 24:25]

        ffn_call_no = [0]

        def run_ffn_call(call, src, dst, do_ag):
          ffn_call_no[0] += 1
          cn = ffn_call_no[0]
          with contextlib.ExitStack() as fstack:
            fsb = lambda name, shape, dt: fstack.enter_context(
                nc.sbuf_tensor(f"{name}_{cn}", shape, dt))
            fpsb = lambda name, shape: fstack.enter_context(
                nc.psum_tensor(f"{name}_{cn}", shape, F32))
            xb = fsb("xb", [P, KC, PASS_TOK], BF16)
            xf32 = fsb("xf32", [P, KC, GRP], F32)
            xlo = xf32.ap().bitcast(BF16)
            y_acc = fsb("y_acc", [P, MO2, PASS_TOK], F32)
            y_act = fsb("y_act", [P, NP1, PASS_TOK], BF16)
            sqs = fsb("sqs", [P, 2, GRP], BF16)
            tscr = fsb("tscr", [P, GRP], F32)
            sgb = fsb("sgb", [P, 2, GRP], BF16)
            tub = tscr.ap().bitcast(BF16).rearrange("p (g t) -> p g t", g=2)
            m_bc = fsb("m_bc", [P, E, 2, GRP], BF16)
            m_fm = fsb("m_fm", [E, 2, GRP], F32)
            rwT_s = fsb("rwT_s", [P, KC, E], BF16)
            rwT_32 = fsb("rwT_32", [P, KC, E], F32)
            rwT_lo = fsb("rwT_lo", [P, KC, E], BF16)
            w1b = fsb("w1b", [P, 2, KC, P], BF16)
            w2b = fsb("w2b", [P, 2, KC2, P], BF16)
            rowA = fsb("rowA", [1, 4 * GRP], F32)
            msk = fsb("msk", [P, 64], F32)
            pb = [fpsb(f"pb{i}", [P, GRP]) for i in range(8)]

            cx.dma_w(rwT_s.ap(), rwT[:, :, :].rearrange("kc f e -> f kc e"))
            cx.dma_in(rwT_32.ap(), rwT[:, :, :].rearrange("kc f e -> f kc e"))
            V.wait_ge(cx.dq_w.h, cx.dq_w.n)
            V.wait_ge(cx.dq_in.h, cx.dq_in.n)
            inst = V.tensor_sub(rwT_lo.ap(), rwT_32.ap(), rwT_s.ap())
            cx.dve.inc(inst)

            def emit_ln_group(src_f32_view, dst_bf_view, grp_tok, with_router, g):
                ps_sum, ps_sq, ps_b1, ps_b2, ps_lg, ps_mt = pb[0], pb[1], pb[2], pb[3], pb[4], pb[5]
                if src_f32_view is not None:
                    inst = V.tensor_copy(dst_bf_view, src_f32_view)
                    cx.dve.inc(inst)
                xready = cx.dve.n
                Tn.wait_ge(cx.dq_w.h, cx.dq_w.n)
                Tn.wait_ge(cx.dq_in.h, cx.dq_in.n)
                Tn.wait_ge(cx.dve.h, xready)
                S.wait_ge(cx.dq_w.h, cx.dq_w.n)
                S.wait_ge(cx.dq_in.h, cx.dq_in.n)
                S.wait_ge(cx.dve.h, xready)
                for kc in range(KC):
                    inst = Tn.matmul(ps_sum[0:1, 0:grp_tok], ones_c1[:, :],
                                     dst_bf_view[:, kc, :], start=(kc == 0), stop=(kc == KC - 1))
                cx.pe.inc(inst)
                for kc in range(KC):
                    par = kc % 2
                    if kc >= 2:
                        S.wait_ge(cx.pe.h, cx.pe.n)
                    S.wait_ge(cx.dve.h, xready)
                    inst = S.activation(sqs[:, par, 0:grp_tok], dst_bf_view[:, kc, :], AF.Square)
                    cx.act.inc(inst)
                    Tn.wait_ge(cx.act.h, cx.act.n)
                    inst = Tn.matmul(ps_sq[0:1, 0:grp_tok], ones_c1[:, :], sqs[:, par, 0:grp_tok],
                                     start=(kc == 0), stop=(kc == KC - 1))
                    cx.pe.inc(inst)
                mu = rowA[0:1, 0:grp_tok]
                ex2 = rowA[0:1, GRP:GRP + grp_tok]
                var = rowA[0:1, 2 * GRP:2 * GRP + grp_tok]
                sd = var
                rstd = ex2
                numu = rowA[0:1, 3 * GRP:3 * GRP + grp_tok]
                V.wait_ge(cx.pe.h, cx.pe.n)
                V.tensor_scalar_mul(mu, ps_sum[0:1, 0:grp_tok], 1.0 / D)
                V.tensor_scalar_mul(ex2, ps_sq[0:1, 0:grp_tok], 1.0 / D)
                V.tensor_tensor(var, mu, mu, ALU.mult)
                inst = V.tensor_sub(var, ex2, var)
                cx.dve.inc(inst)
                S.wait_ge(cx.dve.h, cx.dve.n)
                inst = S.activation(sd, var, AF.Sqrt, bias=EPS_LN)
                cx.act.inc(inst)
                V.wait_ge(cx.act.h, cx.act.n)
                V.reciprocal(rstd, sd)
                V.tensor_mul(numu, mu, rstd)
                inst = V.tensor_scalar_mul(numu, numu, -1.0)
                cx.dve.inc(inst)
                Tn.wait_ge(cx.dve.h, cx.dve.n)
                inst = Tn.matmul(ps_b1[:, 0:grp_tok], ones_k1[:, :], rstd, start=True, stop=True)
                inst = Tn.matmul(ps_b2[:, 0:grp_tok], ones_k1[:, :], numu, start=True, stop=True)
                cx.pe.inc(inst)
                V.wait_ge(cx.pe.h, cx.pe.n)
                for kc in range(KC):
                    V.tensor_mul(tscr[:, 0:grp_tok], dst_bf_view[:, kc, :], ps_b1[:, 0:grp_tok])
                    if with_router:
                        V.tensor_add(tscr[:, 0:grp_tok], tscr[:, 0:grp_tok], ps_b2[:, 0:grp_tok])
                        V.tensor_copy(dst_bf_view[:, kc, :], tscr[:, 0:grp_tok])
                        inst = V.tensor_sub(xlo[:, kc, 0:grp_tok], tscr[:, 0:grp_tok],
                                            dst_bf_view[:, kc, :])
                    else:
                        inst = V.tensor_add(dst_bf_view[:, kc, :], tscr[:, 0:grp_tok],
                                            ps_b2[:, 0:grp_tok])
                cx.dve.inc(inst)
                xn_ready = cx.dve.n

                if not with_router:
                    return xn_ready

                nmb = grp_tok // P
                for mb in range(nmb):
                    Tn.wait_ge(cx.dve.h, xn_ready)
                    if mb > 0:
                        Tn.wait_ge(cx.dve.h, cx.dve.n)
                    for ti, kc in enumerate([(a, b) for a in range(KC) for b in range(3)]):
                        kcc, term = kc
                        lhs = (dst_bf_view if term != 1 else xlo)[:, kcc, mb * P:(mb + 1) * P]
                        rhs = (rwT_s if term != 2 else rwT_lo)[:, kcc, :]
                        inst = Tn.matmul(ps_lg[0:P, 0:E], lhs, rhs,
                                         start=(ti == 0), stop=(ti == 3 * KC - 1))
                    cx.pe.inc(inst)
                    l_ = msk[:, 0:E]
                    mx1 = msk[:, 8:9]
                    mx2 = msk[:, 12:13]
                    dd = msk[:, 16:17]
                    ew = msk[:, 20:21]
                    inv = msk[:, 24:25]
                    teq = msk[:, 32:40]
                    sel = msk[:, 40:48]
                    expw = msk[:, 48:56]
                    mm_ = msk[:, 56:64]

                    def sv(f):
                        inst = f()
                        cx.dve.inc(inst)
                        V.drain()
                        return inst

                    V.wait_ge(cx.pe.h, cx.pe.n)
                    sv(lambda: V.tensor_copy(l_, ps_lg[0:P, 0:E]))
                    if DBG_DUMP and g == 0 and mb == 0 and ffn_call_no[0] == 1:
                        S.wait_ge(cx.dve.h, cx.dve.n)
                        cx.dma_out(dbg_logit[:, :], l_)
                    sv(lambda: V.reduce_max(mx1, l_, AX.X))
                    sv(lambda: V.tensor_tensor(teq, l_, mx1.broadcast_to([P, E]), ALU.is_equal))
                    sv(lambda: V.tensor_scalar_mul(teq, teq, -1e30))
                    sv(lambda: V.tensor_add(teq, teq, l_))
                    sv(lambda: V.reduce_max(mx2, teq, AX.X))
                    sv(lambda: V.tensor_sub(dd, mx2, mx1))
                    S.wait_ge(cx.dve.h, cx.dve.n)
                    inst = S.activation(ew, dd, AF.Exp)
                    cx.act.inc(inst)
                    V.wait_ge(cx.act.h, cx.act.n)
                    sv(lambda: V.tensor_scalar_add(ew, ew, 1.0))
                    sv(lambda: V.reciprocal(inv, ew))
                    sv(lambda: V.tensor_tensor(sel, l_, mx2.broadcast_to([P, E]), ALU.is_ge))
                    sv(lambda: V.tensor_sub(expw, l_, mx1.broadcast_to([P, E])))
                    S.wait_ge(cx.dve.h, cx.dve.n)
                    inst = S.activation(expw, expw, AF.Exp)
                    cx.act.inc(inst)
                    V.wait_ge(cx.act.h, cx.act.n)
                    sv(lambda: V.tensor_mul(mm_, sel, expw))
                    sv(lambda: V.tensor_scalar_mul(mm_, mm_, inv))
                    Tn.wait_ge(cx.dve.h, cx.dve.n)
                    inst = Tn.transpose(ps_mt[0:E, 0:P], mm_, ident_f[:, 0:P])
                    cx.pe.inc(inst)
                    V.wait_ge(cx.pe.h, cx.pe.n)
                    inst = V.tensor_copy(m_fm[:, g, mb * P:(mb + 1) * P], ps_mt[0:E, 0:P])
                    cx.dve.inc(inst)
                for e in range(E):
                    Tn.wait_ge(cx.dve.h, cx.dve.n)
                    inst = Tn.matmul(ps_lg[:, 0:grp_tok], onehot[:, e * P:(e + 1) * P],
                                     m_fm[:, g, 0:grp_tok], start=True, stop=True)
                    cx.pe.inc(inst)
                    V.wait_ge(cx.pe.h, cx.pe.n)
                    inst = V.tensor_copy(m_bc[:, e, g, 0:grp_tok], ps_lg[:, 0:grp_tok])
                    cx.dve.inc(inst)
                return xn_ready

            def emit_ffn_pass(call, p, src_, dst, do_ag):
                blocks = [4 * p + i for i in range(4)]
                for g in range(2):
                    bs = blocks[2 * g: 2 * g + 2]
                    for bi, s in enumerate(bs):
                        cx.dma_w(xb[:, :, g * GRP + bi * BLK: g * GRP + (bi + 1) * BLK],
                                 src_[s].rearrange("kc f t -> f kc t"))
                    V.wait_ge(cx.dq_w.h, cx.dq_w.n)
                    emit_ln_group(None, xb[:, :, g * GRP:(g + 1) * GRP], GRP, True, g)

                ln_done_dve = cx.dve.n
                if DBG_DUMP and call == 1 and p == 0:
                    S.wait_ge(cx.dve.h, cx.dve.n)
                    cx.dma_out(dbg_xn[:, :, :], xb[:, :, 0:GRP])
                    cx.dma_out(dbg_m[:, :, :], m_fm.ap())
                Tn.wait_ge(cx.dve.h, ln_done_dve)

                wpar1 = [0]
                wpar2 = [0]
                w1_last_use = [0, 0]
                w2_last_use = [0, 0]
                y2_evac_dve = [0, 0, 0, 0]

                def load_w1(e, mo):
                    par = wpar1[0] % 2
                    wpar1[0] += 1
                    G.wait_ge(cx.pe.h, w1_last_use[par])
                    n = cx.dma_par(G, cx.sw1, par, w1b[:, par, :, :],
                                   w1t[e, :, :, mo * P:(mo + 1) * P].rearrange("kc f m -> f kc m"))
                    return par, n

                def load_w2(e, mo2):
                    par = wpar2[0] % 2
                    wpar2[0] += 1
                    G.wait_ge(cx.pe.h, w2_last_use[par])
                    n = cx.dma_par(G, cx.sw2, par, w2b[:, par, :, :],
                                   w2t[e, :, :, mo2 * P:(mo2 + 1) * P].rearrange("kc f m -> f kc m"))
                    return par, n

                for e in range(E):
                    for i in range(NP1):
                        evac_dve = [0, 0]
                        evac_act = [0, 0]
                        for half, mo in ((0, i), (1, NP1 + i)):
                            par, wn = load_w1(e, mo)
                            Tn.wait_ge(cx.sw1[par].h, wn)
                            for g in range(2):
                                bank = pb[half * 2 + g]
                                if i > 0:
                                    if half == 0:
                                        Tn.wait_ge(cx.act.h, evac_act_prev[g])
                                    else:
                                        Tn.wait_ge(cx.dve.h, evac_dve_prev[g])
                                for kc in range(KC):
                                    inst = Tn.matmul(bank[:, :], w1b[:, par, kc, :],
                                                     xb[:, kc, g * GRP:(g + 1) * GRP],
                                                     start=(kc == 0), stop=(kc == KC - 1))
                                cx.pe.inc(inst)
                            w1_last_use[par] = cx.pe.n
                            if half == 0:
                                gate_pe = cx.pe.n
                            else:
                                up_pe = cx.pe.n
                        for g in range(2):
                            S.wait_ge(cx.pe.h, gate_pe)
                            if i > 0:
                                S.wait_ge(cx.dve.h, yact_dve[g])
                            inst = S.activation(sgb[:, g, :], pb[g][:, :], AF.Silu)
                            evac_act[g] = cx.act.inc(inst)
                            V.wait_ge(cx.pe.h, up_pe)
                            V.tensor_mul(tub[:, g, :], pb[2 + g][:, :], m_bc[:, e, g, :])
                            V.wait_ge(cx.act.h, evac_act[g])
                            inst = V.tensor_mul(y_act[:, i, g * GRP:(g + 1) * GRP],
                                                sgb[:, g, :], tub[:, g, :])
                            evac_dve[g] = cx.dve.inc(inst)
                        yact_dve = list(evac_dve)
                        evac_act_prev = list(evac_act)
                        evac_dve_prev = list(evac_dve)
                    for mo2 in range(MO2):
                        par, wn = load_w2(e, mo2)
                        Tn.wait_ge(cx.sw2[par].h, wn)
                        for g in range(2):
                            bank = pb[4 + (mo2 % 2) * 2 + g]
                            Tn.wait_ge(cx.dve.h, max(y2_evac_dve[(mo2 % 2) * 2 + g],
                                                     yact_dve[g]))
                            for kc in range(KC2):
                                inst = Tn.matmul(bank[:, :], w2b[:, par, kc, :],
                                                 y_act[:, kc, g * GRP:(g + 1) * GRP],
                                                 start=(kc == 0), stop=(kc == KC2 - 1))
                            cx.pe.inc(inst)
                            w2_last_use[par] = cx.pe.n
                            V.wait_ge(cx.pe.h, cx.pe.n)
                            if e == 0:
                                inst = V.tensor_copy(y_acc[:, mo2, g * GRP:(g + 1) * GRP], bank[:, :])
                            else:
                                inst = V.tensor_add(y_acc[:, mo2, g * GRP:(g + 1) * GRP],
                                                    y_acc[:, mo2, g * GRP:(g + 1) * GRP], bank[:, :])
                            y2_evac_dve[(mo2 % 2) * 2 + g] = cx.dve.inc(inst)

                if DBG_DUMP and call == 1 and p == 0:
                    S.wait_ge(cx.dve.h, cx.dve.n)
                    cx.dma_out(dbg_yacc[:, :, :], y_acc[:, :, 0:GRP])
                for g in range(2):
                    bs = blocks[2 * g: 2 * g + 2]
                    SY.wait_ge(cx.dq_out.h, cx.dq_out.n)
                    SY.wait_ge(cx.dve.h, cx.dve.n)
                    for bi, s in enumerate(bs):
                        cx.dma_in(xf32[:, :, bi * BLK:(bi + 1) * BLK],
                                  src_[s].rearrange("kc f t -> f kc t"))
                    V.wait_ge(cx.dq_in.h, cx.dq_in.n)
                    V.wait_ge(cx.dve.h, max(y2_evac_dve))
                    inst = V.scalar_tensor_tensor(
                        xf32.ap(), y_acc[:, :, g * GRP:(g + 1) * GRP], half_ap, xf32.ap(),
                        ALU.mult, ALU.add)
                    xnew = cx.dve.inc(inst)
                    S.wait_ge(cx.dve.h, xnew)
                    for bi, s in enumerate(bs):
                        cx.dma_out(dst[s].rearrange("kc f t -> f kc t"),
                                   xf32[:, :, bi * BLK:(bi + 1) * BLK])
                    if do_ag:
                        emit_ln_group(xf32.ap(), xb[:, :, g * GRP:(g + 1) * GRP], GRP, False, g)
                        S.wait_ge(cx.dve.h, cx.dve.n)
                        for bi, s in enumerate(bs):
                            cx.dma_out(ag1i[s].rearrange("(kc f) t -> f kc t", f=P),
                                       xb[:, :, g * GRP + bi * BLK: g * GRP + (bi + 1) * BLK])
                            G.wait_ge(cx.dq_out.h, cx.dq_out.n)
                            inst = G.collective_compute(
                                "AllGather", ALU.bypass, replica_groups=RG,
                                ins=[ag1i[s].opt()], outs=[ag1o[s].opt()])
                            cx.cc.inc(inst)
            for p in range(NPASS):
                emit_ffn_pass(call, p, src, dst, do_ag)
                cx.barrier()

        if DBG_SKIP_FFN:
            for s in range(NS):
                inst = G.collective_compute(
                    "AllGather", ALU.bypass, replica_groups=RG,
                    ins=[ag1i[s].opt()], outs=[ag1o[s].opt()])
                cx.cc.inc(inst)
        else:
            run_ffn_call(1, x_own, x1sp if dbg != "ffn1" else out_own,
                         do_ag=(dbg != "ffn1"))

        if dbg == "ffn1":
            SY.wait_ge(cx.dq_out.h, cx.dq_out.n)
            return nc


        import contextlib as _ctl
        with _ctl.ExitStack() as astack:
            asb = lambda name, shape, dt: astack.enter_context(nc.sbuf_tensor(name, shape, dt))
            apsb = lambda name: astack.enter_context(nc.psum_tensor(name, [P, GRP], F32))
            wq_s = asb("wq_s", [P, KC, 2, P], BF16)
            wk_s = asb("wk_s", [P, KC, 2, P], BF16)
            wv_s = asb("wv_s", [P, KC, 2, P], BF16)
            Qf = asb("Qf", [P, 2, 2048], BF16)
            Kf = asb("Kf", [P, 2, 2048], BF16)
            VMKf = asb("VMKf", [P, 2, 2048], BF16)
            Cs = asb("Cs", [P, 2048], BF16)
            Ss = asb("Ss", [P, 2048], BF16)
            xt = asb("xt", [P, 2, KC, BLK], BF16)
            rsc = asb("rsc", [P, 3, BLK], BF16)
            out_fm = asb("out_fm", [P, 2, 2048], BF16)
            x1b = asb("x1b", [P, KC, BLK], F32)
            worhs = asb("worhs", [P, KC, BLK], BF16)
            wo_b = asb("wo_b", [P, 2, KC, P], BF16)
            bdK = asb("bdK", [P, 2, 48], BF16)
            bdQ = asb("bdQ", [P, 2, 48], BF16)
            bdW = asb("bdW", [P, P], BF16)
            atbd = asb("atbd", [P, P], BF16)
            gbt = asb("gbt", [P, HD], F32)
            vm_s = asb("vm_s", [P, HD], BF16)
            xk_s = asb("xk_s", [P, HD], BF16)
            zs = asb("zs", [P, HD], F32)
            xh = asb("xh", [P, HD], F32)
            gxh = asb("gxh", [P, HD], F32)
            tp = asb("tp", [P, HD], F32)
            t5 = asb("t5", [P, HD], F32)
            grd = asb("grd", [P, HD], F32)
            grdbf = asb("grdbf", [P, HD], BF16)
            zbar = asb("zbar", [P, HD], F32)
            xh2 = asb("xh2", [P, HD], F32)
            ot = asb("ot", [P, HD], BF16)
            bn6 = asb("bn6", [P, 6], F32)
            st2 = asb("st2", [P, 2], F32)
            st2b = asb("st2b", [P, 2], F32)
            ssc = asb("ssc", [P, 16], F32)
            ssc2 = asb("ssc2", [P, 8], F32)
            pa = [apsb(f"pa{i}") for i in range(6)]
            pa5b = pa[5].ap().bitcast(BF16)
            p_z1 = pa[2][:, 0:HD]
            p_zq = pa[2][:, 64:128]
            p_at = pa[1][:, 0:48]
            p_zb = pa[3][:, 0:HD]
            p_w = pa[4][:, 0:P]
            p_b = pa[4][:, P:P + HD]
            p_xkT = pa5b[:, 0:HD]
            p_vmT = pa5b[:, 64:128]
            pa3b = pa[3].ap().bitcast(BF16)
            p_bt0 = pa5b[:, 128:144]
            p_bt1 = pa3b[:, 256:272]

            sd_a = ssc[:, 0:1]
            rstd_a = ssc[:, 4:5]
            aps_a = ssc[:, 8:9]
            b2n_a = ssc[:, 12:13]
            sd2_a = ssc2[:, 0:1]
            rstd2_a = ssc2[:, 4:5]

            pid_sy = SY.partition_id()

            for w_i, wbuf in ((0, wq_s), (1, wk_s), (2, wv_s)):
                cx.dma_w(wbuf.ap(),
                         wqkv[w_i].rearrange("kc f (hg m) -> f kc hg m", hg=2))
            V.memset(bdK.ap(), 0.0)
            V.memset(bdQ.ap(), 0.0)
            V.memset(bdW.ap(), 0.0)
            V.memset(atbd.ap(), 0.0)
            V.memset(zs.ap(), 0.0)
            V.memset(pa[5].ap(), 0.0)
            V.memset(pa[2].ap(), 0.0)
            inst = V.memset(out_fm.ap(), 0.0)
            cx.dve.inc(inst)
            ag1_cc = cx.cc.n - NS

            def dv(f):
                inst = f()
                cx.dve.inc(inst)
                V.drain()
                return cx.dve.n

            def gv(f):
                inst = f()
                cx.gp.inc(inst)
                G.drain()
                return cx.gp.n

            def av(f):
                inst = f()
                cx.act.inc(inst)
                S.drain()
                return cx.act.n

            for s in range(DBG_TILES):
                SY.wait_ge(cx.cc.h, ag1_cc + s + 1)
                cx.dma_in(Cs[:, :], ropeC[:, 2048 * s:2048 * (s + 1)])
                cx.dma_in(Ss[:, :], ropeS[:, 2048 * s:2048 * (s + 1)])
                xt_pe_last = [0, 0]
                for r in range(NCORE):
                    par = r % 2
                    SY.wait_ge(cx.pe.h, xt_pe_last[par])
                    xt_n = cx.dma_par(SY, cx.sxt, par, xt[:, par, :, :],
                                      ag1o[s][2048 * r:2048 * (r + 1), :]
                                      .rearrange("(kc f) t -> f kc t", f=P))
                    tok0 = BLK * r
                    for w_i, wbuf in ((0, wq_s), (1, wk_s), (2, wv_s)):
                        for hg in range(2):
                            bank = pa[(w_i * 2 + hg) % 2]
                            Tn.wait_ge(cx.sxt[par].h, xt_n)
                            Tn.wait_ge(cx.dq_w.h, cx.dq_w.n)
                            Tn.wait_ge(cx.dve.h, cx.dve.n)
                            for kc in range(KC):
                                inst = Tn.matmul(bank[:, 0:BLK], wbuf[:, kc, hg, :],
                                                 xt[:, par, kc, :],
                                                 start=(kc == 0), stop=(kc == KC - 1))
                            cx.pe.inc(inst)
                            xt_pe_last[par] = cx.pe.n
                            V.wait_ge(cx.pe.h, cx.pe.n)
                            V.wait_ge(cx.dq_in.h, cx.dq_in.n)
                            if w_i < 2:
                                dst = (Qf if w_i == 0 else Kf)
                                V.tensor_copy(rsc[:, 0, :], bank[:, 0:BLK])
                                shmask = []
                                for i2 in range(0, 32, 2):
                                    shmask += [i2 + 1, i2]
                                V.stream_shuffle(rsc[:, 1, :], rsc[:, 0, :], shmask)
                                V.tensor_mul(rsc[:, 2, :], rsc[:, 0, :],
                                             Cs[:, tok0:tok0 + BLK])
                                V.tensor_mul(rsc[:, 1, :], rsc[:, 1, :],
                                             Ss[:, tok0:tok0 + BLK])
                                inst = V.tensor_add(dst[:, hg, tok0:tok0 + BLK],
                                                    rsc[:, 2, :], rsc[:, 1, :])
                            else:
                                inst = V.tensor_sub(VMKf[:, hg, tok0:tok0 + BLK],
                                                    bank[:, 0:BLK],
                                                    Kf[:, hg, tok0:tok0 + BLK])
                            cx.dve.inc(inst)
                proj_done_dve = cx.dve.n

                Tn.wait_ge(cx.dve.h, proj_done_dve)
                V.wait_ge(cx.dve.h, proj_done_dve)
                prev_pe = cx.pe.n
                for j in range(DBG_CHUNKS):
                    sl = slice(CS * j, CS * (j + 1))
                    V.wait_ge(cx.pe.h, prev_pe)
                    for hg in range(2):
                        V.tensor_copy(bdK[0:64, hg, 0:CS], Kf[0:64, hg, sl])
                        V.tensor_copy(bdK[64:128, hg, 32:48], Kf[64:128, hg, sl])
                        V.tensor_copy(bdQ[0:64, hg, 0:CS], Qf[0:64, hg, sl])
                        V.tensor_copy(bdQ[64:128, hg, 32:48], Qf[64:128, hg, sl])
                    bd_dve = dv(lambda: V.engine_nop())
                    if DBG_SCANLV < 1:
                        prev_pe = cx.pe.n
                        continue
                    Tn.wait_ge(cx.dve.h, bd_dve)
                    Tn.wait_ge(cx.act.h, cx.act.n)
                    for l in range(LH):
                        hp, hg = l % 2, l // 2
                        inst = Tn.transpose(
                            p_xkT[32 * l:32 * l + CS, :], Kf[64 * hp:64 * hp + 64, hg, sl],
                            ident_b[64 * hp:64 * hp + 64, 64 * hp:64 * hp + 64],
                            tile_position=(64 * hp, 32 * l))
                        inst = Tn.transpose(
                            p_vmT[32 * l:32 * l + CS, :], VMKf[64 * hp:64 * hp + 64, hg, sl],
                            ident_b[64 * hp:64 * hp + 64, 64 * hp:64 * hp + 64],
                            tile_position=(64 * hp, 32 * l))
                    inst = Tn.matmul(p_at[0:48, :], bdK[:, 0, :], bdQ[:, 0, 0:48],
                                     start=True, stop=True)
                    inst = Tn.matmul(p_at[64:112, :], bdK[:, 1, :], bdQ[:, 1, 0:48],
                                     start=True, stop=True, tile_position=(0, 64))
                    cx.pe.inc(inst)
                    tr_pe = cx.pe.n
                    S.wait_ge(cx.pe.h, tr_pe)
                    av(lambda: S.copy(xk_s[:, :], p_xkT[:, :]))
                    vm_act = av(lambda: S.copy(vm_s[:, :], p_vmT[:, :]))
                    V.wait_ge(cx.act.h, vm_act)
                    for l in range(LH):
                        hp = l % 2
                        V.tensor_copy(bdW[32 * l:32 * l + CS, 64 * hp:64 * hp + 64],
                                      xk_s[32 * l:32 * l + CS, :])
                        inst = V.tensor_mul(
                            atbd[32 * l:32 * l + CS, 32 * l:32 * l + CS],
                            p_at[32 * l:32 * l + CS, 32 * hp:32 * hp + CS],
                            mneg[32 * l:32 * l + CS, :])
                    evac_dve = dv(lambda: V.engine_nop())
                    G.wait_ge(cx.act.h, vm_act)
                    gv(lambda: G.tensor_mul(gbt[:, :], g_rep[:, :], vm_s[:, :]))
                    gbt_gp = gv(lambda: G.tensor_sub(gbt[:, :], gb_rep[:, :], gbt[:, :]))
                    if DBG_SCANLV < 2:
                        prev_pe = cx.pe.n
                        continue

                    wcast = dv(lambda: V.tensor_copy(W1bf.ap(), W1s.ap()))
                    Tn.wait_ge(cx.dve.h, wcast)
                    inst = Tn.matmul(p_z1[0:48, :], bdK[:, 0, 0:48], W1bf[:, 0, :],
                                     start=True, stop=True)
                    inst = Tn.matmul(p_z1[64:112, :], bdK[:, 1, 0:48], W1bf[:, 1, :],
                                     start=True, stop=True, tile_position=(0, 64))
                    cx.pe.inc(inst)
                    z1_pe = cx.pe.n
                    V.wait_ge(cx.pe.h, z1_pe)
                    V.tensor_add(zs[0:48, :], p_z1[0:48, :], b1s[0:48, :])
                    dv(lambda: V.tensor_add(zs[64:112, :], p_z1[64:112, :], b1s[64:112, :]))
                    dv(lambda: V.bn_stats(bn6[:, :], zs[:, :]))
                    chain_dve = dv(lambda: V.bn_aggr(st2[:, :], bn6[:, :]))
                    S.wait_ge(cx.dve.h, chain_dve)
                    sd_act = av(lambda: S.activation(sd_a, st2[:, 1:2], AF.Sqrt, bias=EPS_TTT))
                    V.wait_ge(cx.act.h, sd_act)
                    dv(lambda: V.reciprocal(rstd_a, sd_a))
                    dv(lambda: V.tensor_scalar(xh[:, :], zs[:, :], st2[:, 0:1], rstd_a,
                                               ALU.subtract, ALU.mult))
                    xh_dve = cx.dve.n
                    G.wait_ge(cx.dve.h, xh_dve)
                    G.wait_ge(cx.gp.h, gbt_gp)
                    gv(lambda: G.tensor_mul(gxh[:, :], g2_rep[:, :], xh[:, :]))
                    gxh_gp = gv(lambda: G.tensor_add(gxh[:, :], gxh[:, :], gbt[:, :]))
                    V.wait_ge(cx.gp.h, gxh_gp)
                    dv(lambda: V.reduce_sum(ssc[:, 8:9], gxh[:, :], AX.X))
                    dv(lambda: V.tensor_mul(tp[:, :], gxh[:, :], xh[:, :]))
                    dv(lambda: V.reduce_sum(ssc[:, 12:13], tp[:, :], AX.X))
                    dv(lambda: V.tensor_scalar_mul(aps_a, aps_a, inv64_ap))
                    dv(lambda: V.tensor_scalar(b2n_a, b2n_a, ninv64_ap, rstd_a,
                                               ALU.mult, ALU.mult))
                    dv(lambda: V.tensor_scalar(t5[:, :], gxh[:, :], aps_a, rstd_a,
                                               ALU.subtract, ALU.mult))
                    dv(lambda: V.scalar_tensor_tensor(grd[:, :], xh[:, :], b2n_a, t5[:, :],
                                                      ALU.mult, ALU.add))
                    grad_dve = cx.dve.n
                    if DBG_SCANLV < 3:
                        prev_pe = cx.pe.n
                        continue
                    S.wait_ge(cx.dve.h, grad_dve)
                    grad_act = av(lambda: S.activation(grdbf[:, :], grd[:, :], AF.Copy))
                    if DBG_SCANLV == 30:
                        prev_pe = cx.pe.n
                        continue
                    Tn.wait_ge(cx.act.h, grad_act)
                    Tn.wait_ge(cx.dve.h, evac_dve)
                    inst = Tn.matmul(p_zb[:, :], atbd[:, :], grdbf[:, :],
                                     start=True, stop=False)
                    inst = Tn.matmul(p_zb[:, :], tril_s[:, :], grdbf[:, :],
                                     start=False, stop=True)
                    if DBG_SCANLV == 31:
                        cx.pe.inc(inst)
                        prev_pe = cx.pe.n
                        continue
                    inst = Tn.matmul(p_zq[0:48, :], bdQ[:, 0, 0:48], W1bf[:, 0, :],
                                     start=True, stop=True)
                    inst = Tn.matmul(p_zq[64:112, :], bdQ[:, 1, 0:48], W1bf[:, 1, :],
                                     start=True, stop=True, tile_position=(0, 64))
                    if DBG_SCANLV == 32:
                        cx.pe.inc(inst)
                        prev_pe = cx.pe.n
                        continue
                    if DBG_SCANLV != 326:
                        inst = Tn.matmul(p_b[:, :], onesbd_s[:, :], grdbf[:, :],
                                         start=True, stop=True)
                    if DBG_SCANLV == 325:
                        cx.pe.inc(inst)
                        prev_pe = cx.pe.n
                        continue
                    inst = Tn.matmul(pa[4][:, 0:64], bdW[0:48, :], grdbf[0:48, :],
                                     start=True, stop=True)
                    inst = Tn.matmul(pa[0][:, 0:64], bdW[64:112, :], grdbf[64:112, :],
                                     start=True, stop=True)
                    cx.pe.inc(inst)
                    upd_pe = cx.pe.n
                    if DBG_SCANLV == 33:
                        prev_pe = cx.pe.n
                        continue
                    V.wait_ge(cx.pe.h, upd_pe)
                    dv(lambda: V.tensor_add(zbar[:, :], p_zb[:, :], b1s[:, :]))
                    dv(lambda: V.tensor_add(zbar[:, :], zbar[:, :], p_zq[:, :]))
                    dv(lambda: V.scalar_tensor_tensor(
                        W1s[:, 0, :], pa[4][:, 0:64], nleta_ap, W1s[:, 0, :],
                        ALU.mult, ALU.add))
                    dv(lambda: V.scalar_tensor_tensor(
                        W1s[:, 1, :], pa[0][:, 0:64], nleta_ap, W1s[:, 1, :],
                        ALU.mult, ALU.add))
                    dv(lambda: V.scalar_tensor_tensor(b1s[:, :], p_b[:, :],
                                                      nleta_ap, b1s[:, :],
                                                      ALU.mult, ALU.add))
                    if DBG_SCANLV == 34:
                        prev_pe = cx.pe.n
                        continue
                    if DBG_SCANLV < 4:
                        prev_pe = cx.pe.n
                        continue
                    dv(lambda: V.bn_stats(bn6[:, :], zbar[:, :]))
                    lf_dve = dv(lambda: V.bn_aggr(st2b[:, :], bn6[:, :]))
                    S.wait_ge(cx.dve.h, lf_dve)
                    sd2_act = av(lambda: S.activation(sd2_a, st2b[:, 1:2], AF.Sqrt,
                                                      bias=EPS_TTT))
                    V.wait_ge(cx.act.h, sd2_act)
                    dv(lambda: V.reciprocal(rstd2_a, sd2_a))
                    xh2_dve = dv(lambda: V.tensor_scalar(xh2[:, :], zbar[:, :], st2b[:, 0:1],
                                                         rstd2_a, ALU.subtract, ALU.mult))
                    G.wait_ge(cx.dve.h, xh2_dve)
                    gv(lambda: G.tensor_mul(xh2[:, :], g_rep[:, :], xh2[:, :]))
                    ot_gp = gv(lambda: G.tensor_add(ot[:, :], xh2[:, :], b_rep[:, :]))
                    Tn.wait_ge(cx.gp.h, ot_gp)
                    Tn.wait_ge(cx.dve.h, cx.dve.n)
                    for l in range(LH):
                        hp, hg = l % 2, l // 2
                        p_bt = p_bt0 if hg == 0 else p_bt1
                        inst = Tn.transpose(
                            p_bt[64 * hp:64 * hp + 64, :],
                            ot[32 * l:32 * l + CS, :],
                            ident_b[32 * l:32 * l + CS, 32 * l:32 * l + CS],
                            tile_position=(32 * l, 64 * hp))
                    cx.pe.inc(inst)
                    bt_pe = cx.pe.n
                    V.wait_ge(cx.pe.h, bt_pe)
                    for l in range(LH):
                        hp, hg = l % 2, l // 2
                        p_bt = p_bt0 if hg == 0 else p_bt1
                        inst = V.tensor_add(out_fm[64 * hp:64 * hp + 64, hg, sl],
                                            p_bt[64 * hp:64 * hp + 64, :],
                                            Qf[64 * hp:64 * hp + 64, hg, sl])
                    cx.dve.inc(inst)
                    prev_pe = cx.pe.n

                if not DBG_WO:
                    cx.barrier()
                    continue
                S.wait_ge(cx.dve.h, cx.dve.n)
                cx.dma_out(ag2i[s].rearrange("(hg hp g) t -> (hp g) hg t", hg=2, hp=2),
                           out_fm.ap())
                G.wait_ge(cx.dq_out.h, cx.dq_out.n)
                inst = G.collective_compute(
                    "AllGather", ALU.bypass, replica_groups=RG,
                    ins=[ag2i[s].opt()], outs=[ag2o[s].opt()])
                ag2_cc = cx.cc.inc(inst)
                SY.wait_ge(cx.cc.h, ag2_cc)
                cx.dma_in(worhs.ap(),
                          ag2o[s].rearrange("(kc f) t -> f kc t", f=P)
                          [:, :, bass.ds(pid_sy * BLK, BLK)])
                cx.dma_in(x1b.ap(), x1sp[s].rearrange("kc f t -> f kc t"))
                rb_n = cx.dq_in.n
                x2_dve = [0, 0]
                wo_pe_last = [0, 0]
                for mo in range(MO2):
                    par = mo % 2
                    G.wait_ge(cx.pe.h, wo_pe_last[par])
                    won = cx.dma_par(G, cx.sw1, par, wo_b[:, par, :, :],
                                     wot[:, :, mo * P:(mo + 1) * P]
                                     .rearrange("kc f m -> f kc m"))
                    bank = pa[par]
                    Tn.wait_ge(cx.sw1[par].h, won)
                    Tn.wait_ge(cx.dq_in.h, rb_n)
                    Tn.wait_ge(cx.dve.h, x2_dve[par])
                    for kc in range(KC):
                        inst = Tn.matmul(bank[:, 0:BLK], wo_b[:, par, kc, :],
                                         worhs[:, kc, :],
                                         start=(kc == 0), stop=(kc == KC - 1))
                    cx.pe.inc(inst)
                    wo_pe_last[par] = cx.pe.n
                    V.wait_ge(cx.pe.h, cx.pe.n)
                    inst = V.tensor_add(x1b[:, mo, :], bank[:, 0:BLK], x1b[:, mo, :])
                    x2_dve[par] = cx.dve.inc(inst)
                S.wait_ge(cx.dve.h, cx.dve.n)
                cx.dma_out(x2sp[s].rearrange("kc f t -> f kc t"), x1b.ap())
                cx.barrier()

        if dbg == "attn":
            for s in range(NS):
                cx.dma_in(out_own[s], x2sp[s])
            SY.wait_ge(cx.dq_in.h, cx.dq_in.n)
            SY.wait_ge(cx.dq_out.h, cx.dq_out.n)
            return nc

        run_ffn_call(2, x2sp, out_own, do_ag=False)

        SY.wait_ge(cx.dq_out.h, cx.dq_out.n)
        return nc



def kernel(L0_attn_hidden_states, L0_attn_wq, L0_attn_wk, L0_attn_wv, L0_attn_wo,
           L0_attn_W1, L0_attn_b1, L0_attn_ttt_norm_weight, L0_attn_ttt_norm_bias,
           L0_ffn_router_w, L0_ffn_w1, L0_ffn_w2):
    x2d = np.ascontiguousarray(np.asarray(L0_attn_hidden_states, np.float32).reshape(T, D))
    w1 = np.asarray(L0_ffn_w1, np.float32)
    w2 = np.asarray(L0_ffn_w2, np.float32)
    rw = np.asarray(L0_ffn_router_w, np.float32)
    wq = np.asarray(L0_attn_wq, np.float32)
    wk = np.asarray(L0_attn_wk, np.float32)
    wv = np.asarray(L0_attn_wv, np.float32)
    wo = np.asarray(L0_attn_wo, np.float32)
    W1 = np.asarray(L0_attn_W1, np.float32)
    b1 = np.asarray(L0_attn_b1, np.float32)
    tnw = np.asarray(L0_attn_ttt_norm_weight, np.float32)
    tnb = np.asarray(L0_attn_ttt_norm_bias, np.float32)

    w1t = np.ascontiguousarray(w1.transpose(0, 2, 1)).reshape(E, KC, P, 2 * I1)
    w2t = np.ascontiguousarray(w2.transpose(0, 2, 1)).reshape(E, KC2, P, D)
    rwT = np.ascontiguousarray(rw.T).reshape(KC, P, E)
    wot = np.ascontiguousarray(wo).reshape(KC, P, D)
    ropeC, ropeS = _rope_tables()
    onehot_h = np.zeros((E, E * P), np.float32)
    for e in range(E):
        onehot_h[e, e * P:(e + 1) * P] = 1.0

    in_maps = []
    for c in range(NCORE):
        consts = _host_consts(c, W1, b1, tnw, tnb)
        wqkv = np.stack([
            np.ascontiguousarray(w[:, LH * HD * c:LH * HD * (c + 1)]).reshape(KC, P, LH * HD)
            for w in (wq, wk, wv)])
        m = {
            "x_own": _fm_blocks(x2d, c),
            "w1t": w1t, "w2t": w2t, "rwT": rwT,
            "wqkv": np.ascontiguousarray(wqkv), "wot": wot,
            "ropeC": ropeC, "ropeS": ropeS,
            "g_rep": consts["g_rep"], "b_rep": consts["b_rep"],
            "g2_rep": consts["g2_rep"], "gb_rep": consts["gb_rep"],
            "w1_init": consts["w1_init"], "b1_init": consts["b1_init"],
            "mneg": consts["mneg"], "trilTneg": consts["trilTneg"],
            "ones_bd": consts["ones_bd"], "onehot": onehot_h,
        }
        in_maps.append(m)

    nc = build_program()
    res = run_bass_kernel_spmd(nc, in_maps, core_ids=list(range(NCORE)))
    out = np.empty((T, D), np.float32)
    for c in range(NCORE):
        _unblocks(res.results[c]["out_own"], c, out)
    return out.reshape(1, T, D)


# version 03 (final)
# speedup vs baseline: 1.0165x; 1.0119x over previous
import numpy as np
import ml_dtypes
import concourse.bass as bass
import concourse.mybir as mybir
from concourse.bass_utils import run_bass_kernel_spmd

F32 = mybir.dt.float32
BF16 = mybir.dt.bfloat16
AF = mybir.ActivationFunctionType
ALU = mybir.AluOpType
AX = mybir.AxisListType

BF = ml_dtypes.bfloat16

T = 16384
D = 2048
P = 128
NCORE = 8
BLK = 256
NS = 8
OWN = NS * BLK
KC = D // P
NH = 32
HD = 64
LH = 4
CS = 16
CPT = 2048 // CS
E = 8
I1 = 1408
MO1 = 2 * I1 // P
NP1 = I1 // P
KC2 = I1 // P
MO2 = D // P
EPS_LN = 1e-5
EPS_TTT = 1e-6
GRP = 512
NPASS = 2
PASS_TOK = OWN // NPASS
RG = [list(range(NCORE))]

ROUND = 32

DBG_PHASE = "full"
DBG_DUMP = False
DBG_TILES = NS
DBG_CHUNKS = CPT
DBG_WO = True
DBG_SCANLV = 4
DBG_SKIP_FFN = False
DBG_NORACE = False



def _fm_blocks(x2d, core):
    out = np.empty((NS, KC, P, BLK), np.float32)
    for s in range(NS):
        rows = x2d[2048 * s + BLK * core: 2048 * s + BLK * (core + 1)]
        out[s] = rows.T.reshape(KC, P, BLK)
    return np.ascontiguousarray(out)


def _unblocks(out_own, core, acc):
    for s in range(NS):
        acc[2048 * s + BLK * core: 2048 * s + BLK * (core + 1)] = \
            out_own[s].reshape(D, BLK).T


def _host_consts(core, W1, b1, tnw, tnb):
    c = {}
    g_rep = np.zeros((P, HD), np.float32)
    b_rep = np.zeros((P, HD), np.float32)
    g2_rep = np.zeros((P, HD), np.float32)
    b1_rep = np.zeros((P, HD), np.float32)
    for l in range(LH):
        h = LH * core + l
        for t in range(CS):
            g_rep[32 * l + t] = tnw[h]
            b_rep[32 * l + t] = tnb[h]
            g2_rep[32 * l + t] = tnw[h] * tnw[h]
            b1_rep[32 * l + t] = b1[h, 0]
    c["g_rep"] = g_rep
    c["b_rep"] = b_rep
    c["g2_rep"] = g2_rep
    c["gb_rep"] = g_rep * b_rep
    c["b1_init"] = b1_rep
    w1i = np.zeros((P, 2, HD), np.float32)
    for l in range(LH):
        h = LH * core + l
        hp, hg = l % 2, l // 2
        w1i[64 * hp:64 * hp + 64, hg, :] = W1[h]
    c["w1_init"] = w1i
    eta = 1.0 / np.arange(1, CS + 1, dtype=np.float32)
    mneg = np.zeros((P, CS), np.float32)
    for l in range(LH):
        for tl in range(CS):
            for k in range(CS):
                if tl <= k:
                    mneg[32 * l + tl, k] = -eta[k]
    c["mneg"] = mneg
    tri = np.zeros((P, P), np.float32)
    for l in range(LH):
        for tl in range(CS):
            for k in range(CS):
                if tl <= k:
                    tri[32 * l + tl, 32 * l + k] = -eta[k]
    c["trilTneg"] = tri.astype(BF)
    ob = np.zeros((P, P), np.float32)
    for l in range(LH):
        ob[32 * l:32 * l + CS, 32 * l:32 * l + CS] = 1.0
    c["ones_bd"] = ob.astype(BF)
    return c


def _rope_tables():
    inv = 1.0 / (10000.0 ** (np.arange(0, HD, 2, dtype=np.float32) / HD))
    pos = np.arange(T, dtype=np.float32)
    th = np.outer(inv, pos)
    C = np.empty((P, T), np.float32)
    S = np.empty((P, T), np.float32)
    for j in range(HD // 2):
        C[2 * j] = C[2 * j + 1] = np.cos(th[j])
        S[2 * j] = -np.sin(th[j])
        S[2 * j + 1] = np.sin(th[j])
    C[64:] = C[:64]
    S[64:] = S[:64]
    return np.ascontiguousarray(C.astype(BF)), np.ascontiguousarray(S.astype(BF))



class Sem:
    def __init__(self, stack, nc, name):
        self.h = stack.enter_context(nc.semaphore(name))
        self.n = 0

    def inc(self, inst, k=1):
        inst.then_inc(self.h, k)
        self.n += k
        return self.n


class Ctx:

    def __init__(self, nc, stack):
        self.nc = nc
        self.stack = stack
        self.V = nc.vector
        self.S = nc.scalar
        self.G = nc.gpsimd
        self.Tn = nc.tensor
        self.SY = nc.sync
        self.pe = Sem(stack, nc, "pe")
        self.dve = Sem(stack, nc, "dve")
        self.act = Sem(stack, nc, "act")
        self.gp = Sem(stack, nc, "gp")
        self.dq_in = Sem(stack, nc, "dq_in")
        self.dq_w = Sem(stack, nc, "dq_w")
        self.dq_out = Sem(stack, nc, "dq_out")
        self.cc = Sem(stack, nc, "cc")
        self.sw1 = [Sem(stack, nc, "sw1a"), Sem(stack, nc, "sw1b")]
        self.sw2 = [Sem(stack, nc, "sw2a"), Sem(stack, nc, "sw2b")]
        self.sxt = [Sem(stack, nc, "sxta"), Sem(stack, nc, "sxtb")]

    def dma_par(self, eng, sems, par, dst, src):
        eng.dma_start(out=dst, in_=src).then_inc(sems[par].h, 16)
        sems[par].n += 16
        return sems[par].n

    def dma_in(self, dst, src, serialize=True):
        if serialize:
            self.SY.wait_ge(self.dq_in.h, self.dq_in.n)
        self.SY.dma_start(out=dst, in_=src).then_inc(self.dq_in.h, 16)
        self.dq_in.n += 16
        return self.dq_in.n

    def dma_w(self, dst, src, serialize=True):
        if serialize:
            self.G.wait_ge(self.dq_w.h, self.dq_w.n)
        self.G.dma_start(out=dst, in_=src).then_inc(self.dq_w.h, 16)
        self.dq_w.n += 16
        return self.dq_w.n

    def dma_out(self, dst, src, serialize=True):
        if serialize:
            self.S.wait_ge(self.dq_out.h, self.dq_out.n)
        self.S.dma_start(out=dst, in_=src).then_inc(self.dq_out.h, 16)
        self.dq_out.n += 16
        return self.dq_out.n

    def barrier(self):
        for e in (self.V, self.S, self.G, self.Tn, self.SY):
            e.wait_ge(self.pe.h, self.pe.n)
            e.wait_ge(self.dve.h, self.dve.n)
            e.wait_ge(self.act.h, self.act.n)
            e.wait_ge(self.gp.h, self.gp.n)
            e.wait_ge(self.dq_in.h, self.dq_in.n)
            e.wait_ge(self.dq_w.h, self.dq_w.n)
            e.wait_ge(self.dq_out.h, self.dq_out.n)
            for st_ in (self.sw1, self.sw2, self.sxt):
                e.wait_ge(st_[0].h, st_[0].n)
                e.wait_ge(st_[1].h, st_[1].n)



def build_program(dbg_phase=None):
    dbg = dbg_phase or DBG_PHASE
    nc = bass.Bass("TRN2", target_bir_lowering=False,
                   detect_race_conditions=not DBG_NORACE)

    x_own = nc.dram_tensor("x_own", [NS, KC, P, BLK], F32, kind="ExternalInput").ap()
    w1t = nc.dram_tensor("w1t", [E, KC, P, 2 * I1], F32, kind="ExternalInput").ap()
    w2t = nc.dram_tensor("w2t", [E, KC2, P, D], F32, kind="ExternalInput").ap()
    rwT = nc.dram_tensor("rwT", [KC, P, E], F32, kind="ExternalInput").ap()
    wqkv = nc.dram_tensor("wqkv", [3, KC, P, LH * HD], F32, kind="ExternalInput").ap()
    wot = nc.dram_tensor("wot", [KC, P, D], F32, kind="ExternalInput").ap()
    ropeC = nc.dram_tensor("ropeC", [P, T], BF16, kind="ExternalInput").ap()
    ropeS = nc.dram_tensor("ropeS", [P, T], BF16, kind="ExternalInput").ap()
    g_rep_d = nc.dram_tensor("g_rep", [P, HD], F32, kind="ExternalInput").ap()
    b_rep_d = nc.dram_tensor("b_rep", [P, HD], F32, kind="ExternalInput").ap()
    g2_rep_d = nc.dram_tensor("g2_rep", [P, HD], F32, kind="ExternalInput").ap()
    gb_rep_d = nc.dram_tensor("gb_rep", [P, HD], F32, kind="ExternalInput").ap()
    w1_init_d = nc.dram_tensor("w1_init", [P, 2, HD], F32, kind="ExternalInput").ap()
    b1_init_d = nc.dram_tensor("b1_init", [P, HD], F32, kind="ExternalInput").ap()
    mneg_d = nc.dram_tensor("mneg", [P, CS], F32, kind="ExternalInput").ap()
    tril_d = nc.dram_tensor("trilTneg", [P, P], BF16, kind="ExternalInput").ap()
    onesbd_d = nc.dram_tensor("ones_bd", [P, P], BF16, kind="ExternalInput").ap()
    onehot_d = nc.dram_tensor("onehot", [E, E * P], F32, kind="ExternalInput").ap()

    out_own = nc.dram_tensor("out_own", [NS, KC, P, BLK], F32, kind="ExternalOutput").ap()
    if DBG_DUMP:
        dbg_xn = nc.dram_tensor("dbg_xn", [P, KC, GRP], BF16, kind="ExternalOutput").ap()
        dbg_m = nc.dram_tensor("dbg_m", [E, 2, GRP], F32, kind="ExternalOutput").ap()
        dbg_yacc = nc.dram_tensor("dbg_yacc", [P, MO2, GRP], F32, kind="ExternalOutput").ap()
        dbg_logit = nc.dram_tensor("dbg_logit", [P, E], F32, kind="ExternalOutput").ap()

    x1sp = nc.dram_tensor("x1sp", [NS, KC, P, BLK], F32, kind="Internal").ap()
    x2sp = nc.dram_tensor("x2sp", [NS, KC, P, BLK], F32, kind="Internal").ap()
    ag1i = nc.dram_tensor("ag1i", [NS, D, BLK], BF16, kind="Internal").ap()
    ag1o = nc.dram_tensor("ag1o", [NS, NCORE * D, BLK], BF16, kind="Internal").ap()
    ag2i = nc.dram_tensor("ag2i", [NS, LH * HD, 2048], BF16, kind="Internal").ap()
    ag2o = nc.dram_tensor("ag2o", [NS, D, 2048], BF16, kind="Internal").ap()

    import contextlib
    stack = contextlib.ExitStack()
    with stack:
        cx = Ctx(nc, stack)
        sb = lambda name, shape, dt: stack.enter_context(nc.sbuf_tensor(name, shape, dt))
        psb = lambda name, shape, dt=F32: stack.enter_context(nc.psum_tensor(name, shape, dt))

        cpool = sb("cpool", [P, 32], F32)
        ones_k1 = sb("ones_k1", [1, P], F32)
        ones_c1 = sb("ones_c1", [P, 1], BF16)
        ident_b = sb("ident_b", [P, P], BF16)
        ident_f = sb("ident_f", [P, P], F32)
        onehot = sb("onehot_s", [E, E * P], F32)
        g_rep = sb("g_rep_s", [P, HD], F32)
        b_rep = sb("b_rep_s", [P, HD], F32)
        g2_rep = sb("g2_rep_s", [P, HD], F32)
        gb_rep = sb("gb_rep_s", [P, HD], F32)
        mneg = sb("mneg_s", [P, CS], F32)
        tril_s = sb("tril_s", [P, P], BF16)
        onesbd_s = sb("onesbd_s", [P, P], BF16)
        W1s = sb("W1s", [P, 2, HD], F32)
        b1s = sb("b1s", [P, HD], F32)
        W1bf = sb("W1bf", [P, 2, HD], BF16)

        V, S, G, Tn, SY = cx.V, cx.S, cx.G, cx.Tn, cx.SY

        cx.dma_in(g_rep[:, :], g_rep_d[:, :])
        cx.dma_in(b_rep[:, :], b_rep_d[:, :])
        cx.dma_in(g2_rep[:, :], g2_rep_d[:, :])
        cx.dma_in(gb_rep[:, :], gb_rep_d[:, :])
        cx.dma_in(mneg[:, :], mneg_d[:, :])
        cx.dma_in(tril_s[:, :], tril_d[:, :])
        cx.dma_in(onesbd_s[:, :], onesbd_d[:, :])
        cx.dma_in(W1s.ap(), w1_init_d[:, :, :])
        cx.dma_in(b1s[:, :], b1_init_d[:, :])
        for col, val in ((0, EPS_LN), (4, EPS_TTT), (8, 1.0), (12, 0.5),
                         (16, -1.0 / CS), (20, 1.0 / HD), (24, -1.0 / HD)):
            inst = V.memset(cpool[:, col:col + 1], val)
            cx.dve.inc(inst)
            V.wait_ge(cx.dve.h, cx.dve.n)
        V.memset(ones_k1[:, :], 1.0)
        V.memset(ones_c1[:, :], 1.0)
        inst = V.memset(ident_f[:, :], 1.0)
        cx.dve.inc(inst)
        G.wait_ge(cx.dve.h, cx.dve.n)
        inst = G.affine_select(ident_f[:, :], ident_f[:, :], [[-1, P]], ALU.is_equal, 0.0,
                               base=0, channel_multiplier=1)
        cx.gp.inc(inst)
        V.wait_ge(cx.gp.h, cx.gp.n)
        cx.dma_in(onehot[:, :], onehot_d[:, :])
        inst = V.tensor_copy(ident_b[:, :], ident_f[:, :])
        cx.dve.inc(inst)
        cx.barrier()

        nc.const_aps.register(F32, EPS_LN, cpool[:, 0:1])
        nc.const_aps.register(F32, EPS_TTT, cpool[:, 4:5])
        half_ap = cpool[:, 12:13]
        nleta_ap = cpool[:, 16:17]
        inv64_ap = cpool[:, 20:21]
        ninv64_ap = cpool[:, 24:25]

        ffn_call_no = [0]

        def run_ffn_call(call, src, dst, do_ag):
          ffn_call_no[0] += 1
          cn = ffn_call_no[0]
          with contextlib.ExitStack() as fstack:
            fsb = lambda name, shape, dt: fstack.enter_context(
                nc.sbuf_tensor(f"{name}_{cn}", shape, dt))
            fpsb = lambda name, shape: fstack.enter_context(
                nc.psum_tensor(f"{name}_{cn}", shape, F32))
            xb = fsb("xb", [P, KC, PASS_TOK], BF16)
            xf32 = fsb("xf32", [P, KC, GRP], F32)
            xlo = xf32.ap().bitcast(BF16)
            y_acc = fsb("y_acc", [P, MO2, PASS_TOK], F32)
            y_act = fsb("y_act", [P, NP1, PASS_TOK], BF16)
            sqs = fsb("sqs", [P, 2, GRP], BF16)
            tscr = fsb("tscr", [P, GRP], F32)
            sgb = fsb("sgb", [P, 2, GRP], BF16)
            tub = tscr.ap().bitcast(BF16).rearrange("p (g t) -> p g t", g=2)
            m_bc = fsb("m_bc", [P, E, 2, GRP], BF16)
            m_fm = fsb("m_fm", [E, 2, GRP], F32)
            rwT_s = fsb("rwT_s", [P, KC, E], BF16)
            rwT_32 = fsb("rwT_32", [P, KC, E], F32)
            rwT_lo = fsb("rwT_lo", [P, KC, E], BF16)
            w1b = fsb("w1b", [P, 2, KC, P], BF16)
            w2b = fsb("w2b", [P, 2, KC2, P], BF16)
            rowA = fsb("rowA", [1, 4 * GRP], F32)
            msk = fsb("msk", [P, 64], F32)
            pb = [fpsb(f"pb{i}", [P, GRP]) for i in range(8)]

            cx.dma_w(rwT_s.ap(), rwT[:, :, :].rearrange("kc f e -> f kc e"))
            cx.dma_in(rwT_32.ap(), rwT[:, :, :].rearrange("kc f e -> f kc e"))
            V.wait_ge(cx.dq_w.h, cx.dq_w.n)
            V.wait_ge(cx.dq_in.h, cx.dq_in.n)
            inst = V.tensor_sub(rwT_lo.ap(), rwT_32.ap(), rwT_s.ap())
            cx.dve.inc(inst)

            def emit_ln_group(src_f32_view, dst_bf_view, grp_tok, with_router, g):
                ps_sum, ps_sq, ps_b1, ps_b2, ps_lg, ps_mt = pb[0], pb[1], pb[2], pb[3], pb[4], pb[5]
                if src_f32_view is not None:
                    inst = V.tensor_copy(dst_bf_view, src_f32_view)
                    cx.dve.inc(inst)
                xready = cx.dve.n
                Tn.wait_ge(cx.dq_w.h, cx.dq_w.n)
                Tn.wait_ge(cx.dq_in.h, cx.dq_in.n)
                Tn.wait_ge(cx.dve.h, xready)
                S.wait_ge(cx.dq_w.h, cx.dq_w.n)
                S.wait_ge(cx.dq_in.h, cx.dq_in.n)
                S.wait_ge(cx.dve.h, xready)
                for kc in range(KC):
                    inst = Tn.matmul(ps_sum[0:1, 0:grp_tok], ones_c1[:, :],
                                     dst_bf_view[:, kc, :], start=(kc == 0), stop=(kc == KC - 1))
                cx.pe.inc(inst)
                for kc in range(KC):
                    par = kc % 2
                    if kc >= 2:
                        S.wait_ge(cx.pe.h, cx.pe.n)
                    S.wait_ge(cx.dve.h, xready)
                    inst = S.activation(sqs[:, par, 0:grp_tok], dst_bf_view[:, kc, :], AF.Square)
                    cx.act.inc(inst)
                    Tn.wait_ge(cx.act.h, cx.act.n)
                    inst = Tn.matmul(ps_sq[0:1, 0:grp_tok], ones_c1[:, :], sqs[:, par, 0:grp_tok],
                                     start=(kc == 0), stop=(kc == KC - 1))
                    cx.pe.inc(inst)
                mu = rowA[0:1, 0:grp_tok]
                ex2 = rowA[0:1, GRP:GRP + grp_tok]
                var = rowA[0:1, 2 * GRP:2 * GRP + grp_tok]
                sd = var
                rstd = ex2
                numu = rowA[0:1, 3 * GRP:3 * GRP + grp_tok]
                V.wait_ge(cx.pe.h, cx.pe.n)
                V.tensor_scalar_mul(mu, ps_sum[0:1, 0:grp_tok], 1.0 / D)
                V.tensor_scalar_mul(ex2, ps_sq[0:1, 0:grp_tok], 1.0 / D)
                V.tensor_tensor(var, mu, mu, ALU.mult)
                inst = V.tensor_sub(var, ex2, var)
                cx.dve.inc(inst)
                S.wait_ge(cx.dve.h, cx.dve.n)
                inst = S.activation(sd, var, AF.Sqrt, bias=EPS_LN)
                cx.act.inc(inst)
                V.wait_ge(cx.act.h, cx.act.n)
                V.reciprocal(rstd, sd)
                V.tensor_mul(numu, mu, rstd)
                inst = V.tensor_scalar_mul(numu, numu, -1.0)
                cx.dve.inc(inst)
                Tn.wait_ge(cx.dve.h, cx.dve.n)
                inst = Tn.matmul(ps_b1[:, 0:grp_tok], ones_k1[:, :], rstd, start=True, stop=True)
                inst = Tn.matmul(ps_b2[:, 0:grp_tok], ones_k1[:, :], numu, start=True, stop=True)
                cx.pe.inc(inst)
                V.wait_ge(cx.pe.h, cx.pe.n)
                for kc in range(KC):
                    V.tensor_mul(tscr[:, 0:grp_tok], dst_bf_view[:, kc, :], ps_b1[:, 0:grp_tok])
                    if with_router:
                        V.tensor_add(tscr[:, 0:grp_tok], tscr[:, 0:grp_tok], ps_b2[:, 0:grp_tok])
                        V.tensor_copy(dst_bf_view[:, kc, :], tscr[:, 0:grp_tok])
                        inst = V.tensor_sub(xlo[:, kc, 0:grp_tok], tscr[:, 0:grp_tok],
                                            dst_bf_view[:, kc, :])
                    else:
                        inst = V.tensor_add(dst_bf_view[:, kc, :], tscr[:, 0:grp_tok],
                                            ps_b2[:, 0:grp_tok])
                cx.dve.inc(inst)
                xn_ready = cx.dve.n

                if not with_router:
                    return xn_ready

                nmb = grp_tok // P
                for mb in range(nmb):
                    Tn.wait_ge(cx.dve.h, xn_ready)
                    if mb > 0:
                        Tn.wait_ge(cx.dve.h, cx.dve.n)
                    for ti, kc in enumerate([(a, b) for a in range(KC) for b in range(3)]):
                        kcc, term = kc
                        lhs = (dst_bf_view if term != 1 else xlo)[:, kcc, mb * P:(mb + 1) * P]
                        rhs = (rwT_s if term != 2 else rwT_lo)[:, kcc, :]
                        inst = Tn.matmul(ps_lg[0:P, 0:E], lhs, rhs,
                                         start=(ti == 0), stop=(ti == 3 * KC - 1))
                    cx.pe.inc(inst)
                    l_ = msk[:, 0:E]
                    mx1 = msk[:, 8:9]
                    mx2 = msk[:, 12:13]
                    dd = msk[:, 16:17]
                    ew = msk[:, 20:21]
                    inv = msk[:, 24:25]
                    teq = msk[:, 32:40]
                    sel = msk[:, 40:48]
                    expw = msk[:, 48:56]
                    mm_ = msk[:, 56:64]

                    def sv(f):
                        inst = f()
                        cx.dve.inc(inst)
                        V.drain()
                        return inst

                    V.wait_ge(cx.pe.h, cx.pe.n)
                    sv(lambda: V.tensor_copy(l_, ps_lg[0:P, 0:E]))
                    if DBG_DUMP and g == 0 and mb == 0 and ffn_call_no[0] == 1:
                        S.wait_ge(cx.dve.h, cx.dve.n)
                        cx.dma_out(dbg_logit[:, :], l_)
                    sv(lambda: V.reduce_max(mx1, l_, AX.X))
                    sv(lambda: V.tensor_tensor(teq, l_, mx1.broadcast_to([P, E]), ALU.is_equal))
                    sv(lambda: V.tensor_scalar_mul(teq, teq, -1e30))
                    sv(lambda: V.tensor_add(teq, teq, l_))
                    sv(lambda: V.reduce_max(mx2, teq, AX.X))
                    sv(lambda: V.tensor_sub(dd, mx2, mx1))
                    S.wait_ge(cx.dve.h, cx.dve.n)
                    inst = S.activation(ew, dd, AF.Exp)
                    cx.act.inc(inst)
                    V.wait_ge(cx.act.h, cx.act.n)
                    sv(lambda: V.tensor_scalar_add(ew, ew, 1.0))
                    sv(lambda: V.reciprocal(inv, ew))
                    sv(lambda: V.tensor_tensor(sel, l_, mx2.broadcast_to([P, E]), ALU.is_ge))
                    sv(lambda: V.tensor_sub(expw, l_, mx1.broadcast_to([P, E])))
                    S.wait_ge(cx.dve.h, cx.dve.n)
                    inst = S.activation(expw, expw, AF.Exp)
                    cx.act.inc(inst)
                    V.wait_ge(cx.act.h, cx.act.n)
                    sv(lambda: V.tensor_mul(mm_, sel, expw))
                    sv(lambda: V.tensor_scalar_mul(mm_, mm_, inv))
                    Tn.wait_ge(cx.dve.h, cx.dve.n)
                    inst = Tn.transpose(ps_mt[0:E, 0:P], mm_, ident_f[:, 0:P])
                    cx.pe.inc(inst)
                    V.wait_ge(cx.pe.h, cx.pe.n)
                    inst = V.tensor_copy(m_fm[:, g, mb * P:(mb + 1) * P], ps_mt[0:E, 0:P])
                    cx.dve.inc(inst)
                for e in range(E):
                    Tn.wait_ge(cx.dve.h, cx.dve.n)
                    inst = Tn.matmul(ps_lg[:, 0:grp_tok], onehot[:, e * P:(e + 1) * P],
                                     m_fm[:, g, 0:grp_tok], start=True, stop=True)
                    cx.pe.inc(inst)
                    V.wait_ge(cx.pe.h, cx.pe.n)
                    inst = V.tensor_copy(m_bc[:, e, g, 0:grp_tok], ps_lg[:, 0:grp_tok])
                    cx.dve.inc(inst)
                return xn_ready

            def emit_ffn_pass(call, p, src_, dst, do_ag):
                blocks = [4 * p + i for i in range(4)]
                for g in range(2):
                    bs = blocks[2 * g: 2 * g + 2]
                    for bi, s in enumerate(bs):
                        cx.dma_w(xb[:, :, g * GRP + bi * BLK: g * GRP + (bi + 1) * BLK],
                                 src_[s].rearrange("kc f t -> f kc t"))
                    V.wait_ge(cx.dq_w.h, cx.dq_w.n)
                    emit_ln_group(None, xb[:, :, g * GRP:(g + 1) * GRP], GRP, True, g)

                ln_done_dve = cx.dve.n
                if DBG_DUMP and call == 1 and p == 0:
                    S.wait_ge(cx.dve.h, cx.dve.n)
                    cx.dma_out(dbg_xn[:, :, :], xb[:, :, 0:GRP])
                    cx.dma_out(dbg_m[:, :, :], m_fm.ap())
                Tn.wait_ge(cx.dve.h, ln_done_dve)

                wpar1 = [0]
                wpar2 = [0]
                w1_last_use = [0, 0]
                w2_last_use = [0, 0]
                y2_evac_dve = [0, 0, 0, 0]

                def load_w1(e, mo):
                    par = wpar1[0] % 2
                    wpar1[0] += 1
                    G.wait_ge(cx.pe.h, w1_last_use[par])
                    n = cx.dma_par(G, cx.sw1, par, w1b[:, par, :, :],
                                   w1t[e, :, :, mo * P:(mo + 1) * P].rearrange("kc f m -> f kc m"))
                    return par, n

                def load_w2(e, mo2):
                    par = wpar2[0] % 2
                    wpar2[0] += 1
                    G.wait_ge(cx.pe.h, w2_last_use[par])
                    n = cx.dma_par(G, cx.sw2, par, w2b[:, par, :, :],
                                   w2t[e, :, :, mo2 * P:(mo2 + 1) * P].rearrange("kc f m -> f kc m"))
                    return par, n

                for e in range(E):
                    for i in range(NP1):
                        evac_dve = [0, 0]
                        evac_act = [0, 0]
                        for half, mo in ((0, i), (1, NP1 + i)):
                            par, wn = load_w1(e, mo)
                            Tn.wait_ge(cx.sw1[par].h, wn)
                            for g in range(2):
                                bank = pb[half * 2 + g]
                                if i > 0:
                                    if half == 0:
                                        Tn.wait_ge(cx.act.h, evac_act_prev[g])
                                    else:
                                        Tn.wait_ge(cx.dve.h, evac_dve_prev[g])
                                for kc in range(KC):
                                    inst = Tn.matmul(bank[:, :], w1b[:, par, kc, :],
                                                     xb[:, kc, g * GRP:(g + 1) * GRP],
                                                     start=(kc == 0), stop=(kc == KC - 1))
                                cx.pe.inc(inst)
                            w1_last_use[par] = cx.pe.n
                            if half == 0:
                                gate_pe = cx.pe.n
                            else:
                                up_pe = cx.pe.n
                        for g in range(2):
                            S.wait_ge(cx.pe.h, gate_pe)
                            if i > 0:
                                S.wait_ge(cx.dve.h, yact_dve[g])
                            inst = S.activation(sgb[:, g, :], pb[g][:, :], AF.Silu)
                            evac_act[g] = cx.act.inc(inst)
                            V.wait_ge(cx.pe.h, up_pe)
                            V.tensor_mul(tub[:, g, :], pb[2 + g][:, :], m_bc[:, e, g, :])
                            V.wait_ge(cx.act.h, evac_act[g])
                            inst = V.tensor_mul(y_act[:, i, g * GRP:(g + 1) * GRP],
                                                sgb[:, g, :], tub[:, g, :])
                            evac_dve[g] = cx.dve.inc(inst)
                        yact_dve = list(evac_dve)
                        evac_act_prev = list(evac_act)
                        evac_dve_prev = list(evac_dve)
                    for mo2 in range(MO2):
                        par, wn = load_w2(e, mo2)
                        Tn.wait_ge(cx.sw2[par].h, wn)
                        for g in range(2):
                            bank = pb[4 + (mo2 % 2) * 2 + g]
                            Tn.wait_ge(cx.dve.h, max(y2_evac_dve[(mo2 % 2) * 2 + g],
                                                     yact_dve[g]))
                            for kc in range(KC2):
                                inst = Tn.matmul(bank[:, :], w2b[:, par, kc, :],
                                                 y_act[:, kc, g * GRP:(g + 1) * GRP],
                                                 start=(kc == 0), stop=(kc == KC2 - 1))
                            cx.pe.inc(inst)
                            w2_last_use[par] = cx.pe.n
                            V.wait_ge(cx.pe.h, cx.pe.n)
                            if e == 0:
                                inst = V.tensor_copy(y_acc[:, mo2, g * GRP:(g + 1) * GRP], bank[:, :])
                            else:
                                inst = V.tensor_add(y_acc[:, mo2, g * GRP:(g + 1) * GRP],
                                                    y_acc[:, mo2, g * GRP:(g + 1) * GRP], bank[:, :])
                            y2_evac_dve[(mo2 % 2) * 2 + g] = cx.dve.inc(inst)

                if DBG_DUMP and call == 1 and p == 0:
                    S.wait_ge(cx.dve.h, cx.dve.n)
                    cx.dma_out(dbg_yacc[:, :, :], y_acc[:, :, 0:GRP])
                for g in range(2):
                    bs = blocks[2 * g: 2 * g + 2]
                    SY.wait_ge(cx.dq_out.h, cx.dq_out.n)
                    SY.wait_ge(cx.dve.h, cx.dve.n)
                    for bi, s in enumerate(bs):
                        cx.dma_in(xf32[:, :, bi * BLK:(bi + 1) * BLK],
                                  src_[s].rearrange("kc f t -> f kc t"))
                    V.wait_ge(cx.dq_in.h, cx.dq_in.n)
                    V.wait_ge(cx.dve.h, max(y2_evac_dve))
                    inst = V.scalar_tensor_tensor(
                        xf32.ap(), y_acc[:, :, g * GRP:(g + 1) * GRP], half_ap, xf32.ap(),
                        ALU.mult, ALU.add)
                    xnew = cx.dve.inc(inst)
                    S.wait_ge(cx.dve.h, xnew)
                    for bi, s in enumerate(bs):
                        cx.dma_out(dst[s].rearrange("kc f t -> f kc t"),
                                   xf32[:, :, bi * BLK:(bi + 1) * BLK])
                    if do_ag:
                        emit_ln_group(xf32.ap(), xb[:, :, g * GRP:(g + 1) * GRP], GRP, False, g)
                        S.wait_ge(cx.dve.h, cx.dve.n)
                        for bi, s in enumerate(bs):
                            cx.dma_out(ag1i[s].rearrange("(kc f) t -> f kc t", f=P),
                                       xb[:, :, g * GRP + bi * BLK: g * GRP + (bi + 1) * BLK])
                            G.wait_ge(cx.dq_out.h, cx.dq_out.n)
                            inst = G.collective_compute(
                                "AllGather", ALU.bypass, replica_groups=RG,
                                ins=[ag1i[s].opt()], outs=[ag1o[s].opt()])
                            cx.cc.inc(inst)
            for p in range(NPASS):
                emit_ffn_pass(call, p, src, dst, do_ag)
                cx.barrier()

        if DBG_SKIP_FFN:
            for s in range(NS):
                inst = G.collective_compute(
                    "AllGather", ALU.bypass, replica_groups=RG,
                    ins=[ag1i[s].opt()], outs=[ag1o[s].opt()])
                cx.cc.inc(inst)
        else:
            run_ffn_call(1, x_own, x1sp if dbg != "ffn1" else out_own,
                         do_ag=(dbg != "ffn1"))

        if dbg == "ffn1":
            SY.wait_ge(cx.dq_out.h, cx.dq_out.n)
            return nc


        import contextlib as _ctl
        with _ctl.ExitStack() as astack:
            asb = lambda name, shape, dt: astack.enter_context(nc.sbuf_tensor(name, shape, dt))
            apsb = lambda name: astack.enter_context(nc.psum_tensor(name, [P, GRP], F32))
            wq_s = asb("wq_s", [P, KC, 2, P], BF16)
            wk_s = asb("wk_s", [P, KC, 2, P], BF16)
            wv_s = asb("wv_s", [P, KC, 2, P], BF16)
            Qf = asb("Qf", [P, 2, 2048], BF16)
            Kf = asb("Kf", [P, 2, 2048], BF16)
            VMKf = asb("VMKf", [P, 2, 2048], BF16)
            Cs = asb("Cs", [P, 2048], BF16)
            Ss = asb("Ss", [P, 2048], BF16)
            xt = asb("xt", [P, 2, KC, BLK], BF16)
            rsc = asb("rsc", [P, 3, BLK], BF16)
            out_fm = asb("out_fm", [P, 2, 2048], BF16)
            x1b = asb("x1b", [P, KC, BLK], F32)
            worhs = asb("worhs", [P, KC, BLK], BF16)
            wo_b = asb("wo_b", [P, 2, KC, P], BF16)
            bdK = asb("bdK", [P, 2, 48], BF16)
            bdQ = asb("bdQ", [P, 2, 48], BF16)
            bdW = asb("bdW", [P, P], BF16)
            atbd = asb("atbd", [P, P], BF16)
            gbt = asb("gbt", [P, HD], F32)
            vm_s = asb("vm_s", [P, HD], BF16)
            xk_s = asb("xk_s", [P, HD], BF16)
            zs = asb("zs", [P, HD], F32)
            xh = asb("xh", [P, HD], F32)
            gxh = asb("gxh", [P, HD], F32)
            tp = asb("tp", [P, HD], F32)
            t5 = asb("t5", [P, HD], F32)
            grd = asb("grd", [P, HD], F32)
            grdbf = asb("grdbf", [P, HD], BF16)
            zbar = asb("zbar", [P, HD], F32)
            xh2 = asb("xh2", [P, HD], F32)
            ot = asb("ot", [P, HD], BF16)
            bn6 = asb("bn6", [P, 6], F32)
            st2 = asb("st2", [P, 2], F32)
            st2b = asb("st2b", [P, 2], F32)
            ssc = asb("ssc", [P, 16], F32)
            ssc2 = asb("ssc2", [P, 8], F32)
            pa = [apsb(f"pa{i}") for i in range(6)]
            pa5b = pa[5].ap().bitcast(BF16)
            p_z1 = pa[2][:, 0:HD]
            p_zq = pa[2][:, 64:128]
            p_at = pa[1][:, 0:48]
            p_zb = pa[3][:, 0:HD]
            p_w = pa[4][:, 0:P]
            p_b = pa[4][:, P:P + HD]
            p_xkT = pa5b[:, 0:HD]
            p_vmT = pa5b[:, 64:128]
            pa3b = pa[3].ap().bitcast(BF16)
            p_bt0 = pa5b[:, 128:144]
            p_bt1 = pa3b[:, 256:272]

            sd_a = ssc[:, 0:1]
            rstd_a = ssc[:, 4:5]
            aps_a = ssc[:, 8:9]
            b2n_a = ssc[:, 12:13]
            sd2_a = ssc2[:, 0:1]
            rstd2_a = ssc2[:, 4:5]

            pid_sy = SY.partition_id()

            for w_i, wbuf in ((0, wq_s), (1, wk_s), (2, wv_s)):
                cx.dma_w(wbuf.ap(),
                         wqkv[w_i].rearrange("kc f (hg m) -> f kc hg m", hg=2))
            V.memset(bdK.ap(), 0.0)
            V.memset(bdQ.ap(), 0.0)
            V.memset(bdW.ap(), 0.0)
            V.memset(atbd.ap(), 0.0)
            V.memset(zs.ap(), 0.0)
            V.memset(pa[5].ap(), 0.0)
            V.memset(pa[2].ap(), 0.0)
            inst = V.memset(out_fm.ap(), 0.0)
            cx.dve.inc(inst)
            ag1_cc = cx.cc.n - NS

            def dv(f):
                inst = f()
                cx.dve.inc(inst)
                V.drain()
                return cx.dve.n

            def gv(f):
                inst = f()
                cx.gp.inc(inst)
                G.drain()
                return cx.gp.n

            def av(f):
                inst = f()
                cx.act.inc(inst)
                S.drain()
                return cx.act.n

            for s in range(DBG_TILES):
                SY.wait_ge(cx.cc.h, ag1_cc + s + 1)
                cx.dma_in(Cs[:, :], ropeC[:, 2048 * s:2048 * (s + 1)])
                cx.dma_in(Ss[:, :], ropeS[:, 2048 * s:2048 * (s + 1)])
                xt_pe_last = [0, 0]
                for r in range(NCORE):
                    par = r % 2
                    SY.wait_ge(cx.pe.h, xt_pe_last[par])
                    xt_n = cx.dma_par(SY, cx.sxt, par, xt[:, par, :, :],
                                      ag1o[s][2048 * r:2048 * (r + 1), :]
                                      .rearrange("(kc f) t -> f kc t", f=P))
                    tok0 = BLK * r
                    for w_i, wbuf in ((0, wq_s), (1, wk_s), (2, wv_s)):
                        for hg in range(2):
                            bank = pa[(w_i * 2 + hg) % 2]
                            Tn.wait_ge(cx.sxt[par].h, xt_n)
                            Tn.wait_ge(cx.dq_w.h, cx.dq_w.n)
                            Tn.wait_ge(cx.dve.h, cx.dve.n)
                            for kc in range(KC):
                                inst = Tn.matmul(bank[:, 0:BLK], wbuf[:, kc, hg, :],
                                                 xt[:, par, kc, :],
                                                 start=(kc == 0), stop=(kc == KC - 1))
                            cx.pe.inc(inst)
                            xt_pe_last[par] = cx.pe.n
                            V.wait_ge(cx.pe.h, cx.pe.n)
                            V.wait_ge(cx.dq_in.h, cx.dq_in.n)
                            if w_i < 2:
                                dst = (Qf if w_i == 0 else Kf)
                                V.tensor_copy(rsc[:, 0, :], bank[:, 0:BLK])
                                shmask = []
                                for i2 in range(0, 32, 2):
                                    shmask += [i2 + 1, i2]
                                V.stream_shuffle(rsc[:, 1, :], rsc[:, 0, :], shmask)
                                V.tensor_mul(rsc[:, 2, :], rsc[:, 0, :],
                                             Cs[:, tok0:tok0 + BLK])
                                V.tensor_mul(rsc[:, 1, :], rsc[:, 1, :],
                                             Ss[:, tok0:tok0 + BLK])
                                inst = V.tensor_add(dst[:, hg, tok0:tok0 + BLK],
                                                    rsc[:, 2, :], rsc[:, 1, :])
                            else:
                                inst = V.tensor_sub(VMKf[:, hg, tok0:tok0 + BLK],
                                                    bank[:, 0:BLK],
                                                    Kf[:, hg, tok0:tok0 + BLK])
                            cx.dve.inc(inst)
                proj_done_dve = cx.dve.n

                Tn.wait_ge(cx.dve.h, proj_done_dve)
                V.wait_ge(cx.dve.h, proj_done_dve)
                prev_pe = cx.pe.n
                pend_bt = None

                def flush_bt(pbt):
                    bt_pe_, sl_ = pbt
                    V.wait_ge(cx.pe.h, bt_pe_)
                    for l in range(LH):
                        hp, hg = l % 2, l // 2
                        p_bt = p_bt0 if hg == 0 else p_bt1
                        inst = V.tensor_add(out_fm[64 * hp:64 * hp + 64, hg, sl_],
                                            p_bt[64 * hp:64 * hp + 64, :],
                                            Qf[64 * hp:64 * hp + 64, hg, sl_])
                    cx.dve.inc(inst)

                for j in range(DBG_CHUNKS):
                    sl = slice(CS * j, CS * (j + 1))
                    V.wait_ge(cx.pe.h, prev_pe)
                    if pend_bt is not None:
                        flush_bt(pend_bt)
                        pend_bt = None
                    for hg in range(2):
                        V.tensor_copy(bdK[0:64, hg, 0:CS], Kf[0:64, hg, sl])
                        V.tensor_copy(bdK[64:128, hg, 32:48], Kf[64:128, hg, sl])
                        V.tensor_copy(bdQ[0:64, hg, 0:CS], Qf[0:64, hg, sl])
                        V.tensor_copy(bdQ[64:128, hg, 32:48], Qf[64:128, hg, sl])
                    bd_dve = dv(lambda: V.engine_nop())
                    if DBG_SCANLV < 1:
                        prev_pe = cx.pe.n
                        continue
                    Tn.wait_ge(cx.dve.h, bd_dve)
                    Tn.wait_ge(cx.act.h, cx.act.n)
                    for l in range(LH):
                        hp, hg = l % 2, l // 2
                        inst = Tn.transpose(
                            p_xkT[32 * l:32 * l + CS, :], Kf[64 * hp:64 * hp + 64, hg, sl],
                            ident_b[64 * hp:64 * hp + 64, 64 * hp:64 * hp + 64],
                            tile_position=(64 * hp, 32 * l))
                        inst = Tn.transpose(
                            p_vmT[32 * l:32 * l + CS, :], VMKf[64 * hp:64 * hp + 64, hg, sl],
                            ident_b[64 * hp:64 * hp + 64, 64 * hp:64 * hp + 64],
                            tile_position=(64 * hp, 32 * l))
                    inst = Tn.matmul(p_at[0:48, :], bdK[:, 0, :], bdQ[:, 0, 0:48],
                                     start=True, stop=True)
                    inst = Tn.matmul(p_at[64:112, :], bdK[:, 1, :], bdQ[:, 1, 0:48],
                                     start=True, stop=True, tile_position=(0, 64))
                    cx.pe.inc(inst)
                    tr_pe = cx.pe.n
                    S.wait_ge(cx.pe.h, tr_pe)
                    av(lambda: S.copy(xk_s[:, :], p_xkT[:, :]))
                    vm_act = av(lambda: S.copy(vm_s[:, :], p_vmT[:, :]))
                    V.wait_ge(cx.act.h, vm_act)
                    for l in range(LH):
                        hp = l % 2
                        V.tensor_copy(bdW[32 * l:32 * l + CS, 64 * hp:64 * hp + 64],
                                      xk_s[32 * l:32 * l + CS, :])
                        inst = V.tensor_mul(
                            atbd[32 * l:32 * l + CS, 32 * l:32 * l + CS],
                            p_at[32 * l:32 * l + CS, 32 * hp:32 * hp + CS],
                            mneg[32 * l:32 * l + CS, :])
                    evac_dve = dv(lambda: V.engine_nop())
                    V.wait_ge(cx.act.h, vm_act)
                    dv(lambda: V.tensor_mul(gbt[:, :], g_rep[:, :], vm_s[:, :]))
                    dv(lambda: V.tensor_sub(gbt[:, :], gb_rep[:, :], gbt[:, :]))
                    if DBG_SCANLV < 2:
                        prev_pe = cx.pe.n
                        continue

                    wcast = dv(lambda: V.tensor_copy(W1bf.ap(), W1s.ap()))
                    Tn.wait_ge(cx.dve.h, wcast)
                    inst = Tn.matmul(p_z1[0:48, :], bdK[:, 0, 0:48], W1bf[:, 0, :],
                                     start=True, stop=True)
                    inst = Tn.matmul(p_z1[64:112, :], bdK[:, 1, 0:48], W1bf[:, 1, :],
                                     start=True, stop=True, tile_position=(0, 64))
                    cx.pe.inc(inst)
                    z1_pe = cx.pe.n
                    V.wait_ge(cx.pe.h, z1_pe)
                    V.tensor_add(zs[0:48, :], p_z1[0:48, :], b1s[0:48, :])
                    dv(lambda: V.tensor_add(zs[64:112, :], p_z1[64:112, :], b1s[64:112, :]))
                    dv(lambda: V.bn_stats(bn6[:, :], zs[:, :]))
                    chain_dve = dv(lambda: V.bn_aggr(st2[:, :], bn6[:, :]))
                    S.wait_ge(cx.dve.h, chain_dve)
                    sd_act = av(lambda: S.activation(sd_a, st2[:, 1:2], AF.Sqrt, bias=EPS_TTT))
                    V.wait_ge(cx.act.h, sd_act)
                    dv(lambda: V.reciprocal(rstd_a, sd_a))
                    dv(lambda: V.tensor_scalar(xh[:, :], zs[:, :], st2[:, 0:1], rstd_a,
                                               ALU.subtract, ALU.mult))
                    dv(lambda: V.tensor_mul(gxh[:, :], g2_rep[:, :], xh[:, :]))
                    dv(lambda: V.tensor_add(gxh[:, :], gxh[:, :], gbt[:, :]))
                    dv(lambda: V.reduce_sum(ssc[:, 8:9], gxh[:, :], AX.X))
                    dv(lambda: V.tensor_mul(tp[:, :], gxh[:, :], xh[:, :]))
                    dv(lambda: V.reduce_sum(ssc[:, 12:13], tp[:, :], AX.X))
                    dv(lambda: V.tensor_scalar_mul(aps_a, aps_a, inv64_ap))
                    dv(lambda: V.tensor_scalar(b2n_a, b2n_a, ninv64_ap, rstd_a,
                                               ALU.mult, ALU.mult))
                    dv(lambda: V.tensor_scalar(t5[:, :], gxh[:, :], aps_a, rstd_a,
                                               ALU.subtract, ALU.mult))
                    dv(lambda: V.scalar_tensor_tensor(grd[:, :], xh[:, :], b2n_a, t5[:, :],
                                                      ALU.mult, ALU.add))
                    grad_dve = dv(lambda: V.tensor_copy(grdbf[:, :], grd[:, :]))
                    Tn.wait_ge(cx.dve.h, grad_dve)
                    inst = Tn.matmul(p_zb[:, :], atbd[:, :], grdbf[:, :],
                                     start=True, stop=False)
                    inst = Tn.matmul(p_zb[:, :], tril_s[:, :], grdbf[:, :],
                                     start=False, stop=True)
                    if DBG_SCANLV == 31:
                        cx.pe.inc(inst)
                        prev_pe = cx.pe.n
                        continue
                    inst = Tn.matmul(p_zq[0:48, :], bdQ[:, 0, 0:48], W1bf[:, 0, :],
                                     start=True, stop=True)
                    inst = Tn.matmul(p_zq[64:112, :], bdQ[:, 1, 0:48], W1bf[:, 1, :],
                                     start=True, stop=True, tile_position=(0, 64))
                    if DBG_SCANLV == 32:
                        cx.pe.inc(inst)
                        prev_pe = cx.pe.n
                        continue
                    if DBG_SCANLV != 326:
                        inst = Tn.matmul(p_b[:, :], onesbd_s[:, :], grdbf[:, :],
                                         start=True, stop=True)
                    if DBG_SCANLV == 325:
                        cx.pe.inc(inst)
                        prev_pe = cx.pe.n
                        continue
                    inst = Tn.matmul(pa[4][:, 0:64], bdW[0:48, :], grdbf[0:48, :],
                                     start=True, stop=True)
                    inst = Tn.matmul(pa[0][:, 0:64], bdW[64:112, :], grdbf[64:112, :],
                                     start=True, stop=True)
                    cx.pe.inc(inst)
                    upd_pe = cx.pe.n
                    if DBG_SCANLV == 33:
                        prev_pe = cx.pe.n
                        continue
                    V.wait_ge(cx.pe.h, upd_pe)
                    dv(lambda: V.tensor_add(zbar[:, :], p_zb[:, :], b1s[:, :]))
                    dv(lambda: V.tensor_add(zbar[:, :], zbar[:, :], p_zq[:, :]))
                    dv(lambda: V.scalar_tensor_tensor(
                        W1s[:, 0, :], pa[4][:, 0:64], nleta_ap, W1s[:, 0, :],
                        ALU.mult, ALU.add))
                    dv(lambda: V.scalar_tensor_tensor(
                        W1s[:, 1, :], pa[0][:, 0:64], nleta_ap, W1s[:, 1, :],
                        ALU.mult, ALU.add))
                    dv(lambda: V.scalar_tensor_tensor(b1s[:, :], p_b[:, :],
                                                      nleta_ap, b1s[:, :],
                                                      ALU.mult, ALU.add))
                    if DBG_SCANLV == 34:
                        prev_pe = cx.pe.n
                        continue
                    if DBG_SCANLV < 4:
                        prev_pe = cx.pe.n
                        continue
                    dv(lambda: V.bn_stats(bn6[:, :], zbar[:, :]))
                    lf_dve = dv(lambda: V.bn_aggr(st2b[:, :], bn6[:, :]))
                    S.wait_ge(cx.dve.h, lf_dve)
                    sd2_act = av(lambda: S.activation(sd2_a, st2b[:, 1:2], AF.Sqrt,
                                                      bias=EPS_TTT))
                    V.wait_ge(cx.act.h, sd2_act)
                    dv(lambda: V.reciprocal(rstd2_a, sd2_a))
                    dv(lambda: V.tensor_scalar(xh2[:, :], zbar[:, :], st2b[:, 0:1],
                                               rstd2_a, ALU.subtract, ALU.mult))
                    dv(lambda: V.tensor_mul(xh2[:, :], g_rep[:, :], xh2[:, :]))
                    ot_dve = dv(lambda: V.tensor_add(ot[:, :], xh2[:, :], b_rep[:, :]))
                    Tn.wait_ge(cx.dve.h, ot_dve)
                    for l in range(LH):
                        hp, hg = l % 2, l // 2
                        p_bt = p_bt0 if hg == 0 else p_bt1
                        inst = Tn.transpose(
                            p_bt[64 * hp:64 * hp + 64, :],
                            ot[32 * l:32 * l + CS, :],
                            ident_b[32 * l:32 * l + CS, 32 * l:32 * l + CS],
                            tile_position=(32 * l, 64 * hp))
                    cx.pe.inc(inst)
                    pend_bt = (cx.pe.n, sl)
                    prev_pe = upd_pe

                if pend_bt is not None:
                    flush_bt(pend_bt)
                    pend_bt = None
                if not DBG_WO:
                    cx.barrier()
                    continue
                S.wait_ge(cx.dve.h, cx.dve.n)
                cx.dma_out(ag2i[s].rearrange("(hg hp g) t -> (hp g) hg t", hg=2, hp=2),
                           out_fm.ap())
                G.wait_ge(cx.dq_out.h, cx.dq_out.n)
                inst = G.collective_compute(
                    "AllGather", ALU.bypass, replica_groups=RG,
                    ins=[ag2i[s].opt()], outs=[ag2o[s].opt()])
                ag2_cc = cx.cc.inc(inst)
                SY.wait_ge(cx.cc.h, ag2_cc)
                cx.dma_in(worhs.ap(),
                          ag2o[s].rearrange("(kc f) t -> f kc t", f=P)
                          [:, :, bass.ds(pid_sy * BLK, BLK)])
                cx.dma_in(x1b.ap(), x1sp[s].rearrange("kc f t -> f kc t"))
                rb_n = cx.dq_in.n
                x2_dve = [0, 0]
                wo_pe_last = [0, 0]
                for mo in range(MO2):
                    par = mo % 2
                    G.wait_ge(cx.pe.h, wo_pe_last[par])
                    won = cx.dma_par(G, cx.sw1, par, wo_b[:, par, :, :],
                                     wot[:, :, mo * P:(mo + 1) * P]
                                     .rearrange("kc f m -> f kc m"))
                    bank = pa[par]
                    Tn.wait_ge(cx.sw1[par].h, won)
                    Tn.wait_ge(cx.dq_in.h, rb_n)
                    Tn.wait_ge(cx.dve.h, x2_dve[par])
                    for kc in range(KC):
                        inst = Tn.matmul(bank[:, 0:BLK], wo_b[:, par, kc, :],
                                         worhs[:, kc, :],
                                         start=(kc == 0), stop=(kc == KC - 1))
                    cx.pe.inc(inst)
                    wo_pe_last[par] = cx.pe.n
                    V.wait_ge(cx.pe.h, cx.pe.n)
                    inst = V.tensor_add(x1b[:, mo, :], bank[:, 0:BLK], x1b[:, mo, :])
                    x2_dve[par] = cx.dve.inc(inst)
                S.wait_ge(cx.dve.h, cx.dve.n)
                cx.dma_out(x2sp[s].rearrange("kc f t -> f kc t"), x1b.ap())
                cx.barrier()

        if dbg == "attn":
            for s in range(NS):
                cx.dma_in(out_own[s], x2sp[s])
            SY.wait_ge(cx.dq_in.h, cx.dq_in.n)
            SY.wait_ge(cx.dq_out.h, cx.dq_out.n)
            return nc

        run_ffn_call(2, x2sp, out_own, do_ag=False)

        SY.wait_ge(cx.dq_out.h, cx.dq_out.n)
        return nc



def kernel(L0_attn_hidden_states, L0_attn_wq, L0_attn_wk, L0_attn_wv, L0_attn_wo,
           L0_attn_W1, L0_attn_b1, L0_attn_ttt_norm_weight, L0_attn_ttt_norm_bias,
           L0_ffn_router_w, L0_ffn_w1, L0_ffn_w2):
    x2d = np.ascontiguousarray(np.asarray(L0_attn_hidden_states, np.float32).reshape(T, D))
    w1 = np.asarray(L0_ffn_w1, np.float32)
    w2 = np.asarray(L0_ffn_w2, np.float32)
    rw = np.asarray(L0_ffn_router_w, np.float32)
    wq = np.asarray(L0_attn_wq, np.float32)
    wk = np.asarray(L0_attn_wk, np.float32)
    wv = np.asarray(L0_attn_wv, np.float32)
    wo = np.asarray(L0_attn_wo, np.float32)
    W1 = np.asarray(L0_attn_W1, np.float32)
    b1 = np.asarray(L0_attn_b1, np.float32)
    tnw = np.asarray(L0_attn_ttt_norm_weight, np.float32)
    tnb = np.asarray(L0_attn_ttt_norm_bias, np.float32)

    w1t = np.ascontiguousarray(w1.transpose(0, 2, 1)).reshape(E, KC, P, 2 * I1)
    w2t = np.ascontiguousarray(w2.transpose(0, 2, 1)).reshape(E, KC2, P, D)
    rwT = np.ascontiguousarray(rw.T).reshape(KC, P, E)
    wot = np.ascontiguousarray(wo).reshape(KC, P, D)
    ropeC, ropeS = _rope_tables()
    onehot_h = np.zeros((E, E * P), np.float32)
    for e in range(E):
        onehot_h[e, e * P:(e + 1) * P] = 1.0

    in_maps = []
    for c in range(NCORE):
        consts = _host_consts(c, W1, b1, tnw, tnb)
        wqkv = np.stack([
            np.ascontiguousarray(w[:, LH * HD * c:LH * HD * (c + 1)]).reshape(KC, P, LH * HD)
            for w in (wq, wk, wv)])
        m = {
            "x_own": _fm_blocks(x2d, c),
            "w1t": w1t, "w2t": w2t, "rwT": rwT,
            "wqkv": np.ascontiguousarray(wqkv), "wot": wot,
            "ropeC": ropeC, "ropeS": ropeS,
            "g_rep": consts["g_rep"], "b_rep": consts["b_rep"],
            "g2_rep": consts["g2_rep"], "gb_rep": consts["gb_rep"],
            "w1_init": consts["w1_init"], "b1_init": consts["b1_init"],
            "mneg": consts["mneg"], "trilTneg": consts["trilTneg"],
            "ones_bd": consts["ones_bd"], "onehot": onehot_h,
        }
        in_maps.append(m)

    nc = build_program()
    res = run_bass_kernel_spmd(nc, in_maps, core_ids=list(range(NCORE)))
    out = np.empty((T, D), np.float32)
    for c in range(NCORE):
        _unblocks(res.results[c]["out_own"], c, out)
    return out.reshape(1, T, D)


# version 04
# speedup vs baseline: 1.0433x; 1.0264x over previous
import numpy as np
import ml_dtypes
import concourse.bass as bass
import concourse.mybir as mybir
from concourse.bass_utils import run_bass_kernel_spmd

F32 = mybir.dt.float32
BF16 = mybir.dt.bfloat16
AF = mybir.ActivationFunctionType
ALU = mybir.AluOpType
AX = mybir.AxisListType

BF = ml_dtypes.bfloat16

T = 16384
D = 2048
P = 128
NCORE = 8
BLK = 256
NS = 8
OWN = NS * BLK
KC = D // P
NH = 32
HD = 64
LH = 4
CS = 16
CPT = 2048 // CS
E = 8
I1 = 1408
MO1 = 2 * I1 // P
NP1 = I1 // P
KC2 = I1 // P
MO2 = D // P
EPS_LN = 1e-5
EPS_TTT = 1e-6
GRP = 512
NPASS = 2
PASS_TOK = OWN // NPASS
RG = [list(range(NCORE))]

ROUND = 32

DBG_PHASE = "full"
DBG_DUMP = False
DBG_TILES = NS
DBG_CHUNKS = CPT
DBG_WO = True
DBG_SCANLV = 4
DBG_SKIP_FFN = False
DBG_NORACE = False



def _fm_blocks(x2d, core):
    out = np.empty((NS, KC, P, BLK), np.float32)
    for s in range(NS):
        rows = x2d[2048 * s + BLK * core: 2048 * s + BLK * (core + 1)]
        out[s] = rows.T.reshape(KC, P, BLK)
    return np.ascontiguousarray(out)


def _unblocks(out_own, core, acc):
    for s in range(NS):
        acc[2048 * s + BLK * core: 2048 * s + BLK * (core + 1)] = \
            out_own[s].reshape(D, BLK).T


def _host_consts(core, W1, b1, tnw, tnb):
    c = {}
    g_rep = np.zeros((P, HD), np.float32)
    b_rep = np.zeros((P, HD), np.float32)
    g2_rep = np.zeros((P, HD), np.float32)
    b1_rep = np.zeros((P, HD), np.float32)
    for l in range(LH):
        h = LH * core + l
        for t in range(CS):
            g_rep[32 * l + t] = tnw[h]
            b_rep[32 * l + t] = tnb[h]
            g2_rep[32 * l + t] = tnw[h] * tnw[h]
            b1_rep[32 * l + t] = b1[h, 0]
    c["g_rep"] = g_rep
    c["b_rep"] = b_rep
    c["g2_rep"] = g2_rep
    c["gb_rep"] = g_rep * b_rep
    c["b1_init"] = b1_rep
    w1i = np.zeros((P, 2, HD), np.float32)
    for l in range(LH):
        h = LH * core + l
        hp, hg = l % 2, l // 2
        w1i[64 * hp:64 * hp + 64, hg, :] = W1[h]
    c["w1_init"] = w1i
    eta = 1.0 / np.arange(1, CS + 1, dtype=np.float32)
    mneg = np.zeros((P, CS), np.float32)
    for l in range(LH):
        for tl in range(CS):
            for k in range(CS):
                if tl <= k:
                    mneg[32 * l + tl, k] = -eta[k]
    c["mneg"] = mneg
    tri = np.zeros((P, P), np.float32)
    for l in range(LH):
        for tl in range(CS):
            for k in range(CS):
                if tl <= k:
                    tri[32 * l + tl, 32 * l + k] = -eta[k]
    c["trilTneg"] = tri.astype(BF)
    ob = np.zeros((P, P), np.float32)
    for l in range(LH):
        ob[32 * l:32 * l + CS, 32 * l:32 * l + CS] = 1.0
    c["ones_bd"] = ob.astype(BF)
    return c


def _rope_tables():
    inv = 1.0 / (10000.0 ** (np.arange(0, HD, 2, dtype=np.float32) / HD))
    pos = np.arange(T, dtype=np.float32)
    th = np.outer(inv, pos)
    C = np.empty((P, T), np.float32)
    S = np.empty((P, T), np.float32)
    for j in range(HD // 2):
        C[2 * j] = C[2 * j + 1] = np.cos(th[j])
        S[2 * j] = -np.sin(th[j])
        S[2 * j + 1] = np.sin(th[j])
    C[64:] = C[:64]
    S[64:] = S[:64]
    return np.ascontiguousarray(C.astype(BF)), np.ascontiguousarray(S.astype(BF))



class Sem:
    def __init__(self, stack, nc, name):
        self.h = stack.enter_context(nc.semaphore(name))
        self.n = 0

    def inc(self, inst, k=1):
        inst.then_inc(self.h, k)
        self.n += k
        return self.n


class Ctx:

    def __init__(self, nc, stack):
        self.nc = nc
        self.stack = stack
        self.V = nc.vector
        self.S = nc.scalar
        self.G = nc.gpsimd
        self.Tn = nc.tensor
        self.SY = nc.sync
        self.pe = Sem(stack, nc, "pe")
        self.dve = Sem(stack, nc, "dve")
        self.act = Sem(stack, nc, "act")
        self.gp = Sem(stack, nc, "gp")
        self.dq_in = Sem(stack, nc, "dq_in")
        self.dq_w = Sem(stack, nc, "dq_w")
        self.dq_out = Sem(stack, nc, "dq_out")
        self.cc = Sem(stack, nc, "cc")
        self.sw1 = [Sem(stack, nc, "sw1a"), Sem(stack, nc, "sw1b")]
        self.sw2 = [Sem(stack, nc, "sw2a"), Sem(stack, nc, "sw2b")]
        self.sxt = [Sem(stack, nc, "sxta"), Sem(stack, nc, "sxtb")]

    def dma_par(self, eng, sems, par, dst, src):
        eng.dma_start(out=dst, in_=src).then_inc(sems[par].h, 16)
        sems[par].n += 16
        return sems[par].n

    def dma_in(self, dst, src, serialize=True):
        if serialize:
            self.SY.wait_ge(self.dq_in.h, self.dq_in.n)
        self.SY.dma_start(out=dst, in_=src).then_inc(self.dq_in.h, 16)
        self.dq_in.n += 16
        return self.dq_in.n

    def dma_w(self, dst, src, serialize=True):
        if serialize:
            self.G.wait_ge(self.dq_w.h, self.dq_w.n)
        self.G.dma_start(out=dst, in_=src).then_inc(self.dq_w.h, 16)
        self.dq_w.n += 16
        return self.dq_w.n

    def dma_out(self, dst, src, serialize=True):
        if serialize:
            self.S.wait_ge(self.dq_out.h, self.dq_out.n)
        self.S.dma_start(out=dst, in_=src).then_inc(self.dq_out.h, 16)
        self.dq_out.n += 16
        return self.dq_out.n

    def barrier(self):
        for e in (self.V, self.S, self.G, self.Tn, self.SY):
            e.wait_ge(self.pe.h, self.pe.n)
            e.wait_ge(self.dve.h, self.dve.n)
            e.wait_ge(self.act.h, self.act.n)
            e.wait_ge(self.gp.h, self.gp.n)
            e.wait_ge(self.dq_in.h, self.dq_in.n)
            e.wait_ge(self.dq_w.h, self.dq_w.n)
            e.wait_ge(self.dq_out.h, self.dq_out.n)
            for st_ in (self.sw1, self.sw2, self.sxt):
                e.wait_ge(st_[0].h, st_[0].n)
                e.wait_ge(st_[1].h, st_[1].n)



def build_program(dbg_phase=None):
    dbg = dbg_phase or DBG_PHASE
    nc = bass.Bass("TRN2", target_bir_lowering=False,
                   detect_race_conditions=not DBG_NORACE)

    x_own = nc.dram_tensor("x_own", [NS, KC, P, BLK], F32, kind="ExternalInput").ap()
    w1t = nc.dram_tensor("w1t", [E, KC, P, 2 * I1], F32, kind="ExternalInput").ap()
    w2t = nc.dram_tensor("w2t", [E, KC2, P, D], F32, kind="ExternalInput").ap()
    rwT = nc.dram_tensor("rwT", [KC, P, E], F32, kind="ExternalInput").ap()
    wqkv = nc.dram_tensor("wqkv", [3, KC, P, LH * HD], F32, kind="ExternalInput").ap()
    wot = nc.dram_tensor("wot", [KC, P, D], F32, kind="ExternalInput").ap()
    ropeC = nc.dram_tensor("ropeC", [P, T], BF16, kind="ExternalInput").ap()
    ropeS = nc.dram_tensor("ropeS", [P, T], BF16, kind="ExternalInput").ap()
    g_rep_d = nc.dram_tensor("g_rep", [P, HD], F32, kind="ExternalInput").ap()
    b_rep_d = nc.dram_tensor("b_rep", [P, HD], F32, kind="ExternalInput").ap()
    g2_rep_d = nc.dram_tensor("g2_rep", [P, HD], F32, kind="ExternalInput").ap()
    gb_rep_d = nc.dram_tensor("gb_rep", [P, HD], F32, kind="ExternalInput").ap()
    w1_init_d = nc.dram_tensor("w1_init", [P, 2, HD], F32, kind="ExternalInput").ap()
    b1_init_d = nc.dram_tensor("b1_init", [P, HD], F32, kind="ExternalInput").ap()
    mneg_d = nc.dram_tensor("mneg", [P, CS], F32, kind="ExternalInput").ap()
    tril_d = nc.dram_tensor("trilTneg", [P, P], BF16, kind="ExternalInput").ap()
    onesbd_d = nc.dram_tensor("ones_bd", [P, P], BF16, kind="ExternalInput").ap()
    onehot_d = nc.dram_tensor("onehot", [E, E * P], F32, kind="ExternalInput").ap()

    out_own = nc.dram_tensor("out_own", [NS, KC, P, BLK], F32, kind="ExternalOutput").ap()
    if DBG_DUMP:
        dbg_xn = nc.dram_tensor("dbg_xn", [P, KC, GRP], BF16, kind="ExternalOutput").ap()
        dbg_m = nc.dram_tensor("dbg_m", [E, 2, GRP], F32, kind="ExternalOutput").ap()
        dbg_yacc = nc.dram_tensor("dbg_yacc", [P, MO2, GRP], F32, kind="ExternalOutput").ap()
        dbg_logit = nc.dram_tensor("dbg_logit", [P, E], F32, kind="ExternalOutput").ap()

    x1sp = nc.dram_tensor("x1sp", [NS, KC, P, BLK], F32, kind="Internal").ap()
    x2sp = nc.dram_tensor("x2sp", [NS, KC, P, BLK], F32, kind="Internal").ap()
    ag1i = nc.dram_tensor("ag1i", [NS, D, BLK], BF16, kind="Internal").ap()
    ag1o = nc.dram_tensor("ag1o", [NS, NCORE * D, BLK], BF16, kind="Internal").ap()
    ag2i = nc.dram_tensor("ag2i", [NS, LH * HD, 2048], BF16, kind="Internal").ap()
    ag2o = nc.dram_tensor("ag2o", [NS, D, 2048], BF16, kind="Internal").ap()

    import contextlib
    stack = contextlib.ExitStack()
    with stack:
        cx = Ctx(nc, stack)
        sb = lambda name, shape, dt: stack.enter_context(nc.sbuf_tensor(name, shape, dt))
        psb = lambda name, shape, dt=F32: stack.enter_context(nc.psum_tensor(name, shape, dt))

        cpool = sb("cpool", [P, 32], F32)
        ones_k1 = sb("ones_k1", [1, P], F32)
        ones_c1 = sb("ones_c1", [P, 1], BF16)
        ident_b = sb("ident_b", [P, P], BF16)
        ident_f = sb("ident_f", [P, P], F32)
        onehot = sb("onehot_s", [E, E * P], F32)
        g_rep = sb("g_rep_s", [P, HD], F32)
        b_rep = sb("b_rep_s", [P, HD], F32)
        g2_rep = sb("g2_rep_s", [P, HD], F32)
        gb_rep = sb("gb_rep_s", [P, HD], F32)
        mneg = sb("mneg_s", [P, CS], F32)
        tril_s = sb("tril_s", [P, P], BF16)
        onesbd_s = sb("onesbd_s", [P, P], BF16)
        W1s = sb("W1s", [P, 2, HD], F32)
        b1s = sb("b1s", [P, HD], F32)
        W1bf = sb("W1bf", [P, 2, HD], BF16)

        V, S, G, Tn, SY = cx.V, cx.S, cx.G, cx.Tn, cx.SY

        cx.dma_in(g_rep[:, :], g_rep_d[:, :])
        cx.dma_in(b_rep[:, :], b_rep_d[:, :])
        cx.dma_in(g2_rep[:, :], g2_rep_d[:, :])
        cx.dma_in(gb_rep[:, :], gb_rep_d[:, :])
        cx.dma_in(mneg[:, :], mneg_d[:, :])
        cx.dma_in(tril_s[:, :], tril_d[:, :])
        cx.dma_in(onesbd_s[:, :], onesbd_d[:, :])
        cx.dma_in(W1s.ap(), w1_init_d[:, :, :])
        cx.dma_in(b1s[:, :], b1_init_d[:, :])
        for col, val in ((0, EPS_LN), (4, EPS_TTT), (8, 1.0), (12, 0.5),
                         (16, -1.0 / CS), (20, 1.0 / HD), (24, -1.0 / HD)):
            inst = V.memset(cpool[:, col:col + 1], val)
            cx.dve.inc(inst)
            V.wait_ge(cx.dve.h, cx.dve.n)
        V.memset(ones_k1[:, :], 1.0)
        V.memset(ones_c1[:, :], 1.0)
        inst = V.memset(ident_f[:, :], 1.0)
        cx.dve.inc(inst)
        G.wait_ge(cx.dve.h, cx.dve.n)
        inst = G.affine_select(ident_f[:, :], ident_f[:, :], [[-1, P]], ALU.is_equal, 0.0,
                               base=0, channel_multiplier=1)
        cx.gp.inc(inst)
        V.wait_ge(cx.gp.h, cx.gp.n)
        cx.dma_in(onehot[:, :], onehot_d[:, :])
        inst = V.tensor_copy(ident_b[:, :], ident_f[:, :])
        cx.dve.inc(inst)
        cx.barrier()

        nc.const_aps.register(F32, EPS_LN, cpool[:, 0:1])
        nc.const_aps.register(F32, EPS_TTT, cpool[:, 4:5])
        half_ap = cpool[:, 12:13]
        nleta_ap = cpool[:, 16:17]
        inv64_ap = cpool[:, 20:21]
        ninv64_ap = cpool[:, 24:25]

        ffn_call_no = [0]

        def run_ffn_call(call, src, dst, do_ag):
          ffn_call_no[0] += 1
          cn = ffn_call_no[0]
          with contextlib.ExitStack() as fstack:
            fsb = lambda name, shape, dt: fstack.enter_context(
                nc.sbuf_tensor(f"{name}_{cn}", shape, dt))
            fpsb = lambda name, shape: fstack.enter_context(
                nc.psum_tensor(f"{name}_{cn}", shape, F32))
            xb = fsb("xb", [P, KC, PASS_TOK], BF16)
            xf32 = fsb("xf32", [P, KC, GRP], F32)
            xlo = xf32.ap().bitcast(BF16)
            y_acc = fsb("y_acc", [P, MO2, PASS_TOK], F32)
            y_act = fsb("y_act", [P, NP1, PASS_TOK], BF16)
            sqs = fsb("sqs", [P, 2, GRP], BF16)
            tscr = fsb("tscr", [P, GRP], F32)
            sgb = fsb("sgb", [P, 2, GRP], BF16)
            tub = tscr.ap().bitcast(BF16).rearrange("p (g t) -> p g t", g=2)
            m_bc = fsb("m_bc", [P, E, 2, GRP], BF16)
            m_fm = fsb("m_fm", [E, 2, GRP], F32)
            rwT_s = fsb("rwT_s", [P, KC, E], BF16)
            rwT_32 = fsb("rwT_32", [P, KC, E], F32)
            rwT_lo = fsb("rwT_lo", [P, KC, E], BF16)
            w1b = fsb("w1b", [P, 2, KC, P], BF16)
            w2b = fsb("w2b", [P, 2, KC2, P], BF16)
            rowA = fsb("rowA", [1, 4 * GRP], F32)
            msk = fsb("msk", [P, 64], F32)
            pb = [fpsb(f"pb{i}", [P, GRP]) for i in range(8)]

            cx.dma_w(rwT_s.ap(), rwT[:, :, :].rearrange("kc f e -> f kc e"))
            cx.dma_in(rwT_32.ap(), rwT[:, :, :].rearrange("kc f e -> f kc e"))
            V.wait_ge(cx.dq_w.h, cx.dq_w.n)
            V.wait_ge(cx.dq_in.h, cx.dq_in.n)
            inst = V.tensor_sub(rwT_lo.ap(), rwT_32.ap(), rwT_s.ap())
            cx.dve.inc(inst)

            def emit_ln_group(src_f32_view, dst_bf_view, grp_tok, with_router, g):
                ps_sum, ps_sq, ps_b1, ps_b2, ps_lg, ps_mt = pb[0], pb[1], pb[2], pb[3], pb[4], pb[5]
                if src_f32_view is not None:
                    inst = V.tensor_copy(dst_bf_view, src_f32_view)
                    cx.dve.inc(inst)
                xready = cx.dve.n
                Tn.wait_ge(cx.dq_w.h, cx.dq_w.n)
                Tn.wait_ge(cx.dq_in.h, cx.dq_in.n)
                Tn.wait_ge(cx.dve.h, xready)
                S.wait_ge(cx.dq_w.h, cx.dq_w.n)
                S.wait_ge(cx.dq_in.h, cx.dq_in.n)
                S.wait_ge(cx.dve.h, xready)
                for kc in range(KC):
                    inst = Tn.matmul(ps_sum[0:1, 0:grp_tok], ones_c1[:, :],
                                     dst_bf_view[:, kc, :], start=(kc == 0), stop=(kc == KC - 1))
                cx.pe.inc(inst)
                for kc in range(KC):
                    par = kc % 2
                    if kc >= 2:
                        S.wait_ge(cx.pe.h, cx.pe.n)
                    S.wait_ge(cx.dve.h, xready)
                    inst = S.activation(sqs[:, par, 0:grp_tok], dst_bf_view[:, kc, :], AF.Square)
                    cx.act.inc(inst)
                    Tn.wait_ge(cx.act.h, cx.act.n)
                    inst = Tn.matmul(ps_sq[0:1, 0:grp_tok], ones_c1[:, :], sqs[:, par, 0:grp_tok],
                                     start=(kc == 0), stop=(kc == KC - 1))
                    cx.pe.inc(inst)
                mu = rowA[0:1, 0:grp_tok]
                ex2 = rowA[0:1, GRP:GRP + grp_tok]
                var = rowA[0:1, 2 * GRP:2 * GRP + grp_tok]
                sd = var
                rstd = ex2
                numu = rowA[0:1, 3 * GRP:3 * GRP + grp_tok]
                V.wait_ge(cx.pe.h, cx.pe.n)
                V.tensor_scalar_mul(mu, ps_sum[0:1, 0:grp_tok], 1.0 / D)
                V.tensor_scalar_mul(ex2, ps_sq[0:1, 0:grp_tok], 1.0 / D)
                V.tensor_tensor(var, mu, mu, ALU.mult)
                inst = V.tensor_sub(var, ex2, var)
                cx.dve.inc(inst)
                S.wait_ge(cx.dve.h, cx.dve.n)
                inst = S.activation(sd, var, AF.Sqrt, bias=EPS_LN)
                cx.act.inc(inst)
                V.wait_ge(cx.act.h, cx.act.n)
                V.reciprocal(rstd, sd)
                V.tensor_mul(numu, mu, rstd)
                inst = V.tensor_scalar_mul(numu, numu, -1.0)
                cx.dve.inc(inst)
                Tn.wait_ge(cx.dve.h, cx.dve.n)
                inst = Tn.matmul(ps_b1[:, 0:grp_tok], ones_k1[:, :], rstd, start=True, stop=True)
                inst = Tn.matmul(ps_b2[:, 0:grp_tok], ones_k1[:, :], numu, start=True, stop=True)
                cx.pe.inc(inst)
                V.wait_ge(cx.pe.h, cx.pe.n)
                for kc in range(KC):
                    V.tensor_mul(tscr[:, 0:grp_tok], dst_bf_view[:, kc, :], ps_b1[:, 0:grp_tok])
                    if with_router:
                        V.tensor_add(tscr[:, 0:grp_tok], tscr[:, 0:grp_tok], ps_b2[:, 0:grp_tok])
                        V.tensor_copy(dst_bf_view[:, kc, :], tscr[:, 0:grp_tok])
                        inst = V.tensor_sub(xlo[:, kc, 0:grp_tok], tscr[:, 0:grp_tok],
                                            dst_bf_view[:, kc, :])
                    else:
                        inst = V.tensor_add(dst_bf_view[:, kc, :], tscr[:, 0:grp_tok],
                                            ps_b2[:, 0:grp_tok])
                cx.dve.inc(inst)
                xn_ready = cx.dve.n

                if not with_router:
                    return xn_ready

                nmb = grp_tok // P
                for mb in range(nmb):
                    Tn.wait_ge(cx.dve.h, xn_ready)
                    if mb > 0:
                        Tn.wait_ge(cx.dve.h, cx.dve.n)
                    for ti, kc in enumerate([(a, b) for a in range(KC) for b in range(3)]):
                        kcc, term = kc
                        lhs = (dst_bf_view if term != 1 else xlo)[:, kcc, mb * P:(mb + 1) * P]
                        rhs = (rwT_s if term != 2 else rwT_lo)[:, kcc, :]
                        inst = Tn.matmul(ps_lg[0:P, 0:E], lhs, rhs,
                                         start=(ti == 0), stop=(ti == 3 * KC - 1))
                    cx.pe.inc(inst)
                    l_ = msk[:, 0:E]
                    mx1 = msk[:, 8:9]
                    mx2 = msk[:, 12:13]
                    dd = msk[:, 16:17]
                    ew = msk[:, 20:21]
                    inv = msk[:, 24:25]
                    teq = msk[:, 32:40]
                    sel = msk[:, 40:48]
                    expw = msk[:, 48:56]
                    mm_ = msk[:, 56:64]

                    def sv(f):
                        inst = f()
                        cx.dve.inc(inst)
                        V.drain()
                        return inst

                    V.wait_ge(cx.pe.h, cx.pe.n)
                    sv(lambda: V.tensor_copy(l_, ps_lg[0:P, 0:E]))
                    if DBG_DUMP and g == 0 and mb == 0 and ffn_call_no[0] == 1:
                        S.wait_ge(cx.dve.h, cx.dve.n)
                        cx.dma_out(dbg_logit[:, :], l_)
                    sv(lambda: V.reduce_max(mx1, l_, AX.X))
                    sv(lambda: V.tensor_tensor(teq, l_, mx1.broadcast_to([P, E]), ALU.is_equal))
                    sv(lambda: V.tensor_scalar_mul(teq, teq, -1e30))
                    sv(lambda: V.tensor_add(teq, teq, l_))
                    sv(lambda: V.reduce_max(mx2, teq, AX.X))
                    sv(lambda: V.tensor_sub(dd, mx2, mx1))
                    S.wait_ge(cx.dve.h, cx.dve.n)
                    inst = S.activation(ew, dd, AF.Exp)
                    cx.act.inc(inst)
                    V.wait_ge(cx.act.h, cx.act.n)
                    sv(lambda: V.tensor_scalar_add(ew, ew, 1.0))
                    sv(lambda: V.reciprocal(inv, ew))
                    sv(lambda: V.tensor_tensor(sel, l_, mx2.broadcast_to([P, E]), ALU.is_ge))
                    sv(lambda: V.tensor_sub(expw, l_, mx1.broadcast_to([P, E])))
                    S.wait_ge(cx.dve.h, cx.dve.n)
                    inst = S.activation(expw, expw, AF.Exp)
                    cx.act.inc(inst)
                    V.wait_ge(cx.act.h, cx.act.n)
                    sv(lambda: V.tensor_mul(mm_, sel, expw))
                    sv(lambda: V.tensor_scalar_mul(mm_, mm_, inv))
                    Tn.wait_ge(cx.dve.h, cx.dve.n)
                    inst = Tn.transpose(ps_mt[0:E, 0:P], mm_, ident_f[:, 0:P])
                    cx.pe.inc(inst)
                    V.wait_ge(cx.pe.h, cx.pe.n)
                    inst = V.tensor_copy(m_fm[:, g, mb * P:(mb + 1) * P], ps_mt[0:E, 0:P])
                    cx.dve.inc(inst)
                for e in range(E):
                    Tn.wait_ge(cx.dve.h, cx.dve.n)
                    inst = Tn.matmul(ps_lg[:, 0:grp_tok], onehot[:, e * P:(e + 1) * P],
                                     m_fm[:, g, 0:grp_tok], start=True, stop=True)
                    cx.pe.inc(inst)
                    V.wait_ge(cx.pe.h, cx.pe.n)
                    inst = V.tensor_copy(m_bc[:, e, g, 0:grp_tok], ps_lg[:, 0:grp_tok])
                    cx.dve.inc(inst)
                return xn_ready

            def emit_ffn_pass(call, p, src_, dst, do_ag):
                blocks = [4 * p + i for i in range(4)]
                for g in range(2):
                    bs = blocks[2 * g: 2 * g + 2]
                    for bi, s in enumerate(bs):
                        cx.dma_w(xb[:, :, g * GRP + bi * BLK: g * GRP + (bi + 1) * BLK],
                                 src_[s].rearrange("kc f t -> f kc t"))
                    V.wait_ge(cx.dq_w.h, cx.dq_w.n)
                    emit_ln_group(None, xb[:, :, g * GRP:(g + 1) * GRP], GRP, True, g)

                ln_done_dve = cx.dve.n
                if DBG_DUMP and call == 1 and p == 0:
                    S.wait_ge(cx.dve.h, cx.dve.n)
                    cx.dma_out(dbg_xn[:, :, :], xb[:, :, 0:GRP])
                    cx.dma_out(dbg_m[:, :, :], m_fm.ap())
                Tn.wait_ge(cx.dve.h, ln_done_dve)

                wpar1 = [0]
                wpar2 = [0]
                w1_last_use = [0, 0]
                w2_last_use = [0, 0]
                y2_evac_dve = [0, 0, 0, 0]

                def load_w1(e, mo):
                    par = wpar1[0] % 2
                    wpar1[0] += 1
                    G.wait_ge(cx.pe.h, w1_last_use[par])
                    n = cx.dma_par(G, cx.sw1, par, w1b[:, par, :, :],
                                   w1t[e, :, :, mo * P:(mo + 1) * P].rearrange("kc f m -> f kc m"))
                    return par, n

                def load_w2(e, mo2):
                    par = wpar2[0] % 2
                    wpar2[0] += 1
                    G.wait_ge(cx.pe.h, w2_last_use[par])
                    n = cx.dma_par(G, cx.sw2, par, w2b[:, par, :, :],
                                   w2t[e, :, :, mo2 * P:(mo2 + 1) * P].rearrange("kc f m -> f kc m"))
                    return par, n

                for e in range(E):
                    for i in range(NP1):
                        evac_dve = [0, 0]
                        evac_act = [0, 0]
                        for half, mo in ((0, i), (1, NP1 + i)):
                            par, wn = load_w1(e, mo)
                            Tn.wait_ge(cx.sw1[par].h, wn)
                            for g in range(2):
                                bank = pb[half * 2 + g]
                                if i > 0:
                                    if half == 0:
                                        Tn.wait_ge(cx.act.h, evac_act_prev[g])
                                    else:
                                        Tn.wait_ge(cx.dve.h, evac_dve_prev[g])
                                for kc in range(KC):
                                    inst = Tn.matmul(bank[:, :], w1b[:, par, kc, :],
                                                     xb[:, kc, g * GRP:(g + 1) * GRP],
                                                     start=(kc == 0), stop=(kc == KC - 1))
                                cx.pe.inc(inst)
                            w1_last_use[par] = cx.pe.n
                            if half == 0:
                                gate_pe = cx.pe.n
                            else:
                                up_pe = cx.pe.n
                        for g in range(2):
                            S.wait_ge(cx.pe.h, gate_pe)
                            if i > 0:
                                S.wait_ge(cx.dve.h, yact_dve[g])
                            inst = S.activation(sgb[:, g, :], pb[g][:, :], AF.Silu)
                            evac_act[g] = cx.act.inc(inst)
                            V.wait_ge(cx.pe.h, up_pe)
                            V.tensor_mul(tub[:, g, :], pb[2 + g][:, :], m_bc[:, e, g, :])
                            V.wait_ge(cx.act.h, evac_act[g])
                            inst = V.tensor_mul(y_act[:, i, g * GRP:(g + 1) * GRP],
                                                sgb[:, g, :], tub[:, g, :])
                            evac_dve[g] = cx.dve.inc(inst)
                        yact_dve = list(evac_dve)
                        evac_act_prev = list(evac_act)
                        evac_dve_prev = list(evac_dve)
                    for mo2 in range(MO2):
                        par, wn = load_w2(e, mo2)
                        Tn.wait_ge(cx.sw2[par].h, wn)
                        for g in range(2):
                            bank = pb[4 + (mo2 % 2) * 2 + g]
                            Tn.wait_ge(cx.dve.h, max(y2_evac_dve[(mo2 % 2) * 2 + g],
                                                     yact_dve[g]))
                            for kc in range(KC2):
                                inst = Tn.matmul(bank[:, :], w2b[:, par, kc, :],
                                                 y_act[:, kc, g * GRP:(g + 1) * GRP],
                                                 start=(kc == 0), stop=(kc == KC2 - 1))
                            cx.pe.inc(inst)
                            w2_last_use[par] = cx.pe.n
                            V.wait_ge(cx.pe.h, cx.pe.n)
                            if e == 0:
                                inst = V.tensor_copy(y_acc[:, mo2, g * GRP:(g + 1) * GRP], bank[:, :])
                            else:
                                inst = V.tensor_add(y_acc[:, mo2, g * GRP:(g + 1) * GRP],
                                                    y_acc[:, mo2, g * GRP:(g + 1) * GRP], bank[:, :])
                            y2_evac_dve[(mo2 % 2) * 2 + g] = cx.dve.inc(inst)

                if DBG_DUMP and call == 1 and p == 0:
                    S.wait_ge(cx.dve.h, cx.dve.n)
                    cx.dma_out(dbg_yacc[:, :, :], y_acc[:, :, 0:GRP])
                for g in range(2):
                    bs = blocks[2 * g: 2 * g + 2]
                    SY.wait_ge(cx.dq_out.h, cx.dq_out.n)
                    SY.wait_ge(cx.dve.h, cx.dve.n)
                    for bi, s in enumerate(bs):
                        cx.dma_in(xf32[:, :, bi * BLK:(bi + 1) * BLK],
                                  src_[s].rearrange("kc f t -> f kc t"))
                    V.wait_ge(cx.dq_in.h, cx.dq_in.n)
                    V.wait_ge(cx.dve.h, max(y2_evac_dve))
                    inst = V.scalar_tensor_tensor(
                        xf32.ap(), y_acc[:, :, g * GRP:(g + 1) * GRP], half_ap, xf32.ap(),
                        ALU.mult, ALU.add)
                    xnew = cx.dve.inc(inst)
                    S.wait_ge(cx.dve.h, xnew)
                    for bi, s in enumerate(bs):
                        cx.dma_out(dst[s].rearrange("kc f t -> f kc t"),
                                   xf32[:, :, bi * BLK:(bi + 1) * BLK])
                    if do_ag:
                        emit_ln_group(xf32.ap(), xb[:, :, g * GRP:(g + 1) * GRP], GRP, False, g)
                        S.wait_ge(cx.dve.h, cx.dve.n)
                        for bi, s in enumerate(bs):
                            cx.dma_out(ag1i[s].rearrange("(kc f) t -> f kc t", f=P),
                                       xb[:, :, g * GRP + bi * BLK: g * GRP + (bi + 1) * BLK])
                            G.wait_ge(cx.dq_out.h, cx.dq_out.n)
                            inst = G.collective_compute(
                                "AllGather", ALU.bypass, replica_groups=RG,
                                ins=[ag1i[s].opt()], outs=[ag1o[s].opt()])
                            cx.cc.inc(inst)
            for p in range(NPASS):
                emit_ffn_pass(call, p, src, dst, do_ag)
                cx.barrier()

        if DBG_SKIP_FFN:
            for s in range(NS):
                inst = G.collective_compute(
                    "AllGather", ALU.bypass, replica_groups=RG,
                    ins=[ag1i[s].opt()], outs=[ag1o[s].opt()])
                cx.cc.inc(inst)
        else:
            run_ffn_call(1, x_own, x1sp if dbg != "ffn1" else out_own,
                         do_ag=(dbg != "ffn1"))

        if dbg == "ffn1":
            SY.wait_ge(cx.dq_out.h, cx.dq_out.n)
            return nc


        import contextlib as _ctl
        with _ctl.ExitStack() as astack:
            asb = lambda name, shape, dt: astack.enter_context(nc.sbuf_tensor(name, shape, dt))
            apsb = lambda name: astack.enter_context(nc.psum_tensor(name, [P, GRP], F32))
            wq_s = asb("wq_s", [P, KC, 2, P], BF16)
            wk_s = asb("wk_s", [P, KC, 2, P], BF16)
            wv_s = asb("wv_s", [P, KC, 2, P], BF16)
            Qf = asb("Qf", [P, 2, 2048], BF16)
            Kf = asb("Kf", [P, 2, 2048], BF16)
            VMKf = asb("VMKf", [P, 2, 2048], BF16)
            Cs = asb("Cs", [P, 2048], BF16)
            Ss = asb("Ss", [P, 2048], BF16)
            xt = asb("xt", [P, 2, KC, BLK], BF16)
            rsc = asb("rsc", [P, 3, BLK], BF16)
            out_fm = asb("out_fm", [P, 2, 2048], BF16)
            x1b = asb("x1b", [P, KC, BLK], F32)
            worhs = asb("worhs", [P, KC, BLK], BF16)
            wo_b = asb("wo_b", [P, 2, KC, P], BF16)
            bdK = asb("bdK", [P, 2, 48], BF16)
            bdQ = asb("bdQ", [P, 2, 48], BF16)
            bdW = asb("bdW", [P, P], BF16)
            atbd = asb("atbd", [P, P], BF16)
            gbt = asb("gbt", [P, HD], F32)
            vm_s = asb("vm_s", [P, HD], BF16)
            xk_s = asb("xk_s", [P, HD], BF16)
            zs = asb("zs", [P, HD], F32)
            xh = asb("xh", [P, HD], F32)
            gxh = asb("gxh", [P, HD], F32)
            tp = asb("tp", [P, HD], F32)
            t5 = asb("t5", [P, HD], F32)
            grd = asb("grd", [P, HD], F32)
            grdbf = asb("grdbf", [P, HD], BF16)
            zbar = asb("zbar", [P, HD], F32)
            xh2 = asb("xh2", [P, HD], F32)
            ot = asb("ot", [P, HD], BF16)
            bn6 = asb("bn6", [P, 6], F32)
            st2 = asb("st2", [P, 2], F32)
            st2b = asb("st2b", [P, 2], F32)
            ssc = asb("ssc", [P, 16], F32)
            ssc2 = asb("ssc2", [P, 8], F32)
            pa = [apsb(f"pa{i}") for i in range(6)]
            pa5b = pa[5].ap().bitcast(BF16)
            p_z1 = pa[2][:, 0:HD]
            p_zq = pa[2][:, 64:128]
            p_at = pa[1][:, 0:48]
            p_zb = pa[3][:, 0:HD]
            p_w = pa[4][:, 0:P]
            p_b = pa[4][:, P:P + HD]
            p_xkT = pa5b[:, 0:HD]
            p_vmT = pa5b[:, 64:128]
            pa3b = pa[3].ap().bitcast(BF16)
            p_bt0 = pa5b[:, 128:144]
            p_bt1 = pa3b[:, 256:272]

            sd_a = ssc[:, 0:1]
            rstd_a = ssc[:, 4:5]
            aps_a = ssc[:, 8:9]
            b2n_a = ssc[:, 12:13]
            sd2_a = ssc2[:, 0:1]
            rstd2_a = ssc2[:, 4:5]

            pid_sy = SY.partition_id()

            for w_i, wbuf in ((0, wq_s), (1, wk_s), (2, wv_s)):
                cx.dma_w(wbuf.ap(),
                         wqkv[w_i].rearrange("kc f (hg m) -> f kc hg m", hg=2))
            V.memset(bdK.ap(), 0.0)
            V.memset(bdQ.ap(), 0.0)
            V.memset(bdW.ap(), 0.0)
            V.memset(atbd.ap(), 0.0)
            V.memset(zs.ap(), 0.0)
            V.memset(pa[5].ap(), 0.0)
            V.memset(pa[2].ap(), 0.0)
            inst = V.memset(out_fm.ap(), 0.0)
            cx.dve.inc(inst)
            ag1_cc = cx.cc.n - NS

            def dv(f):
                inst = f()
                cx.dve.inc(inst)
                V.drain()
                return cx.dve.n

            def gv(f):
                inst = f()
                cx.gp.inc(inst)
                G.drain()
                return cx.gp.n

            def av(f):
                inst = f()
                cx.act.inc(inst)
                S.drain()
                return cx.act.n

            for s in range(DBG_TILES):
                SY.wait_ge(cx.cc.h, ag1_cc + s + 1)
                cx.dma_in(Cs[:, :], ropeC[:, 2048 * s:2048 * (s + 1)])
                cx.dma_in(Ss[:, :], ropeS[:, 2048 * s:2048 * (s + 1)])
                xt_pe_last = [0, 0]
                for r in range(NCORE):
                    par = r % 2
                    SY.wait_ge(cx.pe.h, xt_pe_last[par])
                    xt_n = cx.dma_par(SY, cx.sxt, par, xt[:, par, :, :],
                                      ag1o[s][2048 * r:2048 * (r + 1), :]
                                      .rearrange("(kc f) t -> f kc t", f=P))
                    tok0 = BLK * r
                    for w_i, wbuf in ((0, wq_s), (1, wk_s), (2, wv_s)):
                        for hg in range(2):
                            bank = pa[(w_i * 2 + hg) % 2]
                            Tn.wait_ge(cx.sxt[par].h, xt_n)
                            Tn.wait_ge(cx.dq_w.h, cx.dq_w.n)
                            Tn.wait_ge(cx.dve.h, cx.dve.n)
                            for kc in range(KC):
                                inst = Tn.matmul(bank[:, 0:BLK], wbuf[:, kc, hg, :],
                                                 xt[:, par, kc, :],
                                                 start=(kc == 0), stop=(kc == KC - 1))
                            cx.pe.inc(inst)
                            xt_pe_last[par] = cx.pe.n
                            V.wait_ge(cx.pe.h, cx.pe.n)
                            V.wait_ge(cx.dq_in.h, cx.dq_in.n)
                            if w_i < 2:
                                dst = (Qf if w_i == 0 else Kf)
                                V.tensor_copy(rsc[:, 0, :], bank[:, 0:BLK])
                                shmask = []
                                for i2 in range(0, 32, 2):
                                    shmask += [i2 + 1, i2]
                                V.stream_shuffle(rsc[:, 1, :], rsc[:, 0, :], shmask)
                                V.tensor_mul(rsc[:, 2, :], rsc[:, 0, :],
                                             Cs[:, tok0:tok0 + BLK])
                                V.tensor_mul(rsc[:, 1, :], rsc[:, 1, :],
                                             Ss[:, tok0:tok0 + BLK])
                                inst = V.tensor_add(dst[:, hg, tok0:tok0 + BLK],
                                                    rsc[:, 2, :], rsc[:, 1, :])
                            else:
                                inst = V.tensor_sub(VMKf[:, hg, tok0:tok0 + BLK],
                                                    bank[:, 0:BLK],
                                                    Kf[:, hg, tok0:tok0 + BLK])
                            cx.dve.inc(inst)
                proj_done_dve = cx.dve.n

                Tn.wait_ge(cx.dve.h, proj_done_dve)
                V.wait_ge(cx.dve.h, proj_done_dve)
                prev_pe = cx.pe.n
                pend_bt = None

                def flush_bt(pbt):
                    bt_pe_, sl_ = pbt
                    V.wait_ge(cx.pe.h, bt_pe_)
                    for l in range(LH):
                        hp, hg = l % 2, l // 2
                        p_bt = p_bt0 if hg == 0 else p_bt1
                        inst = V.tensor_add(out_fm[64 * hp:64 * hp + 64, hg, sl_],
                                            p_bt[64 * hp:64 * hp + 64, :],
                                            Qf[64 * hp:64 * hp + 64, hg, sl_])
                    cx.dve.inc(inst)

                for j in range(DBG_CHUNKS):
                    sl = slice(CS * j, CS * (j + 1))
                    V.wait_ge(cx.pe.h, prev_pe)
                    if pend_bt is not None:
                        flush_bt(pend_bt)
                        pend_bt = None
                    for hg in range(2):
                        V.tensor_copy(bdK[0:64, hg, 0:CS], Kf[0:64, hg, sl])
                        V.tensor_copy(bdK[64:128, hg, 32:48], Kf[64:128, hg, sl])
                        V.tensor_copy(bdQ[0:64, hg, 0:CS], Qf[0:64, hg, sl])
                        inst = V.tensor_copy(bdQ[64:128, hg, 32:48], Qf[64:128, hg, sl])
                    cx.dve.inc(inst)
                    bd_dve = cx.dve.n
                    if DBG_SCANLV < 1:
                        prev_pe = cx.pe.n
                        continue
                    Tn.wait_ge(cx.dve.h, bd_dve)
                    Tn.wait_ge(cx.act.h, cx.act.n)
                    for l in range(LH):
                        hp, hg = l % 2, l // 2
                        inst = Tn.transpose(
                            p_xkT[32 * l:32 * l + CS, :], Kf[64 * hp:64 * hp + 64, hg, sl],
                            ident_b[64 * hp:64 * hp + 64, 64 * hp:64 * hp + 64],
                            tile_position=(64 * hp, 32 * l))
                        inst = Tn.transpose(
                            p_vmT[32 * l:32 * l + CS, :], VMKf[64 * hp:64 * hp + 64, hg, sl],
                            ident_b[64 * hp:64 * hp + 64, 64 * hp:64 * hp + 64],
                            tile_position=(64 * hp, 32 * l))
                    inst = Tn.matmul(p_at[0:48, :], bdK[:, 0, :], bdQ[:, 0, 0:48],
                                     start=True, stop=True)
                    inst = Tn.matmul(p_at[64:112, :], bdK[:, 1, :], bdQ[:, 1, 0:48],
                                     start=True, stop=True, tile_position=(0, 64))
                    cx.pe.inc(inst)
                    tr_pe = cx.pe.n
                    S.wait_ge(cx.pe.h, tr_pe)
                    av(lambda: S.copy(xk_s[:, :], p_xkT[:, :]))
                    vm_act = av(lambda: S.copy(vm_s[:, :], p_vmT[:, :]))
                    V.wait_ge(cx.act.h, vm_act)
                    for l in range(LH):
                        hp = l % 2
                        V.tensor_copy(bdW[32 * l:32 * l + CS, 64 * hp:64 * hp + 64],
                                      xk_s[32 * l:32 * l + CS, :])
                        inst = V.tensor_mul(
                            atbd[32 * l:32 * l + CS, 32 * l:32 * l + CS],
                            p_at[32 * l:32 * l + CS, 32 * hp:32 * hp + CS],
                            mneg[32 * l:32 * l + CS, :])
                    cx.dve.inc(inst)
                    evac_dve = cx.dve.n
                    V.wait_ge(cx.act.h, vm_act)
                    dv(lambda: V.tensor_mul(gbt[:, :], g_rep[:, :], vm_s[:, :]))
                    dv(lambda: V.tensor_sub(gbt[:, :], gb_rep[:, :], gbt[:, :]))
                    if DBG_SCANLV < 2:
                        prev_pe = cx.pe.n
                        continue

                    wcast = av(lambda: S.copy(W1bf.ap(), W1s.ap()))
                    Tn.wait_ge(cx.act.h, wcast)
                    inst = Tn.matmul(p_z1[0:48, :], bdK[:, 0, 0:48], W1bf[:, 0, :],
                                     start=True, stop=True)
                    inst = Tn.matmul(p_z1[64:112, :], bdK[:, 1, 0:48], W1bf[:, 1, :],
                                     start=True, stop=True, tile_position=(0, 64))
                    cx.pe.inc(inst)
                    z1_pe = cx.pe.n
                    V.wait_ge(cx.pe.h, z1_pe)
                    V.tensor_add(zs[0:48, :], p_z1[0:48, :], b1s[0:48, :])
                    dv(lambda: V.tensor_add(zs[64:112, :], p_z1[64:112, :], b1s[64:112, :]))
                    dv(lambda: V.bn_stats(bn6[:, :], zs[:, :]))
                    chain_dve = dv(lambda: V.bn_aggr(st2[:, :], bn6[:, :]))
                    S.wait_ge(cx.dve.h, chain_dve)
                    sd_act = av(lambda: S.activation(sd_a, st2[:, 1:2], AF.Sqrt, bias=EPS_TTT))
                    V.wait_ge(cx.act.h, sd_act)
                    dv(lambda: V.reciprocal(rstd_a, sd_a))
                    dv(lambda: V.tensor_scalar(xh[:, :], zs[:, :], st2[:, 0:1], rstd_a,
                                               ALU.subtract, ALU.mult))
                    dv(lambda: V.tensor_mul(gxh[:, :], g2_rep[:, :], xh[:, :]))
                    dv(lambda: V.tensor_add(gxh[:, :], gxh[:, :], gbt[:, :]))
                    dv(lambda: V.reduce_sum(ssc[:, 8:9], gxh[:, :], AX.X))
                    dv(lambda: V.tensor_mul(tp[:, :], gxh[:, :], xh[:, :]))
                    dv(lambda: V.reduce_sum(ssc[:, 12:13], tp[:, :], AX.X))
                    dv(lambda: V.tensor_scalar_mul(aps_a, aps_a, inv64_ap))
                    dv(lambda: V.tensor_scalar(b2n_a, b2n_a, ninv64_ap, rstd_a,
                                               ALU.mult, ALU.mult))
                    dv(lambda: V.tensor_scalar(t5[:, :], gxh[:, :], aps_a, rstd_a,
                                               ALU.subtract, ALU.mult))
                    dv(lambda: V.scalar_tensor_tensor(grd[:, :], xh[:, :], b2n_a, t5[:, :],
                                                      ALU.mult, ALU.add))
                    grad_dve = dv(lambda: V.tensor_copy(grdbf[:, :], grd[:, :]))
                    Tn.wait_ge(cx.dve.h, grad_dve)
                    inst = Tn.matmul(p_zb[:, :], atbd[:, :], grdbf[:, :],
                                     start=True, stop=False)
                    inst = Tn.matmul(p_zb[:, :], tril_s[:, :], grdbf[:, :],
                                     start=False, stop=True)
                    if DBG_SCANLV == 31:
                        cx.pe.inc(inst)
                        prev_pe = cx.pe.n
                        continue
                    inst = Tn.matmul(p_zq[0:48, :], bdQ[:, 0, 0:48], W1bf[:, 0, :],
                                     start=True, stop=True)
                    inst = Tn.matmul(p_zq[64:112, :], bdQ[:, 1, 0:48], W1bf[:, 1, :],
                                     start=True, stop=True, tile_position=(0, 64))
                    if DBG_SCANLV == 32:
                        cx.pe.inc(inst)
                        prev_pe = cx.pe.n
                        continue
                    if DBG_SCANLV != 326:
                        inst = Tn.matmul(p_b[:, :], onesbd_s[:, :], grdbf[:, :],
                                         start=True, stop=True)
                    if DBG_SCANLV == 325:
                        cx.pe.inc(inst)
                        prev_pe = cx.pe.n
                        continue
                    inst = Tn.matmul(pa[4][:, 0:64], bdW[0:48, :], grdbf[0:48, :],
                                     start=True, stop=True)
                    inst = Tn.matmul(pa[0][:, 0:64], bdW[64:112, :], grdbf[64:112, :],
                                     start=True, stop=True)
                    cx.pe.inc(inst)
                    upd_pe = cx.pe.n
                    if DBG_SCANLV == 33:
                        prev_pe = cx.pe.n
                        continue
                    V.wait_ge(cx.pe.h, upd_pe)
                    dv(lambda: V.tensor_add(zbar[:, :], p_zb[:, :], b1s[:, :]))
                    dv(lambda: V.tensor_add(zbar[:, :], zbar[:, :], p_zq[:, :]))
                    dv(lambda: V.scalar_tensor_tensor(
                        W1s[:, 0, :], pa[4][:, 0:64], nleta_ap, W1s[:, 0, :],
                        ALU.mult, ALU.add))
                    dv(lambda: V.scalar_tensor_tensor(
                        W1s[:, 1, :], pa[0][:, 0:64], nleta_ap, W1s[:, 1, :],
                        ALU.mult, ALU.add))
                    dv(lambda: V.scalar_tensor_tensor(b1s[:, :], p_b[:, :],
                                                      nleta_ap, b1s[:, :],
                                                      ALU.mult, ALU.add))
                    if DBG_SCANLV == 34:
                        prev_pe = cx.pe.n
                        continue
                    if DBG_SCANLV < 4:
                        prev_pe = cx.pe.n
                        continue
                    dv(lambda: V.bn_stats(bn6[:, :], zbar[:, :]))
                    lf_dve = dv(lambda: V.bn_aggr(st2b[:, :], bn6[:, :]))
                    S.wait_ge(cx.dve.h, lf_dve)
                    sd2_act = av(lambda: S.activation(sd2_a, st2b[:, 1:2], AF.Sqrt,
                                                      bias=EPS_TTT))
                    V.wait_ge(cx.act.h, sd2_act)
                    dv(lambda: V.reciprocal(rstd2_a, sd2_a))
                    dv(lambda: V.tensor_scalar(xh2[:, :], zbar[:, :], st2b[:, 0:1],
                                               rstd2_a, ALU.subtract, ALU.mult))
                    dv(lambda: V.tensor_mul(xh2[:, :], g_rep[:, :], xh2[:, :]))
                    ot_dve = dv(lambda: V.tensor_add(ot[:, :], xh2[:, :], b_rep[:, :]))
                    Tn.wait_ge(cx.dve.h, ot_dve)
                    for l in range(LH):
                        hp, hg = l % 2, l // 2
                        p_bt = p_bt0 if hg == 0 else p_bt1
                        inst = Tn.transpose(
                            p_bt[64 * hp:64 * hp + 64, :],
                            ot[32 * l:32 * l + CS, :],
                            ident_b[32 * l:32 * l + CS, 32 * l:32 * l + CS],
                            tile_position=(32 * l, 64 * hp))
                    cx.pe.inc(inst)
                    pend_bt = (cx.pe.n, sl)
                    prev_pe = upd_pe

                if pend_bt is not None:
                    flush_bt(pend_bt)
                    pend_bt = None
                if not DBG_WO:
                    cx.barrier()
                    continue
                S.wait_ge(cx.dve.h, cx.dve.n)
                cx.dma_out(ag2i[s].rearrange("(hg hp g) t -> (hp g) hg t", hg=2, hp=2),
                           out_fm.ap())
                G.wait_ge(cx.dq_out.h, cx.dq_out.n)
                inst = G.collective_compute(
                    "AllGather", ALU.bypass, replica_groups=RG,
                    ins=[ag2i[s].opt()], outs=[ag2o[s].opt()])
                ag2_cc = cx.cc.inc(inst)
                SY.wait_ge(cx.cc.h, ag2_cc)
                cx.dma_in(worhs.ap(),
                          ag2o[s].rearrange("(kc f) t -> f kc t", f=P)
                          [:, :, bass.ds(pid_sy * BLK, BLK)])
                cx.dma_in(x1b.ap(), x1sp[s].rearrange("kc f t -> f kc t"))
                rb_n = cx.dq_in.n
                x2_dve = [0, 0]
                wo_pe_last = [0, 0]
                for mo in range(MO2):
                    par = mo % 2
                    G.wait_ge(cx.pe.h, wo_pe_last[par])
                    won = cx.dma_par(G, cx.sw1, par, wo_b[:, par, :, :],
                                     wot[:, :, mo * P:(mo + 1) * P]
                                     .rearrange("kc f m -> f kc m"))
                    bank = pa[par]
                    Tn.wait_ge(cx.sw1[par].h, won)
                    Tn.wait_ge(cx.dq_in.h, rb_n)
                    Tn.wait_ge(cx.dve.h, x2_dve[par])
                    for kc in range(KC):
                        inst = Tn.matmul(bank[:, 0:BLK], wo_b[:, par, kc, :],
                                         worhs[:, kc, :],
                                         start=(kc == 0), stop=(kc == KC - 1))
                    cx.pe.inc(inst)
                    wo_pe_last[par] = cx.pe.n
                    V.wait_ge(cx.pe.h, cx.pe.n)
                    inst = V.tensor_add(x1b[:, mo, :], bank[:, 0:BLK], x1b[:, mo, :])
                    x2_dve[par] = cx.dve.inc(inst)
                S.wait_ge(cx.dve.h, cx.dve.n)
                cx.dma_out(x2sp[s].rearrange("kc f t -> f kc t"), x1b.ap())
                cx.barrier()

        if dbg == "attn":
            for s in range(NS):
                cx.dma_in(out_own[s], x2sp[s])
            SY.wait_ge(cx.dq_in.h, cx.dq_in.n)
            SY.wait_ge(cx.dq_out.h, cx.dq_out.n)
            return nc

        run_ffn_call(2, x2sp, out_own, do_ag=False)

        SY.wait_ge(cx.dq_out.h, cx.dq_out.n)
        return nc



def kernel(L0_attn_hidden_states, L0_attn_wq, L0_attn_wk, L0_attn_wv, L0_attn_wo,
           L0_attn_W1, L0_attn_b1, L0_attn_ttt_norm_weight, L0_attn_ttt_norm_bias,
           L0_ffn_router_w, L0_ffn_w1, L0_ffn_w2):
    x2d = np.ascontiguousarray(np.asarray(L0_attn_hidden_states, np.float32).reshape(T, D))
    w1 = np.asarray(L0_ffn_w1, np.float32)
    w2 = np.asarray(L0_ffn_w2, np.float32)
    rw = np.asarray(L0_ffn_router_w, np.float32)
    wq = np.asarray(L0_attn_wq, np.float32)
    wk = np.asarray(L0_attn_wk, np.float32)
    wv = np.asarray(L0_attn_wv, np.float32)
    wo = np.asarray(L0_attn_wo, np.float32)
    W1 = np.asarray(L0_attn_W1, np.float32)
    b1 = np.asarray(L0_attn_b1, np.float32)
    tnw = np.asarray(L0_attn_ttt_norm_weight, np.float32)
    tnb = np.asarray(L0_attn_ttt_norm_bias, np.float32)

    w1t = np.ascontiguousarray(w1.transpose(0, 2, 1)).reshape(E, KC, P, 2 * I1)
    w2t = np.ascontiguousarray(w2.transpose(0, 2, 1)).reshape(E, KC2, P, D)
    rwT = np.ascontiguousarray(rw.T).reshape(KC, P, E)
    wot = np.ascontiguousarray(wo).reshape(KC, P, D)
    ropeC, ropeS = _rope_tables()
    onehot_h = np.zeros((E, E * P), np.float32)
    for e in range(E):
        onehot_h[e, e * P:(e + 1) * P] = 1.0

    in_maps = []
    for c in range(NCORE):
        consts = _host_consts(c, W1, b1, tnw, tnb)
        wqkv = np.stack([
            np.ascontiguousarray(w[:, LH * HD * c:LH * HD * (c + 1)]).reshape(KC, P, LH * HD)
            for w in (wq, wk, wv)])
        m = {
            "x_own": _fm_blocks(x2d, c),
            "w1t": w1t, "w2t": w2t, "rwT": rwT,
            "wqkv": np.ascontiguousarray(wqkv), "wot": wot,
            "ropeC": ropeC, "ropeS": ropeS,
            "g_rep": consts["g_rep"], "b_rep": consts["b_rep"],
            "g2_rep": consts["g2_rep"], "gb_rep": consts["gb_rep"],
            "w1_init": consts["w1_init"], "b1_init": consts["b1_init"],
            "mneg": consts["mneg"], "trilTneg": consts["trilTneg"],
            "ones_bd": consts["ones_bd"], "onehot": onehot_h,
        }
        in_maps.append(m)

    nc = build_program()
    res = run_bass_kernel_spmd(nc, in_maps, core_ids=list(range(NCORE)))
    out = np.empty((T, D), np.float32)
    for c in range(NCORE):
        _unblocks(res.results[c]["out_own"], c, out)
    return out.reshape(1, T, D)
